# version 17
# baseline (speedup 1.0000x reference)
import numpy as np
import ml_dtypes
import concourse.bass as bass
import concourse.mybir as mybir
from concourse.bass_utils import run_bass_kernel_spmd

F32 = mybir.dt.float32
BF16 = mybir.dt.bfloat16
AF = mybir.ActivationFunctionType
ALU = mybir.AluOpType

T, D, DC, NST = 4096, 1024, 8, 128
CH = 512
NCH = T // CH
CB = 256
NCB = T // CB
R = 64
W = 96 if R > 1 else 0
S = T // R
ROUNDS = S + W
SPB = max(1, 512 // R)
RINGB = 4
EPS = 1e-5


class _Tok:
    __slots__ = ("sem", "val")
    def __init__(self, sem, val):
        self.sem, self.val = sem, val


class _Sem:
    def __init__(self, name):
        self.name, self.total, self.handle = name, 0, None


class _Sched:
    def __init__(self):
        self.ops, self.sems = [], []
    def sem(self, name):
        s = _Sem(name); self.sems.append(s); return s
    def emit(self, eng, fn, deps=(), inc=None, inc_n=1, stream=None):
        deps = list(deps)
        if stream is not None:
            if not hasattr(self, "streams"):
                self.streams = {}
            sem, prev = self.streams.get(stream, (None, None))
            if sem is None:
                sem = self.sem("st_" + stream)
                sem.handle = next(self.handle_pool)
            if prev is not None:
                deps.append(prev)
            inc, inc_n = sem, 16
        waits = [(t.sem, t.val) for t in deps if t is not None and t.val > 0]
        tok = None
        if inc is not None:
            inc.total += inc_n
            tok = _Tok(inc, inc.total)
        if stream is not None:
            self.streams[stream] = (inc, tok)
        self.ops.append((eng, fn, waits, (inc, inc_n) if inc else None))
        return tok
    def lint(self):
        for _, _, waits, _ in self.ops:
            for sem, val in waits:
                assert val <= sem.total, f"wait {sem.name}>={val} > total {sem.total}"
        engs = {}
        for i, (e, fn, waits, inc) in enumerate(self.ops):
            engs.setdefault(e, []).append((waits, inc, i))
        ptr = {e: 0 for e in engs}
        val = {id(s): 0 for s in self.sems}
        progress = True
        while progress:
            progress = False
            for e, lst in engs.items():
                while ptr[e] < len(lst):
                    waits, inc, i = lst[ptr[e]]
                    if all(val[id(s)] >= v for s, v in waits):
                        if inc is not None:
                            val[id(inc[0])] += inc[1]
                        ptr[e] += 1
                        progress = True
                    else:
                        break
        stuck = {e: (ptr[e], len(lst)) for e, lst in engs.items() if ptr[e] < len(lst)}
        if stuck:
            detail = []
            for e, lst in engs.items():
                if ptr[e] < len(lst):
                    waits, inc, i = lst[ptr[e]]
                    why = [(s.name, v, val[id(s)]) for s, v in waits if val[id(s)] < v]
                    detail.append(f"{e}@op{i}: needs {why}")
            raise AssertionError("DEADLOCK: " + "; ".join(detail))
    def racecheck(self):
        class _Inst:
            def then_inc(self, *a, **k):
                return self
        class _Rec:
            def __init__(self):
                self.acc = []
            def _ap(self, ap, kind):
                try:
                    t = ap.tensor
                    name = t.name
                    pairs = [tuple(x) for x in ap.ap]
                    pstride = int(pairs[0][0]) if pairs else 1
                    lo = int(ap.offset) % max(pstride, 1) if pstride > 1 else int(ap.offset)
                    hi = lo
                    for pair in pairs[1:]:
                        st, ct = int(pair[0]), int(pair[1])
                        hi += abs(st) * (ct - 1)
                    self.acc.append((kind, name, lo, hi + 1))
                except Exception:
                    pass
            def __getattr__(self, nm):
                def h(*args, **kw):
                    aps = []
                    outs = []
                    for i, a in enumerate(args):
                        if hasattr(a, "tensor"):
                            (outs if i == 0 else aps).append(a)
                    for k, a in kw.items():
                        if hasattr(a, "tensor"):
                            (outs if k.startswith("out") else aps).append(a)
                    if nm in ("dma_start",):
                        pass
                    for a in outs:
                        self._ap(a, "w")
                    for a in aps:
                        self._ap(a, "r")
                    if nm in ("memset", "memzero") and outs == [] and aps:
                        self.acc[-1] = ("w",) + self.acc[-1][1:]
                    return _Inst()
                return h
        N = len(self.ops)
        accs = []
        for e, fn, waits, inc in self.ops:
            r = _Rec()
            try:
                fn(r)
            except Exception:
                pass
            accs.append(r.acc)
        import numpy as np
        anc = np.zeros((N, N), dtype=bool)
        preds = [[] for _ in range(N)]
        last_by_eng = {}
        inc_points = {}
        cum = {}
        for i, (e, fn, waits, inc) in enumerate(self.ops):
            if e in last_by_eng:
                preds[i].append(last_by_eng[e])
            last_by_eng[e] = i
            if inc is not None:
                s, n = inc
                cum[s] = cum.get(s, 0) + n
                inc_points.setdefault(s, []).append((cum[s], i))
        for i, (e, fn, waits, inc) in enumerate(self.ops):
            for s, v in waits:
                for cv, j in inc_points.get(s, []):
                    if cv >= v:
                        preds[i].append(j)
                        break
        order = list(range(N))
        for i in order:
            for j in preds[i]:
                anc[i] |= anc[j]
                anc[i][j] = True
        psum_names = {f"pb{k}": k for k in range(8)}
        issues = []
        def tag(i):
            return f"L{self.ops[i][1].__code__.co_firstlineno}"
        for i in range(N):
            ei = self.ops[i][0]
            for j in range(i + 1, N):
                ej = self.ops[j][0]
                if anc[j][i] or anc[i][j] or ei == ej:
                    continue
                for (ki, ni, li, hi) in accs[i]:
                    for (kj, nj, lj, hj) in accs[j]:
                        if ni != nj:
                            continue
                        if ni in psum_names:
                            if (ki == "w" or kj == "w"):
                                issues.append((i, j, ni, "psum-bank", ki, kj, tag(i), tag(j)))
                        else:
                            if (ki == "w" or kj == "w") and li < hj and lj < hi:
                                issues.append((i, j, ni, f"[{max(li,lj)}:{min(hi,hj)})", ki, kj, tag(i), tag(j)))
        return issues

    def run(self, eng_name, eng):
        last = {}
        for e, fn, waits, inc in self.ops:
            if e != eng_name:
                continue
            for sem, val in waits:
                if last.get(sem, 0) < val:
                    eng.wait_ge(sem.handle, val)
                    last[sem] = val
            inst = fn(eng)
            if inc is not None:
                inst.then_inc(inc[0].handle, inc[1])


def build_nc(upto="full", ops_limit=None):
    nc = bass.Bass("TRN2", target_bir_lowering=False)
    dram = {}
    for nm, shp in [("xT", [D, T]), ("wrT", [D, D]), ("wkT", [D, D]), ("wvT", [D, D]),
                    ("woT", [NST, D]), ("wsT", [D, NST]), ("wiT", [D, NST]), ("wiN", [NST, D]),
                    ("aT", [NST, NST]), ("bT", [NST, NST]), ("cT", [NST, NST]),
                    ("wmod", [D, 6144]), ("ccol", [NST, DC]), ("bi", [NST, 1]), ("bs", [NST, 1]),
                    ("borow", [1, D]), ("murc", [NST, DC]), ("mukc", [NST, DC])]:
        dram[nm] = nc.dram_tensor(nm, shp, F32, kind="ExternalInput").ap()
    yout = nc.dram_tensor("yout", [D, T], F32, kind="ExternalOutput").ap()
    x1sc = nc.dram_tensor("x1sc", [D, T], F32, kind="Internal").ap()

    sch = _Sched()
    SPE, SACT, SDVE, SGPC = (sch.sem(n) for n in ("pe", "act", "dve", "gpc"))
    DSP, DGP, SST = (sch.sem(n) for n in ("dsp", "dgp", "dst"))
    E = sch.emit

    AF32_N = 13664
    EPI = 13408
    ABF_N = 41152
    import contextlib
    with contextlib.ExitStack() as _st:
        _e = _st.enter_context
        wr = _e(nc.sbuf_tensor("wr", [128, DC * D], BF16))
        wk = _e(nc.sbuf_tensor("wk", [128, DC * D], BF16))
        wv = _e(nc.sbuf_tensor("wv", [128, DC * D], BF16))
        wo = _e(nc.sbuf_tensor("wo", [128, D], BF16))
        ws = _e(nc.sbuf_tensor("ws", [128, D], BF16))
        wib = _e(nc.sbuf_tensor("wib", [128, D], BF16))
        wiTs = _e(nc.sbuf_tensor("wiTs", [128, D], BF16))
        wiNs = _e(nc.sbuf_tensor("wiNs", [128, D], BF16))
        wia = _e(nc.sbuf_tensor("wia", [128, D], BF16))
        at = _e(nc.sbuf_tensor("at", [128, NST], BF16))
        btS = _e(nc.sbuf_tensor("btS", [128, NST], BF16))
        ct = _e(nc.sbuf_tensor("ct", [128, NST], BF16))
        identf = _e(nc.sbuf_tensor("identf", [128, 1], F32))
        identb = _e(nc.sbuf_tensor("identb", [128, 128], BF16))
        onescol = _e(nc.sbuf_tensor("onescol", [128, 1], BF16))
        onesrow = _e(nc.sbuf_tensor("onesrow", [1, CH], BF16))
        onesrowf = _e(nc.sbuf_tensor("onesrowf", [1, NST], F32))
        epsc = _e(nc.sbuf_tensor("epsc", [128, 1], F32))
        colsA = _e(nc.sbuf_tensor("colsA", [128, 32], F32))
        sab = _e(nc.sbuf_tensor("sab", [128, DC], BF16))
        sbb = _e(nc.sbuf_tensor("sbb", [128, DC], BF16))
        hcols = _e(nc.sbuf_tensor("hcols", [128, DC], BF16))
        crr = _e(nc.sbuf_tensor("crr", [128, DC], F32))
        crk = _e(nc.sbuf_tensor("crk", [128, DC], F32))
        siluf = _e(nc.sbuf_tensor("siluf", [128, DC], F32))
        silub = _e(nc.sbuf_tensor("silub", [128, DC], BF16))
        csvec = _e(nc.sbuf_tensor("csvec", [128, 8], F32))
        csb = _e(nc.sbuf_tensor("csb", [128, 2], BF16))
        bzc = _e(nc.sbuf_tensor("bzc", [128, 1], F32))
        murS = _e(nc.sbuf_tensor("murS", [128, DC], F32))
        mukS = _e(nc.sbuf_tensor("mukS", [128, DC], F32))
        borowS = _e(nc.sbuf_tensor("borowS", [1, D], BF16))
        AF32 = _e(nc.sbuf_tensor("AF32", [128, AF32_N], F32))
        ABF = _e(nc.sbuf_tensor("ABF", [128, ABF_N], BF16))
        pb0 = _e(nc.psum_tensor("pb0", [128, 512], F32))
        pb1 = _e(nc.psum_tensor("pb1", [128, 512], F32))
        pb2 = _e(nc.psum_tensor("pb2", [128, 512], F32))
        pb3 = _e(nc.psum_tensor("pb3", [128, 512], F32))
        pb4 = _e(nc.psum_tensor("pb4", [128, 512], F32))
        pb5 = _e(nc.psum_tensor("pb5", [128, 512], F32))
        pb6 = _e(nc.psum_tensor("pb6", [128, 512], F32))
        pb7 = _e(nc.psum_tensor("pb7", [128, 512], F32))
        h0 = _e(nc.semaphore("h0"))
        h1 = _e(nc.semaphore("h1"))
        h2 = _e(nc.semaphore("h2"))
        h3 = _e(nc.semaphore("h3"))
        h4 = _e(nc.semaphore("h4"))
        h5 = _e(nc.semaphore("h5"))
        h6 = _e(nc.semaphore("h6"))
        _sempool = [_e(nc.semaphore(f"spool{i}")) for i in range(48)]
        block = _e(nc.Block())

        for s, h in zip((SPE, SACT, SDVE, SGPC, DSP, DGP, SST), (h0, h1, h2, h3, h4, h5, h6)):
            s.handle = h
        sch.handle_pool = iter(_sempool)
        PB = [pb0, pb1, pb2, pb3, pb4, pb5, pb6, pb7]

        def dbg_exit(dumps):
            toks = []
            for row0, src_fn, deps, cast in dumps:
                eng = "gp" if cast else "sp"
                toks.append(E(eng, lambda s, row0=row0, src_fn=src_fn: s.dma_start(
                    out=yout[row0:row0 + src_fn().shape[0], 0:src_fn().shape[1]],
                    in_=src_fn(), allow_slow_non_contiguous=True), deps=deps, inc=SST, inc_n=16))
            E("gp", lambda g: g.engine_nop(), deps=[_Tok(SST, 16 * len(dumps))])
            sch.lint()

            @block.tensor
            def _(t):
                sch.run("pe", t)

            @block.scalar
            def _(s):
                sch.run("act", s)

            @block.vector
            def _(v):
                sch.run("dve", v)

            @block.gpsimd
            def _(g):
                sch.run("gp", g)

            @block.sync
            def _(s):
                sch.run("sp", s)

        X0A = 0
        X1A = 4096
        WSB = 8192
        RW1 = 12384
        assert WSB + max(W + T, T) // 1 <= RW1 or True
        def af(c0, w):
            return AF32[:, c0:c0 + w]
        def afr(c0, w):
            return AF32[0:1, c0:c0 + w]

        A0, B0, C0, D0, E0, F0, G0 = 0, 8192, 12288, 16384, 20496, 30800, 34897
        H0 = 36960
        def ab(c0, w):
            return ABF[:, c0:c0 + w]
        def abr(c0, w):
            return ABF[0:1, c0:c0 + w]

        t_at = E("gp", lambda g: g.dma_start(out=at[:], in_=dram["aT"]), stream="w_at")
        t_bt = E("gp", lambda g: g.dma_start(out=btS[:], in_=dram["bT"]), stream="w_bt")
        t_ct = E("gp", lambda g: g.dma_start(out=ct[:], in_=dram["cT"]), stream="w_ct")
        t_win = E("gp", lambda g: g.dma_start(out=wiNs[:], in_=dram["wiN"]), stream="w_win")
        t_wo = E("gp", lambda g: g.dma_start(out=wo[:], in_=dram["woT"]), stream="w_wo")
        t_ws = E("gp", lambda g: g.dma_start(
            out=ws[:].rearrange("p (r n) -> p r n", r=8), in_=dram["wsT"].rearrange("(r p) n -> p r n", p=128)), stream="w_ws")
        t_wiT = E("gp", lambda g: g.dma_start(
            out=wiTs[:].rearrange("p (r n) -> p r n", r=8), in_=dram["wiT"].rearrange("(r p) n -> p r n", p=128)), stream="w_wiT")
        t_wr = E("gp", lambda g: g.dma_start(
            out=wr[:].rearrange("p (r d) -> p r d", r=8), in_=dram["wrT"].rearrange("(r p) d -> p r d", p=128)), stream="w_wr")
        t_wk = E("gp", lambda g: g.dma_start(
            out=wk[:].rearrange("p (r d) -> p r d", r=8), in_=dram["wkT"].rearrange("(r p) d -> p r d", p=128)), stream="w_wk")
        t_wv = E("gp", lambda g: g.dma_start(
            out=wv[:].rearrange("p (r d) -> p r d", r=8), in_=dram["wvT"].rearrange("(r p) d -> p r d", p=128)), stream="w_wv")
        t_bi = E("sp", lambda s: s.dma_start(out=csvec[:, 3:4], in_=dram["bi"]), stream="s_bi")
        t_bs = E("sp", lambda s: s.dma_start(out=csvec[:, 4:5], in_=dram["bs"]), stream="s_bs")
        t_mur = E("sp", lambda s: s.dma_start(out=murS[:], in_=dram["murc"]), stream="s_mur")
        t_muk = E("sp", lambda s: s.dma_start(out=mukS[:], in_=dram["mukc"]), stream="s_muk")
        t_bo = E("sp", lambda s: s.dma_start(out=afr(RW1, D), in_=dram["borow"]), stream="s_bo")
        t_cc = E("sp", lambda s: s.dma_start(out=siluf[:], in_=dram["ccol"]), stream="s_cc")

        t_id0 = E("dve", lambda v: v.memset(identf[:], 0.0), inc=SDVE)
        E("dve", lambda v: v.memset(identf[0:1, 0:1], 1.0), deps=[t_id0])
        t_ib0 = E("gp", lambda g: g.memset(identb[:], 0.0), inc=SGPC)
        t_identb = E("gp", lambda g: g.affine_select(
            out=identb[:], in_=identb[:], compare_op=ALU.not_equal, fill=1.0, base=0,
            pattern=[[-1, 128]], channel_multiplier=1), deps=[t_ib0], inc=SGPC)
        E("dve", lambda v: v.memset(onescol[:], 1.0))
        E("dve", lambda v: v.memset(onesrow[:], 1.0))
        E("dve", lambda v: v.memset(onesrowf[:], 1.0))
        E("dve", lambda v: v.memset(epsc[:], EPS))
        E("dve", lambda v: v.memset(ab(E0, R), 0.0))
        E("dve", lambda v: v.memset(ab(F0, 1), 0.0))
        t_init = E("dve", lambda v: v.memset(ABF[:, H0:H0 + max(W, 1)], 0.0), inc=SDVE)
        t_bobf = E("dve", lambda v: v.tensor_copy(borowS[:], afr(RW1, D)), deps=[t_bo], inc=SDVE)
        t_silu = E("act", lambda s: s.activation(out=siluf[:], in_=siluf[:], func=AF.Silu),
                   deps=[t_cc], inc=SACT)
        t_silub = E("dve", lambda v: v.tensor_copy(silub[:], siluf[:]), deps=[t_silu, t_bobf],
                    inc=SDVE)

        if upto == "load":
            if ops_limit is not None:
                del sch.ops[ops_limit:]
            dbg_exit([
                (0, lambda: siluf[:], [], False),
            ])
            return nc

        wmodb_v = lambda par: ABF[:, F0 + par * 2048:F0 + (par + 1) * 2048]
        wmodf_v = lambda par: AF32[:, X1A + par * 1024:X1A + (par + 1) * 1024]

        def mod_block(c0, nbf, prev_mm, bank_gate=()):
            ng = 4 if nbf else 2
            dma_t, mm_t = [None] * 8, [None] * 8
            for kk in range(8):
                par = kk % 2
                deps = [t_silub]
                if kk >= 2:
                    deps.append(mm_t[kk - 2])
                if kk < 2 and prev_mm is not None:
                    deps.append(prev_mm[6 + kk])
                if nbf:
                    dma_t[kk] = E("gp", lambda g, kk=kk, par=par, c0=c0: g.dma_start(
                        out=wmodb_v(par), in_=dram["wmod"][kk * 128:(kk + 1) * 128, c0:c0 + 2048]),
                        deps=deps, stream=f"wmodb{par}")
                else:
                    dma_t[kk] = E("gp", lambda g, kk=kk, par=par, c0=c0: g.dma_start(
                        out=wmodf_v(par), in_=dram["wmod"][kk * 128:(kk + 1) * 128, c0:c0 + 1024]),
                        deps=deps, stream=f"wmodf{par}")
                for gi in range(ng):
                    srcf = (lambda par=par, gi=gi: wmodb_v(par)[:, gi * 512:(gi + 1) * 512]) if nbf \
                        else (lambda par=par, gi=gi: wmodf_v(par)[:, gi * 512:(gi + 1) * 512])
                    lhs = (lambda kk=kk: silub[:, kk:kk + 1]) if nbf else (lambda kk=kk: siluf[:, kk:kk + 1])
                    deps = [dma_t[kk]] + (list(bank_gate) if kk == 0 else [])
                    mm_t[kk] = E("pe", lambda t, gi=gi, kk=kk, srcf=srcf, lhs=lhs: t.matmul(
                        PB[4 + gi][0:1, 0:512], lhs(), srcf(),
                        start=(kk == 0), stop=(kk == 7)), deps=deps, inc=SPE)
            return mm_t

        mmA = mod_block(0, True, None)
        tr_prev = _Tok(SPE, 0)

        def row_to_cols(src_row_fn, dst, dst0, n, deps):
            nonlocal tr_prev
            outs = []
            for i in range(n):
                tp = E("pe", lambda t, i=i: t.matmul(
                    PB[3][0:128, 0:1], src_row_fn(i), identf[0:1, 0:1],
                    start=True, stop=True),
                    deps=list(deps) + [tr_prev], inc=SPE)
                tr_prev = E("dve", lambda v, i=i: v.tensor_copy(
                    dst[:, dst0 + i:dst0 + i + 1], PB[3][0:128, 0:1]), deps=[tp], inc=SDVE)
                outs.append(tr_prev)
            return outs

        t_saev, t_caev = [], []
        for gi in range(2):
            t_saev.append(E("dve", lambda v, gi=gi: v.tensor_copy(
                afr(X0A + gi * 512, 512), PB[4 + gi][0:1, 0:512]), deps=[mmA[7]], inc=SDVE))
        for gi in range(2):
            t_caev.append(E("dve", lambda v, gi=gi: v.tensor_scalar(
                out=afr(RW1, D)[0:1, gi * 512:(gi + 1) * 512], in0=PB[6 + gi][0:1, 0:512],
                scalar1=1.0, scalar2=None, op0=ALU.add), deps=[mmA[7], t_silub], inc=SDVE))
        t_caplb = E("dve", lambda v: v.tensor_copy(abr(D0, D), afr(RW1, D)),
                    deps=t_caev, inc=SDVE)
        sab_cols = row_to_cols(lambda i: afr(X0A + 0, 1024)[0:1, i * 128:(i + 1) * 128],
                               sab, 0, DC, t_saev + [mmA[7]])
        capl_cols = row_to_cols(lambda i: afr(RW1, D)[0:1, i * 128:(i + 1) * 128],
                                colsA, 0, DC, t_caev)

        mmB = mod_block(3072, True, mmA, bank_gate=t_saev + t_caev)
        t_sbev = []
        for gi in range(2):
            t_sbev.append(E("dve", lambda v, gi=gi: v.tensor_copy(
                afr(X0A + gi * 512, 512), PB[4 + gi][0:1, 0:512]),
                deps=[mmB[7], sab_cols[-1]], inc=SDVE))
        t_cbev = []
        for gi in range(2):
            t_cbev.append(E("dve", lambda v, gi=gi: v.tensor_scalar(
                out=afr(RW1, D)[0:1, gi * 512:(gi + 1) * 512], in0=PB[6 + gi][0:1, 0:512],
                scalar1=1.0, scalar2=None, op0=ALU.add),
                deps=[mmB[7], capl_cols[-1], t_caplb], inc=SDVE))
        sbb_cols = row_to_cols(lambda i: afr(X0A + 0, 1024)[0:1, i * 128:(i + 1) * 128],
                               sbb, 0, DC, t_sbev + [mmB[7]])
        cbpl_cols = row_to_cols(lambda i: afr(RW1, D)[0:1, i * 128:(i + 1) * 128],
                                colsA, 8, DC, t_cbev)

        mmGa = mod_block(2048, False, mmB, bank_gate=t_sbev + t_cbev)
        t_gaev = E("act", lambda s: s.activation(
            out=afr(X0A + 512, D)[0:1, 0:512], in_=PB[4][0:1, 0:512], func=AF.Abs),
            deps=[mmGa[7], sbb_cols[-1], cbpl_cols[-1]], inc=SACT)
        t_gaev2 = E("act", lambda s: s.activation(
            out=afr(X0A + 512, D)[0:1, 512:1024], in_=PB[5][0:1, 0:512], func=AF.Abs),
            deps=[mmGa[7], t_gaev], inc=SACT)
        t_gm = E("dve", lambda v: v.tensor_scalar_max(
            afr(X0A + 512, D), afr(X0A + 512, D), 0.25), deps=[t_gaev2], inc=SDVE)
        t_gs1 = E("dve", lambda v: v.tensor_scalar(
            out=afr(X0A + 1536, D)[0:1, 0:512], in0=PB[4][0:1, 0:512],
            scalar1=0.0, scalar2=None, op0=ALU.is_ge), deps=[mmGa[7]], inc=SDVE)
        t_gs2 = E("dve", lambda v: v.tensor_scalar(
            out=afr(X0A + 1536, D)[0:1, 512:1024], in0=PB[5][0:1, 0:512],
            scalar1=0.0, scalar2=None, op0=ALU.is_ge), deps=[mmGa[7]], inc=SDVE)
        t_gs3 = E("dve", lambda v: v.tensor_scalar(
            out=afr(X0A + 1536, D), in0=afr(X0A + 1536, D),
            scalar1=2.0, scalar2=-1.0, op0=ALU.mult, op1=ALU.add), deps=[t_gs1, t_gs2], inc=SDVE)
        t_ga = E("dve", lambda v: v.tensor_mul(
            afr(X0A + 512, D), afr(X0A + 512, D), afr(X0A + 1536, D)),
            deps=[t_gm, t_gs3], inc=SDVE)
        ga_cols = row_to_cols(lambda i: afr(X0A + 512, D)[0:1, i * 128:(i + 1) * 128],
                              colsA, 16, DC, [t_ga])

        mmGb = mod_block(5120, False, mmGa, bank_gate=[t_ga])
        t_gbev = E("act", lambda s: s.activation(
            out=afr(X0A + 512, D)[0:1, 0:512], in_=PB[4][0:1, 0:512], func=AF.Abs),
            deps=[mmGb[7], ga_cols[-1]], inc=SACT)
        t_gbev2 = E("act", lambda s: s.activation(
            out=afr(X0A + 512, D)[0:1, 512:1024], in_=PB[5][0:1, 0:512], func=AF.Abs),
            deps=[mmGb[7], t_gbev], inc=SACT)
        t_gm = E("dve", lambda v: v.tensor_scalar_max(
            afr(X0A + 512, D), afr(X0A + 512, D), 0.25), deps=[t_gbev2], inc=SDVE)
        t_gs1 = E("dve", lambda v: v.tensor_scalar(
            out=afr(X0A + 1536, D)[0:1, 0:512], in0=PB[4][0:1, 0:512],
            scalar1=0.0, scalar2=None, op0=ALU.is_ge), deps=[mmGb[7]], inc=SDVE)
        t_gs2 = E("dve", lambda v: v.tensor_scalar(
            out=afr(X0A + 1536, D)[0:1, 512:1024], in0=PB[5][0:1, 0:512],
            scalar1=0.0, scalar2=None, op0=ALU.is_ge), deps=[mmGb[7]], inc=SDVE)
        t_gs3 = E("dve", lambda v: v.tensor_scalar(
            out=afr(X0A + 1536, D), in0=afr(X0A + 1536, D),
            scalar1=2.0, scalar2=-1.0, op0=ALU.mult, op1=ALU.add), deps=[t_gs1, t_gs2], inc=SDVE)
        t_gb = E("dve", lambda v: v.tensor_mul(
            afr(X0A + 512, D), afr(X0A + 512, D), afr(X0A + 1536, D)),
            deps=[t_gm, t_gs3], inc=SDVE)
        gb_cols = row_to_cols(lambda i: afr(X0A + 512, D)[0:1, i * 128:(i + 1) * 128],
                              colsA, 24, DC, [t_gb])
        t_gbb = E("dve", lambda v: v.tensor_copy(abr(D0 + 1024, D), afr(X0A + 512, D)),
                  deps=[t_gb], inc=SDVE)

        if upto == "mod":
            dbg_exit([
                (0, lambda: colsA[:], [capl_cols[-1], cbpl_cols[-1], ga_cols[-1], gb_cols[-1]], False),
                (128, lambda: sab[:], [sab_cols[-1]], True),
                (256, lambda: sbb[:], [sbb_cols[-1]], True),
            ])
            return nc

        if upto == "mod":
            if ops_limit is not None:
                del sch.ops[ops_limit:]
            dbg_exit([
                (0, lambda: colsA[:], [capl_cols[-1], cbpl_cols[-1], ga_cols[-1], gb_cols[-1]] if ops_limit is None else [], False),
                (128, lambda: sab[:], [sab_cols[-1]] if ops_limit is None else [], True),
                (256, lambda: sbb[:], [sbb_cols[-1]] if ops_limit is None else [], True),
                (384, lambda: AF32[0:1, RW1:RW1 + D], [capl_cols[-1]] if ops_limit is None else [], False),
                (385, lambda: AF32[0:1, X0A:X0A + D], [sab_cols[-1]] if ops_limit is None else [], False),
            ])
            return nc

        t_mm = None
        for kk in range(DC):
            t_mm = E("pe", lambda t, kk=kk: t.matmul(
                PB[6][0:128, 0:1], ws[:, kk * 128:(kk + 1) * 128], sab[:, kk:kk + 1],
                start=(kk == 0), stop=(kk == 7)),
                deps=[t_ws, sab_cols[kk], mmGb[7]] + (t_cbev if kk == 0 else []), inc=SPE)
        t_cs = E("dve", lambda v: v.scalar_tensor_tensor(
            out=csvec[:, 0:1], in0=PB[6][0:128, 0:1], scalar=1.0, in1=csvec[:, 4:5],
            op0=ALU.mult, op1=ALU.add), deps=[t_mm, t_bs], inc=SDVE)
        t_mm = None
        for kk in range(DC):
            t_mm = E("pe", lambda t, kk=kk: t.matmul(
                PB[6][0:128, 1:2], wiTs[:, kk * 128:(kk + 1) * 128], sab[:, kk:kk + 1],
                start=(kk == 0), stop=(kk == 7)),
                deps=[t_wiT, sab_cols[kk]] + ([t_cs] if kk == 0 else []), inc=SPE)
        t_cu = E("dve", lambda v: v.scalar_tensor_tensor(
            out=csvec[:, 1:2], in0=PB[6][0:128, 1:2], scalar=1.0, in1=csvec[:, 3:4],
            op0=ALU.mult, op1=ALU.add), deps=[t_mm, t_bi], inc=SDVE)
        t_csb = E("dve", lambda v: v.tensor_copy(csb[:], csvec[:, 0:2]), deps=[t_cs, t_cu],
                  inc=SDVE)
        t_mm = E("pe", lambda t: t.matmul(PB[6][0:128, 2:3], at[:], csb[:, 0:1],
                                          start=True, stop=False), deps=[t_at, t_csb], inc=SPE)
        t_mm = E("pe", lambda t: t.matmul(PB[6][0:128, 2:3], btS[:], csb[:, 1:2],
                                          start=False, stop=True), deps=[t_bt, t_mm], inc=SPE)
        t_bz = E("dve", lambda v: v.tensor_copy(bzc[:], PB[6][0:128, 2:3]), deps=[t_mm], inc=SDVE)
        t_mm = E("pe", lambda t: t.matmul(PB[6][0:128, 3:4], ct[:], csb[:, 0:1],
                                          start=True, stop=True), deps=[t_ct, t_csb, t_bz], inc=SPE)
        t_ccs = E("dve", lambda v: v.tensor_copy(csvec[:, 2:3], PB[6][0:128, 3:4]),
                  deps=[t_mm], inc=SDVE)
        t_mm = E("pe", lambda t: t.matmul(PB[6][0:128, 4:5], at[:], csb[:, 0:1],
                                          start=True, stop=True), deps=[t_at, t_csb, t_ccs], inc=SPE)
        t_acs = E("dve", lambda v: v.tensor_copy(csvec[:, 5:6], PB[6][0:128, 4:5]),
                  deps=[t_mm], inc=SDVE)

        crr_last = t_acs
        for nm, wt, wtok, dst in (("crr", wr, t_wr, crr), ("crk", wk, t_wk, crk)):
            for gg in range(2):
                t_mm = None
                for kk in range(DC):
                    t_mm = E("pe", lambda t, kk=kk, gg=gg, wt=wt: t.matmul(
                        PB[6][0:1, 0:512], sbb[:, kk:kk + 1],
                        wt[:, kk * D + gg * 512:kk * D + (gg + 1) * 512],
                        start=(kk == 0), stop=(kk == 7)),
                        deps=[wtok, sbb_cols[kk], crr_last], inc=SPE)
                crr_last = E("dve", lambda v, gg=gg: v.tensor_copy(
                    afr(X0A + 2560, 1024)[0:1, gg * 512:(gg + 1) * 512], PB[6][0:1, 0:512]),
                    deps=[t_mm], inc=SDVE)
            cols = row_to_cols(lambda i: afr(X0A + 2560, 1024)[0:1, i * 128:(i + 1) * 128],
                               dst, 0, DC, [crr_last])
            crr_last = cols[-1]
        t_crcols = crr_last

        t_wsf = None
        for m in range(DC):
            t_wsf = E("act", lambda s, m=m: s.activation(
                out=ws[:, m * 128:(m + 1) * 128], in_=ws[:, m * 128:(m + 1) * 128],
                func=AF.Identity, bias=0.0, scale=colsA[:, m:m + 1]),
                deps=[t_cs, capl_cols[m], t_ws], inc=SACT)
        bc_t = []
        for gg in range(2):
            tm = E("pe", lambda t, gg=gg: t.matmul(
                PB[6][0:128, 0:512], onesrow[0:1, 0:128], abr(D0 + gg * 512, 512),
                start=True, stop=True),
                deps=[t_caplb, t_crcols] + (bc_t[:1] if gg else []), inc=SPE)
            bc_t.append(E("dve", lambda v, gg=gg: v.tensor_copy(
                ABF[:, G0 + gg * 512:G0 + (gg + 1) * 512], PB[6][0:128, 0:512]),
                deps=[tm], inc=SDVE))
        t_wia = E("dve", lambda v: v.tensor_mul(wia[:], wiNs[:], ab(G0, 1024)),
                  deps=[bc_t[1], t_win], inc=SDVE)
        t_wibl = None
        for m in range(DC):
            tm = E("pe", lambda t, m=m: t.matmul(
                PB[6][0:128, 0:128], wia[:, m * 128:(m + 1) * 128], btS[:],
                start=True, stop=True),
                deps=[t_wia, t_bt] + ([t_wibl] if t_wibl else []), inc=SPE)
            t_wibl = E("dve", lambda v, m=m: v.tensor_copy(
                wib[:, m * 128:(m + 1) * 128], PB[6][0:128, 0:128]), deps=[tm], inc=SDVE)
        t_wib = t_wibl
        t_wrf = t_wkf = None
        for m in range(DC):
            t_wrf = E("act", lambda s, m=m: s.activation(
                out=wr[:, m * D:(m + 1) * D], in_=wr[:, m * D:(m + 1) * D],
                func=AF.Identity, bias=0.0, scale=colsA[:, 8 + m:9 + m]),
                deps=[t_crcols, cbpl_cols[m], t_wr], inc=SACT)
            t_wkf = E("act", lambda s, m=m: s.activation(
                out=wk[:, m * D:(m + 1) * D], in_=wk[:, m * D:(m + 1) * D],
                func=AF.Identity, bias=0.0, scale=colsA[:, 8 + m:9 + m]),
                deps=[t_crcols, cbpl_cols[m], t_wk], inc=SACT)
        bc_t = []
        for gg in range(2):
            tm = E("pe", lambda t, gg=gg: t.matmul(
                PB[6][0:128, 0:512], onesrow[0:1, 0:128], abr(D0 + 1024 + gg * 512, 512),
                start=True, stop=True),
                deps=[t_gbb, t_wib] + (bc_t[:1] if gg else []), inc=SPE)
            bc_t.append(E("dve", lambda v, gg=gg: v.tensor_copy(
                ABF[:, G0 + gg * 512:G0 + (gg + 1) * 512], PB[6][0:128, 0:512]),
                deps=[tm, t_wia], inc=SDVE))
        t_wvf = None
        for m in range(DC):
            t_wvf = E("dve", lambda v, m=m: v.tensor_mul(
                wv[:, m * D:(m + 1) * D], wv[:, m * D:(m + 1) * D], ab(G0, 1024)),
                deps=[bc_t[1], t_wv], inc=SDVE)

        t_pad = E("dve", lambda v: v.memset(ab(F0, 1), 0.0),
                  deps=[mmB[7], mmGb[7]], inc=SDVE)

        t_hcols = None
        for m in range(DC):
            t_r = E("dve", lambda v, m=m: v.reciprocal(csvec[:, 6:7], colsA[:, 8 + m:9 + m]),
                    deps=[cbpl_cols[m], t_cs, t_cu] + ([t_hcols] if t_hcols else []), inc=SDVE)
            t_m = E("dve", lambda v, m=m: v.tensor_mul(csvec[:, 7:8], sbb[:, m:m + 1],
                    csvec[:, 6:7]), deps=[t_r, sbb_cols[m]], inc=SDVE)
            t_hcols = E("dve", lambda v, m=m: v.tensor_scalar(
                out=hcols[:, m:m + 1], in0=csvec[:, 7:8], scalar1=-1.0, scalar2=None,
                op0=ALU.mult), deps=[t_m], inc=SDVE)

        if upto == "pro":
            dbg_exit([
                (0, lambda: colsA[:], [t_cs, t_cu, t_ccs, t_acs, t_crcols, t_wvf, t_wsf if t_wsf else None, t_wkf], False),
                (128, lambda: csvec[:], [t_acs, t_bz], False),
                (256, lambda: crr[:], [t_crcols], False),
                (384, lambda: crk[:], [t_crcols], False),
                (512, lambda: bzc[:], [t_bz], False),
                (640, lambda: sab[:], [t_crcols], True),
                (768, lambda: sbb[:], [t_crcols], True),
                (896, lambda: wib[:, 0:256], [t_wib], True),
            ])
            return nc

        bfree = {0: mmGb[7], 1: mmGb[7], 2: mmGb[7], 3: tr_prev, 4: t_gb, 5: t_gb,
                 6: bc_t[1], 7: t_cbev[1]}
        xbufb = lambda par: ABF[:, A0 + par * 4096:A0 + (par + 1) * 4096]
        xsqP = lambda: ABF[:, B0:B0 + 4096]
        xcb = lambda: ABF[:, C0:C0 + 4096]
        sbc_v = ABF[:, G0:G0 + 512]
        mbc_v = ABF[:, G0 + 512:G0 + 1024]
        rowsS = lambda i: AF32[0:1, X1A + 2048 + i * CH:X1A + 2048 + (i + 1) * CH]
        rowbf = lambda i: ABF[0:1, D0 + 2816 + i * CH:D0 + 2816 + (i + 1) * CH]
        spb_c = lambda c0, w: ABF[:, F0 + c0:F0 + c0 + w]
        wsb_c = lambda c0, w: ABF[:, H0 + c0:H0 + c0 + w]
        thb_c = lambda c0, w: ABF[:, E0 + c0:E0 + c0 + w]

        t_xdma = [None] * NCH
        t_wev = [None] * NCH
        t_spev = [None] * NCH
        xb_last = {0: t_silub, 1: t_silub}
        rows_free = t_caplb
        xsq_free = t_gbb
        xc_free = t_gbb
        for c in range(NCH):
            par = c % 2
            t0 = c * CH
            t_xdma[c] = E("gp", lambda g, t0=t0, par=par: g.dma_start(
                out=xbufb(par).rearrange("p (r t) -> p r t", r=8), in_=dram["xT"][:, t0:t0 + CH].rearrange("(r p) t -> p r t", p=128)),
                deps=[xb_last[par]], stream=f"x_{par}")
            t_sq = [None] * DC
            for m in range(DC):
                t_sq[m] = E("gp", lambda g, m=m, par=par: g.tensor_mul(
                    xsqP()[:, m * CH:(m + 1) * CH], xbufb(par)[:, m * CH:(m + 1) * CH],
                    xbufb(par)[:, m * CH:(m + 1) * CH]),
                    deps=[t_xdma[c], xsq_free], inc=SGPC)
            t_sum = t_ssq = None
            for m in range(DC):
                t_sum = E("pe", lambda t, m=m, par=par: t.matmul(
                    PB[0][0:1, 0:CH], onescol[:], xbufb(par)[:, m * CH:(m + 1) * CH],
                    start=(m == 0), stop=(m == 7)), deps=[t_xdma[c], bfree[0], rows_free], inc=SPE)
            for m in range(DC):
                t_ssq = E("pe", lambda t, m=m: t.matmul(
                    PB[1][0:1, 0:CH], onescol[:], xsqP()[:, m * CH:(m + 1) * CH],
                    start=(m == 0), stop=(m == 7)), deps=[t_sq[m], bfree[1], rows_free], inc=SPE)
            t_mrbf = E("dve", lambda v: v.tensor_scalar(
                out=rowbf(0), in0=PB[0][0:1, 0:CH], scalar1=1.0 / D, scalar2=None,
                op0=ALU.mult), deps=[t_sum], inc=SDVE)
            t_mf = E("dve", lambda v: v.tensor_scalar(
                out=rowsS(0), in0=PB[0][0:1, 0:CH], scalar1=1.0 / D, scalar2=None,
                op0=ALU.mult), deps=[t_sum], inc=SDVE)
            t_msq = E("dve", lambda v: v.tensor_mul(rowsS(1), rowsS(0), rowsS(0)),
                      deps=[t_mf], inc=SDVE)
            t_var = E("dve", lambda v: v.scalar_tensor_tensor(
                out=rowsS(2), in0=PB[1][0:1, 0:CH], scalar=1.0 / D, in1=rowsS(1),
                op0=ALU.mult, op1=ALU.subtract), deps=[t_ssq, t_msq], inc=SDVE)
            bfree[0] = t_mf
            bfree[1] = t_var
            t_ln = E("act", lambda s: s.activation(
                out=rowsS(3), in_=rowsS(2), func=AF.Ln, bias=epsc[0:1, 0:1], scale=1.0),
                deps=[t_var, t_wsf], inc=SACT)
            t_rstd = E("act", lambda s: s.activation(
                out=rowsS(2), in_=rowsS(3), func=AF.Exp, bias=0.0, scale=-0.5),
                deps=[t_ln], inc=SACT)
            t_rstdb = E("dve", lambda v: v.tensor_copy(rowbf(1), rowsS(2)),
                        deps=[t_rstd], inc=SDVE)
            rows_free = t_rstdb
            t_sbcm = E("pe", lambda t: t.matmul(
                PB[2][0:128, 0:CH], onesrow[0:1, 0:128], rowbf(1), start=True, stop=True),
                deps=[t_rstdb, bfree[2]], inc=SPE)
            t_mbcm = E("pe", lambda t: t.matmul(
                PB[3][0:128, 0:CH], onesrow[0:1, 0:128], rowbf(0), start=True, stop=True),
                deps=[t_mrbf, bfree[3]], inc=SPE)
            t_sbce = E("dve", lambda v: v.tensor_copy(sbc_v, PB[2][0:128, 0:CH]),
                       deps=[t_sbcm], inc=SDVE)
            t_mbce = E("dve", lambda v: v.tensor_copy(mbc_v, PB[3][0:128, 0:CH]),
                       deps=[t_mbcm], inc=SDVE)
            bfree[2] = t_sbce
            bfree[3] = t_mbce
            t_xc = [None] * DC
            for m in range(DC):
                t_xc[m] = E("dve", lambda v, m=m, par=par: v.tensor_sub(
                    xcb()[:, m * CH:(m + 1) * CH], xbufb(par)[:, m * CH:(m + 1) * CH], mbc_v),
                    deps=[t_mbce, t_xdma[c], xc_free], inc=SDVE)
            xb_last[par] = t_xc[DC - 1]
            t_mm = None
            for m in range(DC):
                t_mm = E("pe", lambda t, m=m: t.matmul(
                    PB[5][0:128, 0:CH], ws[:, m * 128:(m + 1) * 128], xcb()[:, m * CH:(m + 1) * CH],
                    start=(m == 0), stop=(m == 7)), deps=[t_xc[m], t_wsf, bfree[5]], inc=SPE)
            t_spev[c] = E("dve", lambda v, t0=t0: v.tensor_mul(
                spb_c(1 + t0, CH), PB[5][0:128, 0:CH], sbc_v), deps=[t_mm, t_sbce], inc=SDVE)
            bfree[5] = t_spev[c]
            t_spa = E("pe", lambda t, t0=t0: t.matmul(
                PB[4][0:128, 0:CH], at[:], spb_c(t0, CH), start=True, stop=True),
                deps=[t_spev[c], t_at, bfree[4]] + ([t_pad] if c == 0 else []), inc=SPE)
            t_mm = None
            for m in range(DC):
                t_mm = E("pe", lambda t, m=m: t.matmul(
                    PB[6][0:128, 0:CH], wib[:, m * 128:(m + 1) * 128], xcb()[:, m * CH:(m + 1) * CH],
                    start=(m == 0), stop=(m == 7)), deps=[t_xc[m], t_wib, bfree[6]], inc=SPE)
            t_w1 = E("dve", lambda v: v.tensor_mul(af(X1A, CH)[:, 0:CH], PB[6][0:128, 0:CH], sbc_v),
                     deps=[t_mm, t_sbce], inc=SDVE)
            t_wev[c] = E("dve", lambda v, t0=t0: v.tensor_add(
                wsb_c(W + t0, CH), af(X1A, CH)[:, 0:CH], PB[4][0:128, 0:CH]),
                deps=[t_w1, t_spa, t_init], inc=SDVE)
            if c == 0:
                t_wev[0] = E("dve", lambda v: v.tensor_sub(
                    wsb_c(W, 1), wsb_c(W, 1), csvec[:, 5:6]),
                    deps=[t_wev[0], t_acs], inc=SDVE)
            bfree[4] = t_wev[c]
            bfree[6] = t_wev[c]
            xsq_free = t_ssq
            xc_free = t_mm

        if upto == "pre":
            dbg_exit([
                (0, lambda: ABF[:, H0 + W:H0 + W + T], [t_wev[NCH - 1]], True),
                (128, lambda: ABF[:, F0 + 1:F0 + 1 + T], [t_spev[NCH - 1]], True),
            ])
            return nc

        all_w = t_wev[NCH - 1]
        def slot(i):
            return PB[i % 8][0:128, 0:R]
        ring_free0 = [bfree[k] for k in range(8)]
        t_fill = [None] * ROUNDS
        t_cmm = [None] * ROUNDS
        t_tanh = [None] * ROUNDS

        def emit_fill(i):
            deps = [all_w, t_identb]
            if i >= 8:
                deps.append(t_tanh[i - 8])
            else:
                deps.append(ring_free0[i % 8])
            if R > 1:
                srcf = lambda i=i: ABF[:, H0 + i:H0 + i + (R - 1) * S + 1:S]
            else:
                srcf = lambda i=i: ABF[:, H0 + W + i:H0 + W + i + 1]
            t_fill[i] = E("pe", lambda t, srcf=srcf, i=i: t.matmul(
                slot(i), identb[:], srcf(), start=True, stop=False), deps=deps, inc=SPE)

        for i in range(min(8, ROUNDS)):
            emit_fill(i)
        for i in range(ROUNDS):
            deps = [t_fill[i], t_bz, t_wvf]
            if i > 0:
                deps.append(t_tanh[i - 1])
            if W > 0 and i == W:
                t_fix = E("dve", lambda v, i=i: v.memset(thb_c(i * R, 1), 0.0),
                          deps=[t_tanh[i - 1]], inc=SDVE)
                deps.append(t_fix)
            t_cmm[i] = E("pe", lambda t, i=i: t.matmul(
                slot(i), at[:], thb_c(i * R, R), start=False, stop=True), deps=deps, inc=SPE)
            t_tanh[i] = E("act", lambda s, i=i: s.activation(
                out=thb_c((i + 1) * R, R), in_=slot(i), func=AF.Tanh, bias=bzc[:], scale=1.0),
                deps=[t_cmm[i]], inc=SACT)
            if i + 8 < ROUNDS:
                emit_fill(i + 8)
        t_scan = t_tanh[ROUNDS - 1]

        if upto == "scan":
            dbg_exit([
                (0, lambda: ABF[:, E0 + (W + 1) * R:E0 + (W + 1) * R + min((ROUNDS - W) * R, T)], [t_scan], True),
            ])
            return nc

        x0b = lambda par: AF32[:, X0A + par * 2048:X0A + (par + 1) * 2048]
        x1bv = lambda par: AF32[:, X1A + par * 2048:X1A + (par + 1) * 2048]
        kscrv = lambda p: AF32[:, WSB + p * CB:WSB + (p + 1) * CB]
        vscrv = lambda p: AF32[:, WSB + 512 + p * CB:WSB + 512 + (p + 1) * CB]
        m2row = lambda par: AF32[0:1, WSB + 1024 + par * CB:WSB + 1024 + (par + 1) * CB]
        rowsA = lambda par, i: AF32[0:1, RW1 + par * 512 + i * CB:RW1 + par * 512 + (i + 1) * CB]
        xbfc = lambda par: ABF[:, A0 + par * 2048:A0 + (par + 1) * 2048]
        xsqv = lambda par: ABF[:, A0 + 4096 + par * 2048:A0 + 4096 + (par + 1) * 2048]
        rbufv = ABF[:, B0:B0 + 2048]
        kabv = ABF[:, B0 + 2048:B0 + 4096]
        xrbv = lambda par: ABF[:, C0 + par * 2048:C0 + (par + 1) * 2048]
        xn2v = lambda par, m: ABF[:, D0 + par * 2056 + m * 257:D0 + par * 2056 + m * 257 + 257]
        xkbv = lambda par: ABF[:, H0 + par * 2048:H0 + (par + 1) * 2048]
        s2bcv = ABF[:, G0:G0 + CB]
        m2bcv = ABF[:, G0 + CB:G0 + 2 * CB]
        rstd2r = lambda par: ABF[0:1, G0 + 512 + par * CB:G0 + 512 + (par + 1) * CB]
        ycbv = lambda par: ABF[:, G0 + 1024 + par * CB:G0 + 1024 + (par + 1) * CB]
        dscrA = ABF[:, G0 + 1536:G0 + 1536 + CB]
        dscrB = ABF[:, G0 + 1792:G0 + 1792 + CB]
        epiv = lambda p: AF32[:, WSB + 1536 + p * CB:WSB + 1536 + (p + 1) * CB]

        bk = {i: t_scan for i in range(8)}
        bkh = {}
        t_x0 = [None] * NCB
        t_yc = [None] * NCB
        out2_last = [None] * NCB
        t_x1m = [[None] * DC for _ in range(NCB)]
        t_cast = [[None] * DC for _ in range(NCB)]
        t_sqx = [[None] * DC for _ in range(NCB)]
        t_sumT = [None] * NCB
        t_ssqT = [None] * NCB
        t_m2r = [None] * NCB
        t_varT = [None] * NCB
        t_rstd = [None] * NCB
        t_s2m = [None] * NCB
        t_m2m = [None] * NCB
        t_s2e = [None] * NCB
        t_m2e = [None] * NCB
        t_sub = [[None] * DC for _ in range(NCB)]
        t_xn = [[None] * DC for _ in range(NCB)]
        t_halo = [[None] * DC for _ in range(NCB)]
        t_dm = [[None] * DC for _ in range(NCB)]
        t_xr = [[None] * DC for _ in range(NCB)]
        t_xk = [[None] * DC for _ in range(NCB)]
        t_sig = [[None] * DC for _ in range(NCB)]
        t_rel = [[None] * DC for _ in range(NCB)]
        t_ksq = [[None] * DC for _ in range(NCB)]
        t_vm = [[None] * DC for _ in range(NCB)]
        t_x2 = [[None] * DC for _ in range(NCB)]
        t_store = [None] * NCB
        r_done = [None] * NCB
        k_done = [None] * NCB
        v_done = [None] * NCB
        pre_gp = t_sq[DC - 1]

        for c in range(NCB + 1):
            par = c % 2
            t0 = c * CB
            if c < NCB:
                dep0 = [gb_cols[-1], t_crcols] if c < 2 else [t_x1m[c - 2][DC - 1]]
                t_x0[c] = E("sp", lambda s, t0=t0, par=par: s.dma_start(
                    out=x0b(par).rearrange("p (r t) -> p r t", r=8),
                    in_=dram["xT"][:, t0:t0 + CB].rearrange("(r p) t -> p r t", p=128)),
                    deps=dep0, stream=f"x0_{par}")
                t_mm = None
                for rr in range(CB // S):
                    base = (W + 1) * R + (t0 // S + rr)
                    t_mm = E("pe", lambda t, base=base, rr=rr: t.matmul(
                        PB[2][0:128, rr * S:rr * S + S],
                        ct[:], ABF[:, E0 + base:E0 + base + (S - 1) * R + 1:R],
                        start=(rr == 0), stop=False),
                        deps=([t_scan, t_ct, bk[2]] if rr == 0 else [t_mm]), inc=SPE)
                t_mm = E("pe", lambda t, t0=t0: t.matmul(
                    PB[2][0:128, 0:CB], ct[:], spb_c(1 + t0, CB), start=False, stop=True),
                    deps=[t_mm], inc=SPE)
                ycw = [out2_last[c - 2]] if c >= 2 else []
                t_yc[c] = E("act", lambda s, par=par: s.activation(
                    out=ycbv(par), in_=PB[2][0:128, 0:CB], func=AF.Identity,
                    bias=csvec[:, 2:3], scale=1.0), deps=[t_mm, t_ccs] + ycw, inc=SACT)
                bk[2] = t_yc[c]
                for m in range(DC):
                    bb = 3 - (m % 2)
                    ep = m % 2
                    dd = [t_yc[c], t_wo, t_bobf, bkh.get(bb, bk[bb])]
                    t_mm = E("pe", lambda t, m=m, bb=bb, par=par: t.matmul(
                        PB[bb][0:128, 0:CB], wo[:, m * 128:(m + 1) * 128], ycbv(par),
                        start=True, stop=False), deps=dd, inc=SPE)
                    t_mm = E("pe", lambda t, m=m, bb=bb: t.matmul(
                        PB[bb][0:128, 0:CB], borowS[0:1, m * 128:(m + 1) * 128],
                        onesrow[0:1, 0:CB], start=False, stop=True), deps=[t_mm], inc=SPE)
                    if m == DC - 1:
                        out2_last[c] = t_mm
                    edeps = [t_mm, ga_cols[m]]
                    if m >= 2:
                        edeps.append(t_x1m[c][m - 2])
                    elif c >= 1:
                        edeps.append(t_x1m[c - 1][DC - 2 + ep])
                    t_sc = E("act", lambda s, m=m, bb=bb, ep=ep: s.activation(
                        out=epiv(ep), in_=PB[bb][0:128, 0:CB], func=AF.Identity,
                        bias=0.0, scale=colsA[:, 16 + m:17 + m]), deps=edeps, inc=SACT)
                    bkh[bb] = t_sc
                    wdeps = [t_sc, t_x0[c]]
                    if c >= 2:
                        wdeps.append(t_store[c - 2])
                    elif par == 0:
                        wdeps.append(t_w1)
                    else:
                        wdeps.append(rows_free)
                    t_x1m[c][m] = E("dve", lambda v, m=m, ep=ep, par=par: v.tensor_add(
                        x1bv(par)[:, m * CB:(m + 1) * CB], epiv(ep),
                        x0b(par)[:, m * CB:(m + 1) * CB]), deps=wdeps, inc=SDVE)
                    cdeps = [t_x1m[c][m], pre_gp]
                    if c >= 2:
                        cdeps += [t_sumT[c - 2], t_sub[c - 2][m]]
                    t_cast[c][m] = E("act", lambda s, m=m, par=par: s.activation(
                        out=xbfc(par)[:, m * CB:(m + 1) * CB],
                        in_=x1bv(par)[:, m * CB:(m + 1) * CB], func=AF.Identity,
                        bias=0.0, scale=1.0), deps=cdeps, inc=SACT)
                    sdeps = [t_cast[c][m]]
                    if c >= 2:
                        sdeps.append(t_ssqT[c - 2])
                    t_sqx[c][m] = E("gp", lambda g, m=m, par=par: g.tensor_mul(
                        xsqv(par)[:, m * CB:(m + 1) * CB],
                        xbfc(par)[:, m * CB:(m + 1) * CB],
                        xbfc(par)[:, m * CB:(m + 1) * CB]), deps=sdeps, inc=SGPC)
                bk[3] = bkh[3]
                bk[2] = bkh[2]
            if c >= 1:
                pr = (c - 1) % 2
                for m in range(DC):
                    bb = 4 + (m % 2)
                    t_mm = None
                    for kk in range(DC):
                        dd = [t_mm] if t_mm else [bk[bb], t_wrf, t_xr[c - 1][DC - 1]]
                        t_mm = E("pe", lambda t, m=m, kk=kk, bb=bb, pr=pr: t.matmul(
                            PB[bb][0:128, 0:CB], wr[:, kk * D + m * 128:kk * D + (m + 1) * 128],
                            xrbv(pr)[:, kk * CB:(kk + 1) * CB], start=(kk == 0), stop=(kk == 7)),
                            deps=dd, inc=SPE)
                    if m == DC - 1:
                        r_done[c - 1] = t_mm
                    rw = [t_vm[c - 2][m]] if c >= 2 else []
                    t_sig[c - 1][m] = E("act", lambda s, m=m, bb=bb: s.activation(
                        out=rbufv[:, m * CB:(m + 1) * CB], in_=PB[bb][0:128, 0:CB],
                        func=AF.Sigmoid, bias=crr[:, m:m + 1], scale=1.0),
                        deps=[t_mm, t_crcols] + rw, inc=SACT)
                    bk[bb] = t_sig[c - 1][m]
            if c < NCB:
                t_mm = None
                for m in range(DC):
                    dd = [t_cast[c][m]] + ([t_mm] if t_mm else [bk[0]])
                    t_mm = E("pe", lambda t, m=m, par=par: t.matmul(
                        PB[0][0:1, 0:CB], onescol[:], xbfc(par)[:, m * CB:(m + 1) * CB],
                        start=(m == 0), stop=(m == 7)), deps=dd, inc=SPE)
                t_sumT[c] = t_mm
                t_mm = None
                for m in range(DC):
                    dd = [t_sqx[c][m]] + ([t_mm] if t_mm else [bk[1]])
                    t_mm = E("pe", lambda t, m=m, par=par: t.matmul(
                        PB[1][0:1, 0:CB], onescol[:], xsqv(par)[:, m * CB:(m + 1) * CB],
                        start=(m == 0), stop=(m == 7)), deps=dd, inc=SPE)
                t_ssqT[c] = t_mm
                mdeps = [t_sumT[c]]
                if c >= 2:
                    mdeps.append(t_m2m[c - 2])
                t_m2r[c] = E("dve", lambda v, par=par: v.tensor_scalar(
                    out=m2row(par), in0=PB[0][0:1, 0:CB], scalar1=1.0 / D, scalar2=None,
                    op0=ALU.mult), deps=mdeps, inc=SDVE)
                qdeps = [t_m2r[c]]
                qdeps.append(t_rstd[c - 2] if c >= 2 else cbpl_cols[-1])
                t_msq = E("dve", lambda v, par=par: v.tensor_mul(
                    rowsA(par, 0), m2row(par), m2row(par)), deps=qdeps, inc=SDVE)
                t_varT[c] = E("dve", lambda v, par=par: v.scalar_tensor_tensor(
                    out=rowsA(par, 1), in0=PB[1][0:1, 0:CB], scalar=1.0 / D,
                    in1=rowsA(par, 0), op0=ALU.mult, op1=ALU.subtract),
                    deps=[t_ssqT[c], t_msq], inc=SDVE)
                t_ln = E("act", lambda s, par=par: s.activation(
                    out=rowsA(par, 0), in_=rowsA(par, 1), func=AF.Ln,
                    bias=epsc[0:1, 0:1], scale=1.0), deps=[t_varT[c], t_msq], inc=SACT)
                rw = [t_s2m[c - 2]] if c >= 2 else []
                t_rstd[c] = E("act", lambda s, par=par: s.activation(
                    out=rstd2r(par), in_=rowsA(par, 0), func=AF.Exp, bias=0.0,
                    scale=-0.5), deps=[t_ln] + rw, inc=SACT)
                t_s2m[c] = E("pe", lambda t, par=par: t.matmul(
                    PB[6][0:128, 0:CB], onesrow[0:1, 0:128], rstd2r(par),
                    start=True, stop=True), deps=[t_rstd[c], bk[6]], inc=SPE)
                s2r = [t_xn[c - 1][DC - 1]] if c >= 1 else []
                t_s2e[c] = E("dve", lambda v: v.tensor_copy(s2bcv, PB[6][0:128, 0:CB]),
                             deps=[t_s2m[c]] + s2r, inc=SDVE)
                t_m2m[c] = E("pe", lambda t, par=par: t.matmul(
                    PB[7][0:128, 0:CB], onesrowf[0:1, 0:128], m2row(par),
                    start=True, stop=True), deps=[t_m2r[c], bk[7]], inc=SPE)
                m2rd = [t_sub[c - 1][DC - 1]] if c >= 1 else []
                t_m2e[c] = E("dve", lambda v: v.tensor_copy(m2bcv, PB[7][0:128, 0:CB]),
                             deps=[t_m2m[c]] + m2rd, inc=SDVE)
                bk[6] = t_s2e[c]
                bk[7] = t_m2e[c]
            if c >= 1:
                pr = (c - 1) % 2
                for m in range(DC):
                    bb = 6 + (m % 2)
                    kp = m % 2
                    t_mm = None
                    for kk in range(DC):
                        dd = [t_mm] if t_mm else [bk[bb], t_wkf, t_xk[c - 1][DC - 1]]
                        t_mm = E("pe", lambda t, m=m, kk=kk, bb=bb, pr=pr: t.matmul(
                            PB[bb][0:128, 0:CB], wk[:, kk * D + m * 128:kk * D + (m + 1) * 128],
                            xkbv(pr)[:, kk * CB:(kk + 1) * CB], start=(kk == 0), stop=(kk == 7)),
                            deps=dd, inc=SPE)
                    if m == DC - 1:
                        k_done[c - 1] = t_mm
                    if m >= 2:
                        kw = [t_ksq[c - 1][m - 2]]
                    elif c >= 2:
                        kw = [t_ksq[c - 2][DC - 2 + kp]]
                    else:
                        kw = []
                    t_rel[c - 1][m] = E("act", lambda s, m=m, bb=bb, kp=kp: s.activation(
                        out=kscrv(kp), in_=PB[bb][0:128, 0:CB], func=AF.Relu,
                        bias=crk[:, m:m + 1], scale=1.0), deps=[t_mm, t_crcols] + kw, inc=SACT)
                    bk[bb] = t_rel[c - 1][m]
                    kabw = [v_done[c - 2]] if c >= 2 else []
                    t_ksq[c - 1][m] = E("gp", lambda g, m=m, kp=kp: g.tensor_mul(
                        kabv[:, m * CB:(m + 1) * CB], kscrv(kp), kscrv(kp)),
                        deps=[t_rel[c - 1][m]] + kabw, inc=SGPC)
            if c < NCB:
                for m in range(DC):
                    sdeps = [t_m2e[c], t_cast[c][m]]
                    if m > 0:
                        sdeps.append(t_xn[c][m - 1])
                    elif c >= 1:
                        sdeps.append(t_xn[c - 1][DC - 1])
                    t_sub[c][m] = E("gp", lambda g, m=m, par=par: g.tensor_sub(
                        dscrA, xbfc(par)[:, m * CB:(m + 1) * CB], m2bcv),
                        deps=sdeps, inc=SGPC)
                    ndeps = [t_sub[c][m], t_s2e[c]]
                    if c == 0:
                        ndeps.append(t_wvf)
                    if c >= 1:
                        ndeps.append(t_halo[c - 1][m])
                    if c >= 2:
                        ndeps.append(t_dm[c - 2][m])
                    t_xn[c][m] = E("dve", lambda v, m=m, par=par: v.tensor_mul(
                        xn2v(par, m)[:, 1:1 + CB], dscrA, s2bcv), deps=ndeps, inc=SDVE)
                for m in range(DC):
                    if c == 0:
                        t_halo[c][m] = E("dve", lambda v, m=m, par=par: v.tensor_copy(
                            xn2v(par, m)[:, 0:1], hcols[:, m:m + 1]),
                            deps=[t_hcols, t_wvf, pre_gp], inc=SDVE)
                    else:
                        hdeps = [t_xn[c - 1][m]]
                        if c >= 2:
                            hdeps.append(t_dm[c - 2][m])
                        t_halo[c][m] = E("dve", lambda v, m=m, par=par: v.tensor_copy(
                            xn2v(par, m)[:, 0:1], xn2v(1 - par, m)[:, CB:CB + 1]),
                            deps=hdeps, inc=SDVE)
                    ddeps = [t_xn[c][m], t_halo[c][m]]
                    if m > 0:
                        ddeps.append(t_xk[c][m - 1])
                    t_dm[c][m] = E("gp", lambda g, m=m, par=par: g.tensor_sub(
                        dscrB, xn2v(par, m)[:, 1:1 + CB], xn2v(par, m)[:, 0:CB]),
                        deps=ddeps, inc=SGPC)
                    xrw = [r_done[c - 2]] if c >= 2 else []
                    t_xr[c][m] = E("dve", lambda v, m=m, par=par: v.scalar_tensor_tensor(
                        out=xrbv(par)[:, m * CB:(m + 1) * CB], in0=dscrB,
                        scalar=murS[:, m:m + 1], in1=xn2v(par, m)[:, 0:CB],
                        op0=ALU.mult, op1=ALU.add), deps=[t_dm[c][m], t_mur] + xrw, inc=SDVE)
                    xkw = [k_done[c - 2]] if c >= 2 else []
                    t_xk[c][m] = E("dve", lambda v, m=m, par=par: v.scalar_tensor_tensor(
                        out=xkbv(par)[:, m * CB:(m + 1) * CB], in0=dscrB,
                        scalar=mukS[:, m:m + 1], in1=xn2v(par, m)[:, 0:CB],
                        op0=ALU.mult, op1=ALU.add), deps=[t_dm[c][m], t_muk] + xkw, inc=SDVE)
            if c >= 1:
                pr = (c - 1) % 2
                vfree = {0: (t_m2r[c] if c < NCB else t_vm[c - 2][DC - 2]),
                         1: (t_varT[c] if c < NCB else t_vm[c - 2][DC - 1])}
                for m in range(DC):
                    bb = m % 2
                    vp = m % 2
                    t_mm = None
                    for kk in range(DC):
                        dd = [t_mm] if t_mm else [t_ksq[c - 1][DC - 1], t_wvf, vfree[bb]]
                        t_mm = E("pe", lambda t, m=m, kk=kk, bb=bb: t.matmul(
                            PB[bb][0:128, 0:CB], wv[:, kk * D + m * 128:kk * D + (m + 1) * 128],
                            kabv[:, kk * CB:(kk + 1) * CB], start=(kk == 0), stop=(kk == 7)),
                            deps=dd, inc=SPE)
                    if m == DC - 1:
                        v_done[c - 1] = t_mm
                    if m >= 2:
                        vwm = [t_x2[c - 1][m - 2]]
                    elif c >= 2:
                        vwm = [t_x2[c - 2][DC - 2 + vp]]
                    else:
                        vwm = []
                    t_vm[c - 1][m] = E("dve", lambda v, m=m, bb=bb, vp=vp: v.tensor_mul(
                        vscrv(vp), rbufv[:, m * CB:(m + 1) * CB], PB[bb][0:128, 0:CB]),
                        deps=[t_mm, t_sig[c - 1][m]] + vwm, inc=SDVE)
                    vfree[bb] = t_vm[c - 1][m]
                    t_x2[c - 1][m] = E("dve", lambda v, m=m, vp=vp, pr=pr: v.tensor_add(
                        x1bv(pr)[:, m * CB:(m + 1) * CB], x1bv(pr)[:, m * CB:(m + 1) * CB],
                        vscrv(vp)),
                        deps=[t_vm[c - 1][m], t_x1m[c - 1][m], t_cast[c - 1][m],
                              t_sqx[c - 1][m]], inc=SDVE)
                bk[0] = t_vm[c - 1][DC - 2]
                bk[1] = t_vm[c - 1][DC - 1]
                t_store[c - 1] = E("sp", lambda s, c=c, pr=pr: s.dma_start(
                    out=yout[:, (c - 1) * CB:c * CB].rearrange("(r p) t -> p r t", p=128),
                    in_=x1bv(pr).rearrange("p (r t) -> p r t", r=8)),
                    deps=[t_x2[c - 1][DC - 1]], stream=f"yo_{pr}")

        E("gp", lambda g: g.engine_nop(), deps=[t_store[NCB - 2], t_store[NCB - 1]])

        sch.lint()
        import os as _os
        if _os.environ.get("KRACE"):
            for issue in sch.racecheck()[:300]:
                print("RACE:", issue)

        @block.tensor
        def _(t):
            sch.run("pe", t)

        @block.scalar
        def _(s):
            sch.run("act", s)

        @block.vector
        def _(v):
            sch.run("dve", v)

        @block.gpsimd
        def _(g):
            sch.run("gp", g)

        @block.sync
        def _(s):
            sch.run("sp", s)

    return nc


_NC_CACHE = None


def _get_nc():
    global _NC_CACHE
    if _NC_CACHE is None:
        _NC_CACHE = build_nc()
    return _NC_CACHE


def kernel(**inputs):
    x = np.asarray(inputs["L0_attn_x"], dtype=np.float32)
    Wi = np.asarray(inputs["L0_attn_Wi"], dtype=np.float32)
    bi = np.asarray(inputs["L0_attn_bi"], dtype=np.float32)
    Ws = np.asarray(inputs["L0_attn_Ws"], dtype=np.float32)
    bs = np.asarray(inputs["L0_attn_bs"], dtype=np.float32)
    Wo = np.asarray(inputs["L0_attn_Wo"], dtype=np.float32)
    bo = np.asarray(inputs["L0_attn_bo"], dtype=np.float32)
    A = np.asarray(inputs["L0_attn_A"], dtype=np.float32)
    B = np.asarray(inputs["L0_attn_B"], dtype=np.float32)
    C = np.asarray(inputs["L0_attn_C"], dtype=np.float32)
    W_r = np.asarray(inputs["L0_ffn_W_r"], dtype=np.float32)
    W_k = np.asarray(inputs["L0_ffn_W_k"], dtype=np.float32)
    W_v = np.asarray(inputs["L0_ffn_W_v"], dtype=np.float32)
    mu_r = np.asarray(inputs["L0_ffn_mu_r"], dtype=np.float32)
    mu_k = np.asarray(inputs["L0_ffn_mu_k"], dtype=np.float32)
    cmat = np.asarray(inputs["L0_c"], dtype=np.float32)
    wmod = np.asarray(inputs["L0_wmod"], dtype=np.float32)

    shared = {
        "wrT": np.ascontiguousarray(W_r.T), "wkT": np.ascontiguousarray(W_k.T),
        "wvT": np.ascontiguousarray(W_v.T), "woT": np.ascontiguousarray(Wo.T),
        "wsT": np.ascontiguousarray(Ws.T), "wiT": np.ascontiguousarray(Wi.T),
        "wiN": np.ascontiguousarray(Wi),
        "aT": np.ascontiguousarray(A.T), "bT": np.ascontiguousarray(B.T),
        "cT": np.ascontiguousarray(C.T),
        "wmod": np.ascontiguousarray(wmod),
        "bi": np.ascontiguousarray(bi.reshape(NST, 1)),
        "bs": np.ascontiguousarray(bs.reshape(NST, 1)),
        "borow": np.ascontiguousarray(bo.reshape(1, D)),
        "murc": np.ascontiguousarray(mu_r.reshape(DC, 128).T),
        "mukc": np.ascontiguousarray(mu_k.reshape(DC, 128).T),
    }
    in_maps = []
    for b in range(8):
        m = dict(shared)
        m["xT"] = np.ascontiguousarray(x[b].T)
        m["ccol"] = np.ascontiguousarray(cmat[b].reshape(DC, 128).T)
        in_maps.append(m)

    nc = _get_nc()
    res = run_bass_kernel_spmd(nc, in_maps, core_ids=list(range(8)))
    out = np.stack([np.ascontiguousarray(r["yout"].T) for r in res.results], axis=0)
    return out.astype(np.float32)


# version 18
# speedup vs baseline: 1.5308x; 1.5308x over previous
import numpy as np
import ml_dtypes
import concourse.bass as bass
import concourse.mybir as mybir
from concourse.bass_utils import run_bass_kernel_spmd

F32 = mybir.dt.float32
BF16 = mybir.dt.bfloat16
AF = mybir.ActivationFunctionType
ALU = mybir.AluOpType

T, D, DC, NST = 4096, 1024, 8, 128
CH = 512
NCH = T // CH
CB = 256
NCB = T // CB
R = 64
W = 96 if R > 1 else 0
S = T // R
ROUNDS = S + W
SPB = max(1, 512 // R)
RINGB = 4
EPS = 1e-5


class _Tok:
    __slots__ = ("sem", "val")
    def __init__(self, sem, val):
        self.sem, self.val = sem, val


class _Sem:
    def __init__(self, name):
        self.name, self.total, self.handle = name, 0, None


class _Sched:
    def __init__(self):
        self.ops, self.sems = [], []
    def sem(self, name):
        s = _Sem(name); self.sems.append(s); return s
    def emit(self, eng, fn, deps=(), inc=None, inc_n=1, stream=None):
        deps = list(deps)
        if stream is not None:
            if not hasattr(self, "streams"):
                self.streams = {}
            sem, prev = self.streams.get(stream, (None, None))
            if sem is None:
                sem = self.sem("st_" + stream)
                sem.handle = next(self.handle_pool)
            if prev is not None:
                deps.append(prev)
            inc, inc_n = sem, 16
        waits = [(t.sem, t.val) for t in deps if t is not None and t.val > 0]
        tok = None
        if inc is not None:
            inc.total += inc_n
            tok = _Tok(inc, inc.total)
        if stream is not None:
            self.streams[stream] = (inc, tok)
        self.ops.append((eng, fn, waits, (inc, inc_n) if inc else None))
        return tok
    def lint(self):
        for _, _, waits, _ in self.ops:
            for sem, val in waits:
                assert val <= sem.total, f"wait {sem.name}>={val} > total {sem.total}"
        engs = {}
        for i, (e, fn, waits, inc) in enumerate(self.ops):
            engs.setdefault(e, []).append((waits, inc, i))
        ptr = {e: 0 for e in engs}
        val = {id(s): 0 for s in self.sems}
        progress = True
        while progress:
            progress = False
            for e, lst in engs.items():
                while ptr[e] < len(lst):
                    waits, inc, i = lst[ptr[e]]
                    if all(val[id(s)] >= v for s, v in waits):
                        if inc is not None:
                            val[id(inc[0])] += inc[1]
                        ptr[e] += 1
                        progress = True
                    else:
                        break
        stuck = {e: (ptr[e], len(lst)) for e, lst in engs.items() if ptr[e] < len(lst)}
        if stuck:
            detail = []
            for e, lst in engs.items():
                if ptr[e] < len(lst):
                    waits, inc, i = lst[ptr[e]]
                    why = [(s.name, v, val[id(s)]) for s, v in waits if val[id(s)] < v]
                    detail.append(f"{e}@op{i}: needs {why}")
            raise AssertionError("DEADLOCK: " + "; ".join(detail))
    def racecheck(self):
        class _Inst:
            def then_inc(self, *a, **k):
                return self
        class _Rec:
            def __init__(self):
                self.acc = []
            def _ap(self, ap, kind):
                try:
                    t = ap.tensor
                    name = t.name
                    pairs = [tuple(x) for x in ap.ap]
                    pstride = int(pairs[0][0]) if pairs else 1
                    lo = int(ap.offset) % max(pstride, 1) if pstride > 1 else int(ap.offset)
                    hi = lo
                    for pair in pairs[1:]:
                        st, ct = int(pair[0]), int(pair[1])
                        hi += abs(st) * (ct - 1)
                    self.acc.append((kind, name, lo, hi + 1))
                except Exception:
                    pass
            def __getattr__(self, nm):
                def h(*args, **kw):
                    aps = []
                    outs = []
                    for i, a in enumerate(args):
                        if hasattr(a, "tensor"):
                            (outs if i == 0 else aps).append(a)
                    for k, a in kw.items():
                        if hasattr(a, "tensor"):
                            (outs if k.startswith("out") else aps).append(a)
                    if nm in ("dma_start",):
                        pass
                    for a in outs:
                        self._ap(a, "w")
                    for a in aps:
                        self._ap(a, "r")
                    if nm in ("memset", "memzero") and outs == [] and aps:
                        self.acc[-1] = ("w",) + self.acc[-1][1:]
                    return _Inst()
                return h
        N = len(self.ops)
        accs = []
        for e, fn, waits, inc in self.ops:
            r = _Rec()
            try:
                fn(r)
            except Exception:
                pass
            accs.append(r.acc)
        import numpy as np
        anc = np.zeros((N, N), dtype=bool)
        preds = [[] for _ in range(N)]
        last_by_eng = {}
        inc_points = {}
        cum = {}
        for i, (e, fn, waits, inc) in enumerate(self.ops):
            if e in last_by_eng:
                preds[i].append(last_by_eng[e])
            last_by_eng[e] = i
            if inc is not None:
                s, n = inc
                cum[s] = cum.get(s, 0) + n
                inc_points.setdefault(s, []).append((cum[s], i))
        for i, (e, fn, waits, inc) in enumerate(self.ops):
            for s, v in waits:
                for cv, j in inc_points.get(s, []):
                    if cv >= v:
                        preds[i].append(j)
                        break
        order = list(range(N))
        for i in order:
            for j in preds[i]:
                anc[i] |= anc[j]
                anc[i][j] = True
        psum_names = {f"pb{k}": k for k in range(8)}
        issues = []
        def tag(i):
            return f"L{self.ops[i][1].__code__.co_firstlineno}"
        for i in range(N):
            ei = self.ops[i][0]
            for j in range(i + 1, N):
                ej = self.ops[j][0]
                if anc[j][i] or anc[i][j] or ei == ej:
                    continue
                for (ki, ni, li, hi) in accs[i]:
                    for (kj, nj, lj, hj) in accs[j]:
                        if ni != nj:
                            continue
                        if ni in psum_names:
                            if (ki == "w" or kj == "w"):
                                issues.append((i, j, ni, "psum-bank", ki, kj, tag(i), tag(j)))
                        else:
                            if (ki == "w" or kj == "w") and li < hj and lj < hi:
                                issues.append((i, j, ni, f"[{max(li,lj)}:{min(hi,hj)})", ki, kj, tag(i), tag(j)))
        return issues

    def run(self, eng_name, eng, skip_sem=None):
        last = {}
        for e, fn, waits, inc in self.ops:
            if e != eng_name:
                continue
            for sem, val in waits:
                if sem is skip_sem:
                    continue
                if last.get(sem, 0) < val:
                    eng.wait_ge(sem.handle, val)
                    last[sem] = val
            inst = fn(eng)
            if inc is not None:
                inst.then_inc(inc[0].handle, inc[1])


def build_nc(upto="full", ops_limit=None):
    nc = bass.Bass("TRN2", target_bir_lowering=False)
    dram = {}
    for nm, shp in [("xT", [D, T]), ("wrT", [D, D]), ("wkT", [D, D]), ("wvT", [D, D]),
                    ("woT", [NST, D]), ("wsT", [D, NST]), ("wiT", [D, NST]), ("wiN", [NST, D]),
                    ("aT", [NST, NST]), ("bT", [NST, NST]), ("cT", [NST, NST]),
                    ("wmod", [D, 6144]), ("ccol", [NST, DC]), ("bi", [NST, 1]), ("bs", [NST, 1]),
                    ("borow", [1, D]), ("murc", [NST, DC]), ("mukc", [NST, DC])]:
        dram[nm] = nc.dram_tensor(nm, shp, F32, kind="ExternalInput").ap()
    yout = nc.dram_tensor("yout", [D, T], F32, kind="ExternalOutput").ap()
    x1sc = nc.dram_tensor("x1sc", [D, T], F32, kind="Internal").ap()

    sch = _Sched()
    SPE, SACT, SDVE, SGPC = (sch.sem(n) for n in ("pe", "act", "dve", "gpc"))
    DSP, DGP, SST = (sch.sem(n) for n in ("dsp", "dgp", "dst"))
    E = sch.emit

    AF32_N = 13664
    EPI = 13408
    ABF_N = 41152
    import contextlib
    with contextlib.ExitStack() as _st:
        _e = _st.enter_context
        wr = _e(nc.sbuf_tensor("wr", [128, DC * D], BF16))
        wk = _e(nc.sbuf_tensor("wk", [128, DC * D], BF16))
        wv = _e(nc.sbuf_tensor("wv", [128, DC * D], BF16))
        wo = _e(nc.sbuf_tensor("wo", [128, D], BF16))
        ws = _e(nc.sbuf_tensor("ws", [128, D], BF16))
        wib = _e(nc.sbuf_tensor("wib", [128, D], BF16))
        wiTs = _e(nc.sbuf_tensor("wiTs", [128, D], BF16))
        wiNs = _e(nc.sbuf_tensor("wiNs", [128, D], BF16))
        wia = _e(nc.sbuf_tensor("wia", [128, D], BF16))
        at = _e(nc.sbuf_tensor("at", [128, NST], BF16))
        btS = _e(nc.sbuf_tensor("btS", [128, NST], BF16))
        ct = _e(nc.sbuf_tensor("ct", [128, NST], BF16))
        identf = _e(nc.sbuf_tensor("identf", [128, 1], F32))
        identb = _e(nc.sbuf_tensor("identb", [128, 128], BF16))
        onescol = _e(nc.sbuf_tensor("onescol", [128, 1], BF16))
        onesrow = _e(nc.sbuf_tensor("onesrow", [1, CH], BF16))
        onesrowf = _e(nc.sbuf_tensor("onesrowf", [1, NST], F32))
        epsc = _e(nc.sbuf_tensor("epsc", [128, 1], F32))
        colsA = _e(nc.sbuf_tensor("colsA", [128, 32], F32))
        sab = _e(nc.sbuf_tensor("sab", [128, DC], BF16))
        sbb = _e(nc.sbuf_tensor("sbb", [128, DC], BF16))
        hcols = _e(nc.sbuf_tensor("hcols", [128, DC], BF16))
        crr = _e(nc.sbuf_tensor("crr", [128, DC], F32))
        crk = _e(nc.sbuf_tensor("crk", [128, DC], F32))
        siluf = _e(nc.sbuf_tensor("siluf", [128, DC], F32))
        silub = _e(nc.sbuf_tensor("silub", [128, DC], BF16))
        csvec = _e(nc.sbuf_tensor("csvec", [128, 8], F32))
        csb = _e(nc.sbuf_tensor("csb", [128, 2], BF16))
        bzc = _e(nc.sbuf_tensor("bzc", [128, 1], F32))
        murS = _e(nc.sbuf_tensor("murS", [128, DC], F32))
        mukS = _e(nc.sbuf_tensor("mukS", [128, DC], F32))
        borowS = _e(nc.sbuf_tensor("borowS", [1, D], BF16))
        AF32 = _e(nc.sbuf_tensor("AF32", [128, AF32_N], F32))
        ABF = _e(nc.sbuf_tensor("ABF", [128, ABF_N], BF16))
        pb0 = _e(nc.psum_tensor("pb0", [128, 512], F32))
        pb1 = _e(nc.psum_tensor("pb1", [128, 512], F32))
        pb2 = _e(nc.psum_tensor("pb2", [128, 512], F32))
        pb3 = _e(nc.psum_tensor("pb3", [128, 512], F32))
        pb4 = _e(nc.psum_tensor("pb4", [128, 512], F32))
        pb5 = _e(nc.psum_tensor("pb5", [128, 512], F32))
        pb6 = _e(nc.psum_tensor("pb6", [128, 512], F32))
        pb7 = _e(nc.psum_tensor("pb7", [128, 512], F32))
        h0 = _e(nc.semaphore("h0"))
        h1 = _e(nc.semaphore("h1"))
        h2 = _e(nc.semaphore("h2"))
        h3 = _e(nc.semaphore("h3"))
        h4 = _e(nc.semaphore("h4"))
        h5 = _e(nc.semaphore("h5"))
        h6 = _e(nc.semaphore("h6"))
        _sempool = [_e(nc.semaphore(f"spool{i}")) for i in range(48)]
        block = _e(nc.Block())

        for s, h in zip((SPE, SACT, SDVE, SGPC, DSP, DGP, SST), (h0, h1, h2, h3, h4, h5, h6)):
            s.handle = h
        sch.handle_pool = iter(_sempool)
        PB = [pb0, pb1, pb2, pb3, pb4, pb5, pb6, pb7]

        def dbg_exit(dumps):
            toks = []
            for row0, src_fn, deps, cast in dumps:
                eng = "gp" if cast else "sp"
                toks.append(E(eng, lambda s, row0=row0, src_fn=src_fn: s.dma_start(
                    out=yout[row0:row0 + src_fn().shape[0], 0:src_fn().shape[1]],
                    in_=src_fn(), allow_slow_non_contiguous=True), deps=deps, inc=SST, inc_n=16))
            E("gp", lambda g: g.engine_nop(), deps=[_Tok(SST, 16 * len(dumps))])
            sch.lint()

            @block.tensor
            def _(t):
                sch.run("pe", t, skip_sem=SPE)

            @block.scalar
            def _(s):
                sch.run("act", s)

            @block.vector
            def _(v):
                sch.run("dve", v)

            @block.gpsimd
            def _(g):
                sch.run("gp", g)

            @block.sync
            def _(s):
                sch.run("sp", s)

        X0A = 0
        X1A = 4096
        WSB = 8192
        RW1 = 12384
        assert WSB + max(W + T, T) // 1 <= RW1 or True
        def af(c0, w):
            return AF32[:, c0:c0 + w]
        def afr(c0, w):
            return AF32[0:1, c0:c0 + w]

        A0, B0, C0, D0, E0, F0, G0 = 0, 8192, 12288, 16384, 20496, 30800, 34897
        H0 = 36960
        def ab(c0, w):
            return ABF[:, c0:c0 + w]
        def abr(c0, w):
            return ABF[0:1, c0:c0 + w]

        t_at = E("gp", lambda g: g.dma_start(out=at[:], in_=dram["aT"]), stream="w_at")
        t_bt = E("gp", lambda g: g.dma_start(out=btS[:], in_=dram["bT"]), stream="w_bt")
        t_ct = E("gp", lambda g: g.dma_start(out=ct[:], in_=dram["cT"]), stream="w_ct")
        t_win = E("gp", lambda g: g.dma_start(out=wiNs[:], in_=dram["wiN"]), stream="w_win")
        t_wo = E("gp", lambda g: g.dma_start(out=wo[:], in_=dram["woT"]), stream="w_wo")
        t_ws = E("gp", lambda g: g.dma_start(
            out=ws[:].rearrange("p (r n) -> p r n", r=8), in_=dram["wsT"].rearrange("(r p) n -> p r n", p=128)), stream="w_ws")
        t_wiT = E("gp", lambda g: g.dma_start(
            out=wiTs[:].rearrange("p (r n) -> p r n", r=8), in_=dram["wiT"].rearrange("(r p) n -> p r n", p=128)), stream="w_wiT")
        t_wr = E("gp", lambda g: g.dma_start(
            out=wr[:].rearrange("p (r d) -> p r d", r=8), in_=dram["wrT"].rearrange("(r p) d -> p r d", p=128)), stream="w_wr")
        t_wk = E("gp", lambda g: g.dma_start(
            out=wk[:].rearrange("p (r d) -> p r d", r=8), in_=dram["wkT"].rearrange("(r p) d -> p r d", p=128)), stream="w_wk")
        t_wv = E("gp", lambda g: g.dma_start(
            out=wv[:].rearrange("p (r d) -> p r d", r=8), in_=dram["wvT"].rearrange("(r p) d -> p r d", p=128)), stream="w_wv")
        t_bi = E("sp", lambda s: s.dma_start(out=csvec[:, 3:4], in_=dram["bi"]), stream="s_bi")
        t_bs = E("sp", lambda s: s.dma_start(out=csvec[:, 4:5], in_=dram["bs"]), stream="s_bs")
        t_mur = E("sp", lambda s: s.dma_start(out=murS[:], in_=dram["murc"]), stream="s_mur")
        t_muk = E("sp", lambda s: s.dma_start(out=mukS[:], in_=dram["mukc"]), stream="s_muk")
        t_bo = E("sp", lambda s: s.dma_start(out=afr(RW1, D), in_=dram["borow"]), stream="s_bo")
        t_cc = E("sp", lambda s: s.dma_start(out=siluf[:], in_=dram["ccol"]), stream="s_cc")

        t_id0 = E("dve", lambda v: v.memset(identf[:], 0.0), inc=SDVE)
        E("dve", lambda v: v.memset(identf[0:1, 0:1], 1.0), deps=[t_id0])
        t_ib0 = E("gp", lambda g: g.memset(identb[:], 0.0), inc=SGPC)
        t_identb = E("gp", lambda g: g.affine_select(
            out=identb[:], in_=identb[:], compare_op=ALU.not_equal, fill=1.0, base=0,
            pattern=[[-1, 128]], channel_multiplier=1), deps=[t_ib0], inc=SGPC)
        E("dve", lambda v: v.memset(onescol[:], 1.0))
        E("dve", lambda v: v.memset(onesrow[:], 1.0))
        E("dve", lambda v: v.memset(onesrowf[:], 1.0))
        E("dve", lambda v: v.memset(epsc[:], EPS))
        E("dve", lambda v: v.memset(ab(E0, R), 0.0))
        E("dve", lambda v: v.memset(ab(F0, 1), 0.0))
        t_init = E("dve", lambda v: v.memset(ABF[:, H0:H0 + max(W, 1)], 0.0), inc=SDVE)
        t_bobf = E("dve", lambda v: v.tensor_copy(borowS[:], afr(RW1, D)), deps=[t_bo], inc=SDVE)
        t_silu = E("act", lambda s: s.activation(out=siluf[:], in_=siluf[:], func=AF.Silu),
                   deps=[t_cc], inc=SACT)
        t_silub = E("dve", lambda v: v.tensor_copy(silub[:], siluf[:]), deps=[t_silu, t_bobf],
                    inc=SDVE)

        if upto == "load":
            if ops_limit is not None:
                del sch.ops[ops_limit:]
            dbg_exit([
                (0, lambda: siluf[:], [], False),
            ])
            return nc

        wmodb_v = lambda par: ABF[:, F0 + par * 2048:F0 + (par + 1) * 2048]
        wmodf_v = lambda par: AF32[:, X1A + par * 1024:X1A + (par + 1) * 1024]

        def mod_block(c0, nbf, prev_mm, bank_gate=()):
            ng = 4 if nbf else 2
            dma_t, mm_t = [None] * 8, [None] * 8
            for kk in range(8):
                par = kk % 2
                deps = [t_silub]
                if kk >= 2:
                    deps.append(mm_t[kk - 2])
                if kk < 2 and prev_mm is not None:
                    deps.append(prev_mm[6 + kk])
                if nbf:
                    dma_t[kk] = E("gp", lambda g, kk=kk, par=par, c0=c0: g.dma_start(
                        out=wmodb_v(par), in_=dram["wmod"][kk * 128:(kk + 1) * 128, c0:c0 + 2048]),
                        deps=deps, stream=f"wmodb{par}")
                else:
                    dma_t[kk] = E("gp", lambda g, kk=kk, par=par, c0=c0: g.dma_start(
                        out=wmodf_v(par), in_=dram["wmod"][kk * 128:(kk + 1) * 128, c0:c0 + 1024]),
                        deps=deps, stream=f"wmodf{par}")
                for gi in range(ng):
                    srcf = (lambda par=par, gi=gi: wmodb_v(par)[:, gi * 512:(gi + 1) * 512]) if nbf \
                        else (lambda par=par, gi=gi: wmodf_v(par)[:, gi * 512:(gi + 1) * 512])
                    lhs = (lambda kk=kk: silub[:, kk:kk + 1]) if nbf else (lambda kk=kk: siluf[:, kk:kk + 1])
                    deps = [dma_t[kk]] + (list(bank_gate) if kk == 0 else [])
                    mm_t[kk] = E("pe", lambda t, gi=gi, kk=kk, srcf=srcf, lhs=lhs: t.matmul(
                        PB[4 + gi][0:1, 0:512], lhs(), srcf(),
                        start=(kk == 0), stop=(kk == 7)), deps=deps, inc=SPE)
            return mm_t

        mmA = mod_block(0, True, None)
        tr_prev = _Tok(SPE, 0)

        def row_to_cols(src_row_fn, dst, dst0, n, deps):
            nonlocal tr_prev
            outs = []
            for i in range(n):
                tp = E("pe", lambda t, i=i: t.matmul(
                    PB[3][0:128, 0:1], src_row_fn(i), identf[0:1, 0:1],
                    start=True, stop=True),
                    deps=list(deps) + [tr_prev], inc=SPE)
                tr_prev = E("dve", lambda v, i=i: v.tensor_copy(
                    dst[:, dst0 + i:dst0 + i + 1], PB[3][0:128, 0:1]), deps=[tp], inc=SDVE)
                outs.append(tr_prev)
            return outs

        t_saev, t_caev = [], []
        for gi in range(2):
            t_saev.append(E("dve", lambda v, gi=gi: v.tensor_copy(
                afr(X0A + gi * 512, 512), PB[4 + gi][0:1, 0:512]), deps=[mmA[7]], inc=SDVE))
        for gi in range(2):
            t_caev.append(E("dve", lambda v, gi=gi: v.tensor_scalar(
                out=afr(RW1, D)[0:1, gi * 512:(gi + 1) * 512], in0=PB[6 + gi][0:1, 0:512],
                scalar1=1.0, scalar2=None, op0=ALU.add), deps=[mmA[7], t_silub], inc=SDVE))
        t_caplb = E("dve", lambda v: v.tensor_copy(abr(D0, D), afr(RW1, D)),
                    deps=t_caev, inc=SDVE)
        sab_cols = row_to_cols(lambda i: afr(X0A + 0, 1024)[0:1, i * 128:(i + 1) * 128],
                               sab, 0, DC, t_saev + [mmA[7]])
        capl_cols = row_to_cols(lambda i: afr(RW1, D)[0:1, i * 128:(i + 1) * 128],
                                colsA, 0, DC, t_caev)

        mmB = mod_block(3072, True, mmA, bank_gate=t_saev + t_caev)
        t_sbev = []
        for gi in range(2):
            t_sbev.append(E("dve", lambda v, gi=gi: v.tensor_copy(
                afr(X0A + gi * 512, 512), PB[4 + gi][0:1, 0:512]),
                deps=[mmB[7], sab_cols[-1]], inc=SDVE))
        t_cbev = []
        for gi in range(2):
            t_cbev.append(E("dve", lambda v, gi=gi: v.tensor_scalar(
                out=afr(RW1, D)[0:1, gi * 512:(gi + 1) * 512], in0=PB[6 + gi][0:1, 0:512],
                scalar1=1.0, scalar2=None, op0=ALU.add),
                deps=[mmB[7], capl_cols[-1], t_caplb], inc=SDVE))
        sbb_cols = row_to_cols(lambda i: afr(X0A + 0, 1024)[0:1, i * 128:(i + 1) * 128],
                               sbb, 0, DC, t_sbev + [mmB[7]])
        cbpl_cols = row_to_cols(lambda i: afr(RW1, D)[0:1, i * 128:(i + 1) * 128],
                                colsA, 8, DC, t_cbev)

        mmGa = mod_block(2048, False, mmB, bank_gate=t_sbev + t_cbev)
        t_gaev = E("act", lambda s: s.activation(
            out=afr(X0A + 512, D)[0:1, 0:512], in_=PB[4][0:1, 0:512], func=AF.Abs),
            deps=[mmGa[7], sbb_cols[-1], cbpl_cols[-1]], inc=SACT)
        t_gaev2 = E("act", lambda s: s.activation(
            out=afr(X0A + 512, D)[0:1, 512:1024], in_=PB[5][0:1, 0:512], func=AF.Abs),
            deps=[mmGa[7], t_gaev], inc=SACT)
        t_gm = E("dve", lambda v: v.tensor_scalar_max(
            afr(X0A + 512, D), afr(X0A + 512, D), 0.25), deps=[t_gaev2], inc=SDVE)
        t_gs1 = E("dve", lambda v: v.tensor_scalar(
            out=afr(X0A + 1536, D)[0:1, 0:512], in0=PB[4][0:1, 0:512],
            scalar1=0.0, scalar2=None, op0=ALU.is_ge), deps=[mmGa[7]], inc=SDVE)
        t_gs2 = E("dve", lambda v: v.tensor_scalar(
            out=afr(X0A + 1536, D)[0:1, 512:1024], in0=PB[5][0:1, 0:512],
            scalar1=0.0, scalar2=None, op0=ALU.is_ge), deps=[mmGa[7]], inc=SDVE)
        t_gs3 = E("dve", lambda v: v.tensor_scalar(
            out=afr(X0A + 1536, D), in0=afr(X0A + 1536, D),
            scalar1=2.0, scalar2=-1.0, op0=ALU.mult, op1=ALU.add), deps=[t_gs1, t_gs2], inc=SDVE)
        t_ga = E("dve", lambda v: v.tensor_mul(
            afr(X0A + 512, D), afr(X0A + 512, D), afr(X0A + 1536, D)),
            deps=[t_gm, t_gs3], inc=SDVE)
        ga_cols = row_to_cols(lambda i: afr(X0A + 512, D)[0:1, i * 128:(i + 1) * 128],
                              colsA, 16, DC, [t_ga])

        mmGb = mod_block(5120, False, mmGa, bank_gate=[t_ga])
        t_gbev = E("act", lambda s: s.activation(
            out=afr(X0A + 512, D)[0:1, 0:512], in_=PB[4][0:1, 0:512], func=AF.Abs),
            deps=[mmGb[7], ga_cols[-1]], inc=SACT)
        t_gbev2 = E("act", lambda s: s.activation(
            out=afr(X0A + 512, D)[0:1, 512:1024], in_=PB[5][0:1, 0:512], func=AF.Abs),
            deps=[mmGb[7], t_gbev], inc=SACT)
        t_gm = E("dve", lambda v: v.tensor_scalar_max(
            afr(X0A + 512, D), afr(X0A + 512, D), 0.25), deps=[t_gbev2], inc=SDVE)
        t_gs1 = E("dve", lambda v: v.tensor_scalar(
            out=afr(X0A + 1536, D)[0:1, 0:512], in0=PB[4][0:1, 0:512],
            scalar1=0.0, scalar2=None, op0=ALU.is_ge), deps=[mmGb[7]], inc=SDVE)
        t_gs2 = E("dve", lambda v: v.tensor_scalar(
            out=afr(X0A + 1536, D)[0:1, 512:1024], in0=PB[5][0:1, 0:512],
            scalar1=0.0, scalar2=None, op0=ALU.is_ge), deps=[mmGb[7]], inc=SDVE)
        t_gs3 = E("dve", lambda v: v.tensor_scalar(
            out=afr(X0A + 1536, D), in0=afr(X0A + 1536, D),
            scalar1=2.0, scalar2=-1.0, op0=ALU.mult, op1=ALU.add), deps=[t_gs1, t_gs2], inc=SDVE)
        t_gb = E("dve", lambda v: v.tensor_mul(
            afr(X0A + 512, D), afr(X0A + 512, D), afr(X0A + 1536, D)),
            deps=[t_gm, t_gs3], inc=SDVE)
        gb_cols = row_to_cols(lambda i: afr(X0A + 512, D)[0:1, i * 128:(i + 1) * 128],
                              colsA, 24, DC, [t_gb])
        t_gbb = E("dve", lambda v: v.tensor_copy(abr(D0 + 1024, D), afr(X0A + 512, D)),
                  deps=[t_gb], inc=SDVE)

        if upto == "mod":
            dbg_exit([
                (0, lambda: colsA[:], [capl_cols[-1], cbpl_cols[-1], ga_cols[-1], gb_cols[-1]], False),
                (128, lambda: sab[:], [sab_cols[-1]], True),
                (256, lambda: sbb[:], [sbb_cols[-1]], True),
            ])
            return nc

        if upto == "mod":
            if ops_limit is not None:
                del sch.ops[ops_limit:]
            dbg_exit([
                (0, lambda: colsA[:], [capl_cols[-1], cbpl_cols[-1], ga_cols[-1], gb_cols[-1]] if ops_limit is None else [], False),
                (128, lambda: sab[:], [sab_cols[-1]] if ops_limit is None else [], True),
                (256, lambda: sbb[:], [sbb_cols[-1]] if ops_limit is None else [], True),
                (384, lambda: AF32[0:1, RW1:RW1 + D], [capl_cols[-1]] if ops_limit is None else [], False),
                (385, lambda: AF32[0:1, X0A:X0A + D], [sab_cols[-1]] if ops_limit is None else [], False),
            ])
            return nc

        t_mm = None
        for kk in range(DC):
            t_mm = E("pe", lambda t, kk=kk: t.matmul(
                PB[6][0:128, 0:1], ws[:, kk * 128:(kk + 1) * 128], sab[:, kk:kk + 1],
                start=(kk == 0), stop=(kk == 7)),
                deps=[t_ws, sab_cols[kk], mmGb[7]] + (t_cbev if kk == 0 else []), inc=SPE)
        t_cs = E("dve", lambda v: v.scalar_tensor_tensor(
            out=csvec[:, 0:1], in0=PB[6][0:128, 0:1], scalar=1.0, in1=csvec[:, 4:5],
            op0=ALU.mult, op1=ALU.add), deps=[t_mm, t_bs], inc=SDVE)
        t_mm = None
        for kk in range(DC):
            t_mm = E("pe", lambda t, kk=kk: t.matmul(
                PB[6][0:128, 1:2], wiTs[:, kk * 128:(kk + 1) * 128], sab[:, kk:kk + 1],
                start=(kk == 0), stop=(kk == 7)),
                deps=[t_wiT, sab_cols[kk]] + ([t_cs] if kk == 0 else []), inc=SPE)
        t_cu = E("dve", lambda v: v.scalar_tensor_tensor(
            out=csvec[:, 1:2], in0=PB[6][0:128, 1:2], scalar=1.0, in1=csvec[:, 3:4],
            op0=ALU.mult, op1=ALU.add), deps=[t_mm, t_bi], inc=SDVE)
        t_csb = E("dve", lambda v: v.tensor_copy(csb[:], csvec[:, 0:2]), deps=[t_cs, t_cu],
                  inc=SDVE)
        t_mm = E("pe", lambda t: t.matmul(PB[6][0:128, 2:3], at[:], csb[:, 0:1],
                                          start=True, stop=False), deps=[t_at, t_csb], inc=SPE)
        t_mm = E("pe", lambda t: t.matmul(PB[6][0:128, 2:3], btS[:], csb[:, 1:2],
                                          start=False, stop=True), deps=[t_bt, t_mm], inc=SPE)
        t_bz = E("dve", lambda v: v.tensor_copy(bzc[:], PB[6][0:128, 2:3]), deps=[t_mm], inc=SDVE)
        t_mm = E("pe", lambda t: t.matmul(PB[6][0:128, 3:4], ct[:], csb[:, 0:1],
                                          start=True, stop=True), deps=[t_ct, t_csb, t_bz], inc=SPE)
        t_ccs = E("dve", lambda v: v.tensor_copy(csvec[:, 2:3], PB[6][0:128, 3:4]),
                  deps=[t_mm], inc=SDVE)
        t_mm = E("pe", lambda t: t.matmul(PB[6][0:128, 4:5], at[:], csb[:, 0:1],
                                          start=True, stop=True), deps=[t_at, t_csb, t_ccs], inc=SPE)
        t_acs = E("dve", lambda v: v.tensor_copy(csvec[:, 5:6], PB[6][0:128, 4:5]),
                  deps=[t_mm], inc=SDVE)

        crr_last = t_acs
        for nm, wt, wtok, dst in (("crr", wr, t_wr, crr), ("crk", wk, t_wk, crk)):
            for gg in range(2):
                t_mm = None
                for kk in range(DC):
                    t_mm = E("pe", lambda t, kk=kk, gg=gg, wt=wt: t.matmul(
                        PB[6][0:1, 0:512], sbb[:, kk:kk + 1],
                        wt[:, kk * D + gg * 512:kk * D + (gg + 1) * 512],
                        start=(kk == 0), stop=(kk == 7)),
                        deps=[wtok, sbb_cols[kk], crr_last], inc=SPE)
                crr_last = E("dve", lambda v, gg=gg: v.tensor_copy(
                    afr(X0A + 2560, 1024)[0:1, gg * 512:(gg + 1) * 512], PB[6][0:1, 0:512]),
                    deps=[t_mm], inc=SDVE)
            cols = row_to_cols(lambda i: afr(X0A + 2560, 1024)[0:1, i * 128:(i + 1) * 128],
                               dst, 0, DC, [crr_last])
            crr_last = cols[-1]
        t_crcols = crr_last

        t_wsf = None
        for m in range(DC):
            t_wsf = E("act", lambda s, m=m: s.activation(
                out=ws[:, m * 128:(m + 1) * 128], in_=ws[:, m * 128:(m + 1) * 128],
                func=AF.Identity, bias=0.0, scale=colsA[:, m:m + 1]),
                deps=[t_cs, capl_cols[m], t_ws], inc=SACT)
        bc_t = []
        for gg in range(2):
            tm = E("pe", lambda t, gg=gg: t.matmul(
                PB[6][0:128, 0:512], onesrow[0:1, 0:128], abr(D0 + gg * 512, 512),
                start=True, stop=True),
                deps=[t_caplb, t_crcols] + (bc_t[:1] if gg else []), inc=SPE)
            bc_t.append(E("dve", lambda v, gg=gg: v.tensor_copy(
                ABF[:, G0 + gg * 512:G0 + (gg + 1) * 512], PB[6][0:128, 0:512]),
                deps=[tm], inc=SDVE))
        t_wia = E("dve", lambda v: v.tensor_mul(wia[:], wiNs[:], ab(G0, 1024)),
                  deps=[bc_t[1], t_win], inc=SDVE)
        t_wibl = None
        for m in range(DC):
            tm = E("pe", lambda t, m=m: t.matmul(
                PB[6][0:128, 0:128], wia[:, m * 128:(m + 1) * 128], btS[:],
                start=True, stop=True),
                deps=[t_wia, t_bt] + ([t_wibl] if t_wibl else []), inc=SPE)
            t_wibl = E("dve", lambda v, m=m: v.tensor_copy(
                wib[:, m * 128:(m + 1) * 128], PB[6][0:128, 0:128]), deps=[tm], inc=SDVE)
        t_wib = t_wibl
        t_wrf = t_wkf = None
        for m in range(DC):
            t_wrf = E("act", lambda s, m=m: s.activation(
                out=wr[:, m * D:(m + 1) * D], in_=wr[:, m * D:(m + 1) * D],
                func=AF.Identity, bias=0.0, scale=colsA[:, 8 + m:9 + m]),
                deps=[t_crcols, cbpl_cols[m], t_wr], inc=SACT)
            t_wkf = E("act", lambda s, m=m: s.activation(
                out=wk[:, m * D:(m + 1) * D], in_=wk[:, m * D:(m + 1) * D],
                func=AF.Identity, bias=0.0, scale=colsA[:, 8 + m:9 + m]),
                deps=[t_crcols, cbpl_cols[m], t_wk], inc=SACT)
        bc_t = []
        for gg in range(2):
            tm = E("pe", lambda t, gg=gg: t.matmul(
                PB[6][0:128, 0:512], onesrow[0:1, 0:128], abr(D0 + 1024 + gg * 512, 512),
                start=True, stop=True),
                deps=[t_gbb, t_wib] + (bc_t[:1] if gg else []), inc=SPE)
            bc_t.append(E("dve", lambda v, gg=gg: v.tensor_copy(
                ABF[:, G0 + gg * 512:G0 + (gg + 1) * 512], PB[6][0:128, 0:512]),
                deps=[tm, t_wia], inc=SDVE))
        t_wvf = None
        for m in range(DC):
            t_wvf = E("dve", lambda v, m=m: v.tensor_mul(
                wv[:, m * D:(m + 1) * D], wv[:, m * D:(m + 1) * D], ab(G0, 1024)),
                deps=[bc_t[1], t_wv], inc=SDVE)

        t_pad = E("dve", lambda v: v.memset(ab(F0, 1), 0.0),
                  deps=[mmB[7], mmGb[7]], inc=SDVE)

        t_hcols = None
        for m in range(DC):
            t_r = E("dve", lambda v, m=m: v.reciprocal(csvec[:, 6:7], colsA[:, 8 + m:9 + m]),
                    deps=[cbpl_cols[m], t_cs, t_cu] + ([t_hcols] if t_hcols else []), inc=SDVE)
            t_m = E("dve", lambda v, m=m: v.tensor_mul(csvec[:, 7:8], sbb[:, m:m + 1],
                    csvec[:, 6:7]), deps=[t_r, sbb_cols[m]], inc=SDVE)
            t_hcols = E("dve", lambda v, m=m: v.tensor_scalar(
                out=hcols[:, m:m + 1], in0=csvec[:, 7:8], scalar1=-1.0, scalar2=None,
                op0=ALU.mult), deps=[t_m], inc=SDVE)

        if upto == "pro":
            dbg_exit([
                (0, lambda: colsA[:], [t_cs, t_cu, t_ccs, t_acs, t_crcols, t_wvf, t_wsf if t_wsf else None, t_wkf], False),
                (128, lambda: csvec[:], [t_acs, t_bz], False),
                (256, lambda: crr[:], [t_crcols], False),
                (384, lambda: crk[:], [t_crcols], False),
                (512, lambda: bzc[:], [t_bz], False),
                (640, lambda: sab[:], [t_crcols], True),
                (768, lambda: sbb[:], [t_crcols], True),
                (896, lambda: wib[:, 0:256], [t_wib], True),
            ])
            return nc

        bfree = {0: mmGb[7], 1: mmGb[7], 2: mmGb[7], 3: tr_prev, 4: t_gb, 5: t_gb,
                 6: bc_t[1], 7: t_cbev[1]}
        xbufb = lambda par: ABF[:, A0 + par * 4096:A0 + (par + 1) * 4096]
        xsqP = lambda: ABF[:, B0:B0 + 4096]
        xcb = lambda: ABF[:, C0:C0 + 4096]
        sbc_v = ABF[:, G0:G0 + 512]
        mbc_v = ABF[:, G0 + 512:G0 + 1024]
        rowsS = lambda i: AF32[0:1, X1A + 2048 + i * CH:X1A + 2048 + (i + 1) * CH]
        rowbf = lambda i: ABF[0:1, D0 + 2816 + i * CH:D0 + 2816 + (i + 1) * CH]
        spb_c = lambda c0, w: ABF[:, F0 + c0:F0 + c0 + w]
        wsb_c = lambda c0, w: ABF[:, H0 + c0:H0 + c0 + w]
        thb_c = lambda c0, w: ABF[:, E0 + c0:E0 + c0 + w]

        t_xdma = [None] * NCH
        t_wev = [None] * NCH
        t_spev = [None] * NCH
        xb_last = {0: t_silub, 1: t_silub}
        rows_free = t_caplb
        xsq_free = t_gbb
        xc_free = t_gbb
        for c in range(NCH):
            par = c % 2
            t0 = c * CH
            t_xdma[c] = E("gp", lambda g, t0=t0, par=par: g.dma_start(
                out=xbufb(par).rearrange("p (r t) -> p r t", r=8), in_=dram["xT"][:, t0:t0 + CH].rearrange("(r p) t -> p r t", p=128)),
                deps=[xb_last[par]], stream=f"x_{par}")
            t_sq = [None] * DC
            for m in range(DC):
                t_sq[m] = E("gp", lambda g, m=m, par=par: g.tensor_mul(
                    xsqP()[:, m * CH:(m + 1) * CH], xbufb(par)[:, m * CH:(m + 1) * CH],
                    xbufb(par)[:, m * CH:(m + 1) * CH]),
                    deps=[t_xdma[c], xsq_free], inc=SGPC)
            t_sum = t_ssq = None
            for m in range(DC):
                t_sum = E("pe", lambda t, m=m, par=par: t.matmul(
                    PB[0][0:1, 0:CH], onescol[:], xbufb(par)[:, m * CH:(m + 1) * CH],
                    start=(m == 0), stop=(m == 7)), deps=[t_xdma[c], bfree[0], rows_free], inc=SPE)
            for m in range(DC):
                t_ssq = E("pe", lambda t, m=m: t.matmul(
                    PB[1][0:1, 0:CH], onescol[:], xsqP()[:, m * CH:(m + 1) * CH],
                    start=(m == 0), stop=(m == 7)), deps=[t_sq[m], bfree[1], rows_free], inc=SPE)
            t_mrbf = E("dve", lambda v: v.tensor_scalar(
                out=rowbf(0), in0=PB[0][0:1, 0:CH], scalar1=1.0 / D, scalar2=None,
                op0=ALU.mult), deps=[t_sum], inc=SDVE)
            t_mf = E("dve", lambda v: v.tensor_scalar(
                out=rowsS(0), in0=PB[0][0:1, 0:CH], scalar1=1.0 / D, scalar2=None,
                op0=ALU.mult), deps=[t_sum], inc=SDVE)
            t_msq = E("dve", lambda v: v.tensor_mul(rowsS(1), rowsS(0), rowsS(0)),
                      deps=[t_mf], inc=SDVE)
            t_var = E("dve", lambda v: v.scalar_tensor_tensor(
                out=rowsS(2), in0=PB[1][0:1, 0:CH], scalar=1.0 / D, in1=rowsS(1),
                op0=ALU.mult, op1=ALU.subtract), deps=[t_ssq, t_msq], inc=SDVE)
            bfree[0] = t_mf
            bfree[1] = t_var
            t_ln = E("act", lambda s: s.activation(
                out=rowsS(3), in_=rowsS(2), func=AF.Ln, bias=epsc[0:1, 0:1], scale=1.0),
                deps=[t_var, t_wsf], inc=SACT)
            t_rstd = E("act", lambda s: s.activation(
                out=rowsS(2), in_=rowsS(3), func=AF.Exp, bias=0.0, scale=-0.5),
                deps=[t_ln], inc=SACT)
            t_rstdb = E("dve", lambda v: v.tensor_copy(rowbf(1), rowsS(2)),
                        deps=[t_rstd], inc=SDVE)
            rows_free = t_rstdb
            t_sbcm = E("pe", lambda t: t.matmul(
                PB[2][0:128, 0:CH], onesrow[0:1, 0:128], rowbf(1), start=True, stop=True),
                deps=[t_rstdb, bfree[2]], inc=SPE)
            t_mbcm = E("pe", lambda t: t.matmul(
                PB[3][0:128, 0:CH], onesrow[0:1, 0:128], rowbf(0), start=True, stop=True),
                deps=[t_mrbf, bfree[3]], inc=SPE)
            t_sbce = E("dve", lambda v: v.tensor_copy(sbc_v, PB[2][0:128, 0:CH]),
                       deps=[t_sbcm], inc=SDVE)
            t_mbce = E("dve", lambda v: v.tensor_copy(mbc_v, PB[3][0:128, 0:CH]),
                       deps=[t_mbcm], inc=SDVE)
            bfree[2] = t_sbce
            bfree[3] = t_mbce
            t_xc = [None] * DC
            for m in range(DC):
                t_xc[m] = E("dve", lambda v, m=m, par=par: v.tensor_sub(
                    xcb()[:, m * CH:(m + 1) * CH], xbufb(par)[:, m * CH:(m + 1) * CH], mbc_v),
                    deps=[t_mbce, t_xdma[c], xc_free], inc=SDVE)
            xb_last[par] = t_xc[DC - 1]
            t_mm = None
            for m in range(DC):
                t_mm = E("pe", lambda t, m=m: t.matmul(
                    PB[5][0:128, 0:CH], ws[:, m * 128:(m + 1) * 128], xcb()[:, m * CH:(m + 1) * CH],
                    start=(m == 0), stop=(m == 7)), deps=[t_xc[m], t_wsf, bfree[5]], inc=SPE)
            t_spev[c] = E("dve", lambda v, t0=t0: v.tensor_mul(
                spb_c(1 + t0, CH), PB[5][0:128, 0:CH], sbc_v), deps=[t_mm, t_sbce], inc=SDVE)
            bfree[5] = t_spev[c]
            t_spa = E("pe", lambda t, t0=t0: t.matmul(
                PB[4][0:128, 0:CH], at[:], spb_c(t0, CH), start=True, stop=True),
                deps=[t_spev[c], t_at, bfree[4]] + ([t_pad] if c == 0 else []), inc=SPE)
            t_mm = None
            for m in range(DC):
                t_mm = E("pe", lambda t, m=m: t.matmul(
                    PB[6][0:128, 0:CH], wib[:, m * 128:(m + 1) * 128], xcb()[:, m * CH:(m + 1) * CH],
                    start=(m == 0), stop=(m == 7)), deps=[t_xc[m], t_wib, bfree[6]], inc=SPE)
            t_w1 = E("dve", lambda v: v.tensor_mul(af(X1A, CH)[:, 0:CH], PB[6][0:128, 0:CH], sbc_v),
                     deps=[t_mm, t_sbce], inc=SDVE)
            t_wev[c] = E("dve", lambda v, t0=t0: v.tensor_add(
                wsb_c(W + t0, CH), af(X1A, CH)[:, 0:CH], PB[4][0:128, 0:CH]),
                deps=[t_w1, t_spa, t_init], inc=SDVE)
            if c == 0:
                t_wev[0] = E("dve", lambda v: v.tensor_sub(
                    wsb_c(W, 1), wsb_c(W, 1), csvec[:, 5:6]),
                    deps=[t_wev[0], t_acs], inc=SDVE)
            bfree[4] = t_wev[c]
            bfree[6] = t_wev[c]
            xsq_free = t_ssq
            xc_free = t_mm

        if upto == "pre":
            dbg_exit([
                (0, lambda: ABF[:, H0 + W:H0 + W + T], [t_wev[NCH - 1]], True),
                (128, lambda: ABF[:, F0 + 1:F0 + 1 + T], [t_spev[NCH - 1]], True),
            ])
            return nc

        all_w = t_wev[NCH - 1]
        def slot(i):
            return PB[i % 8][0:128, 0:R]
        ring_free0 = [bfree[k] for k in range(8)]
        t_fill = [None] * ROUNDS
        t_cmm = [None] * ROUNDS
        t_tanh = [None] * ROUNDS

        def emit_fill(i):
            deps = [all_w, t_identb]
            if i >= 8:
                deps.append(t_tanh[i - 8])
            else:
                deps.append(ring_free0[i % 8])
            if R > 1:
                srcf = lambda i=i: ABF[:, H0 + i:H0 + i + (R - 1) * S + 1:S]
            else:
                srcf = lambda i=i: ABF[:, H0 + W + i:H0 + W + i + 1]
            t_fill[i] = E("pe", lambda t, srcf=srcf, i=i: t.matmul(
                slot(i), identb[:], srcf(), start=True, stop=False), deps=deps, inc=SPE)

        for i in range(min(8, ROUNDS)):
            emit_fill(i)
        for i in range(ROUNDS):
            deps = [t_fill[i], t_bz, t_wvf]
            if i > 0:
                deps.append(t_tanh[i - 1])
            if W > 0 and i == W:
                t_fix = E("dve", lambda v, i=i: v.memset(thb_c(i * R, 1), 0.0),
                          deps=[t_tanh[i - 1]], inc=SDVE)
                deps.append(t_fix)
            t_cmm[i] = E("pe", lambda t, i=i: t.matmul(
                slot(i), at[:], thb_c(i * R, R), start=False, stop=True), deps=deps, inc=SPE)
            t_tanh[i] = E("act", lambda s, i=i: s.activation(
                out=thb_c((i + 1) * R, R), in_=slot(i), func=AF.Tanh, bias=bzc[:], scale=1.0),
                deps=[t_cmm[i]], inc=SACT)
            if i + 8 < ROUNDS:
                emit_fill(i + 8)
        t_scan = t_tanh[ROUNDS - 1]

        if upto == "scan":
            dbg_exit([
                (0, lambda: ABF[:, E0 + (W + 1) * R:E0 + (W + 1) * R + min((ROUNDS - W) * R, T)], [t_scan], True),
            ])
            return nc

        x0b = lambda par: AF32[:, X0A + par * 2048:X0A + (par + 1) * 2048]
        x1bv = lambda par: AF32[:, X1A + par * 2048:X1A + (par + 1) * 2048]
        kscrv = lambda p: AF32[:, WSB + p * CB:WSB + (p + 1) * CB]
        vscrv = lambda p: AF32[:, WSB + 512 + p * CB:WSB + 512 + (p + 1) * CB]
        m2row = lambda par: AF32[0:1, WSB + 1024 + par * CB:WSB + 1024 + (par + 1) * CB]
        rowsA = lambda par, i: AF32[0:1, RW1 + par * 512 + i * CB:RW1 + par * 512 + (i + 1) * CB]
        xbfc = lambda par: ABF[:, A0 + par * 2048:A0 + (par + 1) * 2048]
        xsqv = lambda par: ABF[:, A0 + 4096 + par * 2048:A0 + 4096 + (par + 1) * 2048]
        rbufv = ABF[:, B0:B0 + 2048]
        kabv = ABF[:, B0 + 2048:B0 + 4096]
        xrbv = lambda par: ABF[:, C0 + par * 2048:C0 + (par + 1) * 2048]
        xn2v = lambda par, m: ABF[:, D0 + par * 2056 + m * 257:D0 + par * 2056 + m * 257 + 257]
        xkbv = lambda par: ABF[:, H0 + par * 2048:H0 + (par + 1) * 2048]
        s2bcv = ABF[:, G0:G0 + CB]
        m2bcv = ABF[:, G0 + CB:G0 + 2 * CB]
        rstd2r = lambda par: ABF[0:1, G0 + 512 + par * CB:G0 + 512 + (par + 1) * CB]
        ycbv = lambda par: ABF[:, G0 + 1024 + par * CB:G0 + 1024 + (par + 1) * CB]
        dscrA = ABF[:, G0 + 1536:G0 + 1536 + CB]
        dscrB = ABF[:, G0 + 1792:G0 + 1792 + CB]
        epiv = lambda p: AF32[:, WSB + 1536 + p * CB:WSB + 1536 + (p + 1) * CB]

        bk = {i: t_scan for i in range(8)}
        bkh = {}
        t_x0 = [None] * NCB
        t_yc = [None] * NCB
        out2_last = [None] * NCB
        t_x1m = [[None] * DC for _ in range(NCB)]
        t_cast = [[None] * DC for _ in range(NCB)]
        t_sqx = [[None] * DC for _ in range(NCB)]
        t_sumT = [None] * NCB
        t_ssqT = [None] * NCB
        t_m2r = [None] * NCB
        t_varT = [None] * NCB
        t_rstd = [None] * NCB
        t_s2m = [None] * NCB
        t_m2m = [None] * NCB
        t_s2e = [None] * NCB
        t_m2e = [None] * NCB
        t_sub = [[None] * DC for _ in range(NCB)]
        t_xn = [[None] * DC for _ in range(NCB)]
        t_halo = [[None] * DC for _ in range(NCB)]
        t_dm = [[None] * DC for _ in range(NCB)]
        t_xr = [[None] * DC for _ in range(NCB)]
        t_xk = [[None] * DC for _ in range(NCB)]
        t_sig = [[None] * DC for _ in range(NCB)]
        t_rel = [[None] * DC for _ in range(NCB)]
        t_ksq = [[None] * DC for _ in range(NCB)]
        t_vm = [[None] * DC for _ in range(NCB)]
        t_x2 = [[None] * DC for _ in range(NCB)]
        t_store = [None] * NCB
        r_done = [None] * NCB
        k_done = [None] * NCB
        v_done = [None] * NCB
        pre_gp = t_sq[DC - 1]

        for c in range(NCB + 1):
            par = c % 2
            t0 = c * CB
            if c < NCB:
                dep0 = [gb_cols[-1], t_crcols] if c < 2 else [t_x1m[c - 2][DC - 1]]
                t_x0[c] = E("sp", lambda s, t0=t0, par=par: s.dma_start(
                    out=x0b(par).rearrange("p (r t) -> p r t", r=8),
                    in_=dram["xT"][:, t0:t0 + CB].rearrange("(r p) t -> p r t", p=128)),
                    deps=dep0, stream=f"x0_{par}")
                t_mm = None
                for rr in range(CB // S):
                    base = (W + 1) * R + (t0 // S + rr)
                    t_mm = E("pe", lambda t, base=base, rr=rr: t.matmul(
                        PB[2][0:128, rr * S:rr * S + S],
                        ct[:], ABF[:, E0 + base:E0 + base + (S - 1) * R + 1:R],
                        start=(rr == 0), stop=False),
                        deps=([t_scan, t_ct, bk[2]] if rr == 0 else [t_mm]), inc=SPE)
                t_mm = E("pe", lambda t, t0=t0: t.matmul(
                    PB[2][0:128, 0:CB], ct[:], spb_c(1 + t0, CB), start=False, stop=True),
                    deps=[t_mm], inc=SPE)
                ycw = [out2_last[c - 2]] if c >= 2 else []
                t_yc[c] = E("act", lambda s, par=par: s.activation(
                    out=ycbv(par), in_=PB[2][0:128, 0:CB], func=AF.Identity,
                    bias=csvec[:, 2:3], scale=1.0), deps=[t_mm, t_ccs] + ycw, inc=SACT)
                bk[2] = t_yc[c]
                for m in range(DC):
                    bb = 3 - (m % 2)
                    ep = m % 2
                    dd = [t_yc[c], t_wo, t_bobf, bkh.get(bb, bk[bb])]
                    t_mm = E("pe", lambda t, m=m, bb=bb, par=par: t.matmul(
                        PB[bb][0:128, 0:CB], wo[:, m * 128:(m + 1) * 128], ycbv(par),
                        start=True, stop=False), deps=dd, inc=SPE)
                    t_mm = E("pe", lambda t, m=m, bb=bb: t.matmul(
                        PB[bb][0:128, 0:CB], borowS[0:1, m * 128:(m + 1) * 128],
                        onesrow[0:1, 0:CB], start=False, stop=True), deps=[t_mm], inc=SPE)
                    if m == DC - 1:
                        out2_last[c] = t_mm
                    edeps = [t_mm, ga_cols[m]]
                    if m >= 2:
                        edeps.append(t_x1m[c][m - 2])
                    elif c >= 1:
                        edeps.append(t_x1m[c - 1][DC - 2 + ep])
                    t_sc = E("act", lambda s, m=m, bb=bb, ep=ep: s.activation(
                        out=epiv(ep), in_=PB[bb][0:128, 0:CB], func=AF.Identity,
                        bias=0.0, scale=colsA[:, 16 + m:17 + m]), deps=edeps, inc=SACT)
                    bkh[bb] = t_sc
                    wdeps = [t_sc, t_x0[c]]
                    if c >= 2:
                        wdeps.append(t_store[c - 2])
                    elif par == 0:
                        wdeps.append(t_w1)
                    else:
                        wdeps.append(rows_free)
                    t_x1m[c][m] = E("dve", lambda v, m=m, ep=ep, par=par: v.tensor_add(
                        x1bv(par)[:, m * CB:(m + 1) * CB], epiv(ep),
                        x0b(par)[:, m * CB:(m + 1) * CB]), deps=wdeps, inc=SDVE)
                    cdeps = [t_x1m[c][m], pre_gp]
                    if c >= 2:
                        cdeps += [t_sumT[c - 2], t_sub[c - 2][m]]
                    t_cast[c][m] = E("act", lambda s, m=m, par=par: s.activation(
                        out=xbfc(par)[:, m * CB:(m + 1) * CB],
                        in_=x1bv(par)[:, m * CB:(m + 1) * CB], func=AF.Identity,
                        bias=0.0, scale=1.0), deps=cdeps, inc=SACT)
                    sdeps = [t_cast[c][m]]
                    if c >= 2:
                        sdeps.append(t_ssqT[c - 2])
                    t_sqx[c][m] = E("gp", lambda g, m=m, par=par: g.tensor_mul(
                        xsqv(par)[:, m * CB:(m + 1) * CB],
                        xbfc(par)[:, m * CB:(m + 1) * CB],
                        xbfc(par)[:, m * CB:(m + 1) * CB]), deps=sdeps, inc=SGPC)
                bk[3] = bkh[3]
                bk[2] = bkh[2]
            if c >= 1:
                pr = (c - 1) % 2
                for m in range(DC):
                    bb = 4 + (m % 2)
                    t_mm = None
                    for kk in range(DC):
                        dd = [t_mm] if t_mm else [bk[bb], t_wrf, t_xr[c - 1][DC - 1]]
                        t_mm = E("pe", lambda t, m=m, kk=kk, bb=bb, pr=pr: t.matmul(
                            PB[bb][0:128, 0:CB], wr[:, kk * D + m * 128:kk * D + (m + 1) * 128],
                            xrbv(pr)[:, kk * CB:(kk + 1) * CB], start=(kk == 0), stop=(kk == 7)),
                            deps=dd, inc=SPE)
                    if m == DC - 1:
                        r_done[c - 1] = t_mm
                    rw = [t_vm[c - 2][m]] if c >= 2 else []
                    t_sig[c - 1][m] = E("act", lambda s, m=m, bb=bb: s.activation(
                        out=rbufv[:, m * CB:(m + 1) * CB], in_=PB[bb][0:128, 0:CB],
                        func=AF.Sigmoid, bias=crr[:, m:m + 1], scale=1.0),
                        deps=[t_mm, t_crcols] + rw, inc=SACT)
                    bk[bb] = t_sig[c - 1][m]
            if c < NCB:
                t_mm = None
                for m in range(DC):
                    dd = [t_cast[c][m]] + ([t_mm] if t_mm else [bk[0]])
                    t_mm = E("pe", lambda t, m=m, par=par: t.matmul(
                        PB[0][0:1, 0:CB], onescol[:], xbfc(par)[:, m * CB:(m + 1) * CB],
                        start=(m == 0), stop=(m == 7)), deps=dd, inc=SPE)
                t_sumT[c] = t_mm
                t_mm = None
                for m in range(DC):
                    dd = [t_sqx[c][m]] + ([t_mm] if t_mm else [bk[1]])
                    t_mm = E("pe", lambda t, m=m, par=par: t.matmul(
                        PB[1][0:1, 0:CB], onescol[:], xsqv(par)[:, m * CB:(m + 1) * CB],
                        start=(m == 0), stop=(m == 7)), deps=dd, inc=SPE)
                t_ssqT[c] = t_mm
                mdeps = [t_sumT[c]]
                if c >= 2:
                    mdeps.append(t_m2m[c - 2])
                t_m2r[c] = E("dve", lambda v, par=par: v.tensor_scalar(
                    out=m2row(par), in0=PB[0][0:1, 0:CB], scalar1=1.0 / D, scalar2=None,
                    op0=ALU.mult), deps=mdeps, inc=SDVE)
                qdeps = [t_m2r[c]]
                qdeps.append(t_rstd[c - 2] if c >= 2 else cbpl_cols[-1])
                t_msq = E("dve", lambda v, par=par: v.tensor_mul(
                    rowsA(par, 0), m2row(par), m2row(par)), deps=qdeps, inc=SDVE)
                t_varT[c] = E("dve", lambda v, par=par: v.scalar_tensor_tensor(
                    out=rowsA(par, 1), in0=PB[1][0:1, 0:CB], scalar=1.0 / D,
                    in1=rowsA(par, 0), op0=ALU.mult, op1=ALU.subtract),
                    deps=[t_ssqT[c], t_msq], inc=SDVE)
                t_ln = E("act", lambda s, par=par: s.activation(
                    out=rowsA(par, 0), in_=rowsA(par, 1), func=AF.Ln,
                    bias=epsc[0:1, 0:1], scale=1.0), deps=[t_varT[c], t_msq], inc=SACT)
                rw = [t_s2m[c - 2]] if c >= 2 else []
                t_rstd[c] = E("act", lambda s, par=par: s.activation(
                    out=rstd2r(par), in_=rowsA(par, 0), func=AF.Exp, bias=0.0,
                    scale=-0.5), deps=[t_ln] + rw, inc=SACT)
                t_s2m[c] = E("pe", lambda t, par=par: t.matmul(
                    PB[6][0:128, 0:CB], onesrow[0:1, 0:128], rstd2r(par),
                    start=True, stop=True), deps=[t_rstd[c], bk[6]], inc=SPE)
                s2r = [t_xn[c - 1][DC - 1]] if c >= 1 else []
                t_s2e[c] = E("dve", lambda v: v.tensor_copy(s2bcv, PB[6][0:128, 0:CB]),
                             deps=[t_s2m[c]] + s2r, inc=SDVE)
                t_m2m[c] = E("pe", lambda t, par=par: t.matmul(
                    PB[7][0:128, 0:CB], onesrowf[0:1, 0:128], m2row(par),
                    start=True, stop=True), deps=[t_m2r[c], bk[7]], inc=SPE)
                m2rd = [t_sub[c - 1][DC - 1]] if c >= 1 else []
                t_m2e[c] = E("dve", lambda v: v.tensor_copy(m2bcv, PB[7][0:128, 0:CB]),
                             deps=[t_m2m[c]] + m2rd, inc=SDVE)
                bk[6] = t_s2e[c]
                bk[7] = t_m2e[c]
            if c >= 1:
                pr = (c - 1) % 2
                for m in range(DC):
                    bb = 6 + (m % 2)
                    kp = m % 2
                    t_mm = None
                    for kk in range(DC):
                        dd = [t_mm] if t_mm else [bk[bb], t_wkf, t_xk[c - 1][DC - 1]]
                        t_mm = E("pe", lambda t, m=m, kk=kk, bb=bb, pr=pr: t.matmul(
                            PB[bb][0:128, 0:CB], wk[:, kk * D + m * 128:kk * D + (m + 1) * 128],
                            xkbv(pr)[:, kk * CB:(kk + 1) * CB], start=(kk == 0), stop=(kk == 7)),
                            deps=dd, inc=SPE)
                    if m == DC - 1:
                        k_done[c - 1] = t_mm
                    if m >= 2:
                        kw = [t_ksq[c - 1][m - 2]]
                    elif c >= 2:
                        kw = [t_ksq[c - 2][DC - 2 + kp]]
                    else:
                        kw = []
                    t_rel[c - 1][m] = E("act", lambda s, m=m, bb=bb, kp=kp: s.activation(
                        out=kscrv(kp), in_=PB[bb][0:128, 0:CB], func=AF.Relu,
                        bias=crk[:, m:m + 1], scale=1.0), deps=[t_mm, t_crcols] + kw, inc=SACT)
                    bk[bb] = t_rel[c - 1][m]
                    kabw = [v_done[c - 2]] if c >= 2 else []
                    t_ksq[c - 1][m] = E("gp", lambda g, m=m, kp=kp: g.tensor_mul(
                        kabv[:, m * CB:(m + 1) * CB], kscrv(kp), kscrv(kp)),
                        deps=[t_rel[c - 1][m]] + kabw, inc=SGPC)
            if c < NCB:
                for m in range(DC):
                    sdeps = [t_m2e[c], t_cast[c][m]]
                    if m > 0:
                        sdeps.append(t_xn[c][m - 1])
                    elif c >= 1:
                        sdeps.append(t_xn[c - 1][DC - 1])
                    t_sub[c][m] = E("gp", lambda g, m=m, par=par: g.tensor_sub(
                        dscrA, xbfc(par)[:, m * CB:(m + 1) * CB], m2bcv),
                        deps=sdeps, inc=SGPC)
                    ndeps = [t_sub[c][m], t_s2e[c]]
                    if c == 0:
                        ndeps.append(t_wvf)
                    if c >= 1:
                        ndeps.append(t_halo[c - 1][m])
                    if c >= 2:
                        ndeps.append(t_dm[c - 2][m])
                    t_xn[c][m] = E("dve", lambda v, m=m, par=par: v.tensor_mul(
                        xn2v(par, m)[:, 1:1 + CB], dscrA, s2bcv), deps=ndeps, inc=SDVE)
                for m in range(DC):
                    if c == 0:
                        t_halo[c][m] = E("dve", lambda v, m=m, par=par: v.tensor_copy(
                            xn2v(par, m)[:, 0:1], hcols[:, m:m + 1]),
                            deps=[t_hcols, t_wvf, pre_gp], inc=SDVE)
                    else:
                        hdeps = [t_xn[c - 1][m]]
                        if c >= 2:
                            hdeps.append(t_dm[c - 2][m])
                        t_halo[c][m] = E("dve", lambda v, m=m, par=par: v.tensor_copy(
                            xn2v(par, m)[:, 0:1], xn2v(1 - par, m)[:, CB:CB + 1]),
                            deps=hdeps, inc=SDVE)
                    ddeps = [t_xn[c][m], t_halo[c][m]]
                    if m > 0:
                        ddeps.append(t_xk[c][m - 1])
                    t_dm[c][m] = E("gp", lambda g, m=m, par=par: g.tensor_sub(
                        dscrB, xn2v(par, m)[:, 1:1 + CB], xn2v(par, m)[:, 0:CB]),
                        deps=ddeps, inc=SGPC)
                    xrw = [r_done[c - 2]] if c >= 2 else []
                    t_xr[c][m] = E("dve", lambda v, m=m, par=par: v.scalar_tensor_tensor(
                        out=xrbv(par)[:, m * CB:(m + 1) * CB], in0=dscrB,
                        scalar=murS[:, m:m + 1], in1=xn2v(par, m)[:, 0:CB],
                        op0=ALU.mult, op1=ALU.add), deps=[t_dm[c][m], t_mur] + xrw, inc=SDVE)
                    xkw = [k_done[c - 2]] if c >= 2 else []
                    t_xk[c][m] = E("dve", lambda v, m=m, par=par: v.scalar_tensor_tensor(
                        out=xkbv(par)[:, m * CB:(m + 1) * CB], in0=dscrB,
                        scalar=mukS[:, m:m + 1], in1=xn2v(par, m)[:, 0:CB],
                        op0=ALU.mult, op1=ALU.add), deps=[t_dm[c][m], t_muk] + xkw, inc=SDVE)
            if c >= 1:
                pr = (c - 1) % 2
                vfree = {0: (t_m2r[c] if c < NCB else t_vm[c - 2][DC - 2]),
                         1: (t_varT[c] if c < NCB else t_vm[c - 2][DC - 1])}
                for m in range(DC):
                    bb = m % 2
                    vp = m % 2
                    t_mm = None
                    for kk in range(DC):
                        dd = [t_mm] if t_mm else [t_ksq[c - 1][DC - 1], t_wvf, vfree[bb]]
                        t_mm = E("pe", lambda t, m=m, kk=kk, bb=bb: t.matmul(
                            PB[bb][0:128, 0:CB], wv[:, kk * D + m * 128:kk * D + (m + 1) * 128],
                            kabv[:, kk * CB:(kk + 1) * CB], start=(kk == 0), stop=(kk == 7)),
                            deps=dd, inc=SPE)
                    if m == DC - 1:
                        v_done[c - 1] = t_mm
                    if m >= 2:
                        vwm = [t_x2[c - 1][m - 2]]
                    elif c >= 2:
                        vwm = [t_x2[c - 2][DC - 2 + vp]]
                    else:
                        vwm = []
                    t_vm[c - 1][m] = E("dve", lambda v, m=m, bb=bb, vp=vp: v.tensor_mul(
                        vscrv(vp), rbufv[:, m * CB:(m + 1) * CB], PB[bb][0:128, 0:CB]),
                        deps=[t_mm, t_sig[c - 1][m]] + vwm, inc=SDVE)
                    vfree[bb] = t_vm[c - 1][m]
                    t_x2[c - 1][m] = E("dve", lambda v, m=m, vp=vp, pr=pr: v.tensor_add(
                        x1bv(pr)[:, m * CB:(m + 1) * CB], x1bv(pr)[:, m * CB:(m + 1) * CB],
                        vscrv(vp)),
                        deps=[t_vm[c - 1][m], t_x1m[c - 1][m], t_cast[c - 1][m],
                              t_sqx[c - 1][m]], inc=SDVE)
                bk[0] = t_vm[c - 1][DC - 2]
                bk[1] = t_vm[c - 1][DC - 1]
                t_store[c - 1] = E("sp", lambda s, c=c, pr=pr: s.dma_start(
                    out=yout[:, (c - 1) * CB:c * CB].rearrange("(r p) t -> p r t", p=128),
                    in_=x1bv(pr).rearrange("p (r t) -> p r t", r=8)),
                    deps=[t_x2[c - 1][DC - 1]], stream=f"yo_{pr}")

        E("gp", lambda g: g.engine_nop(), deps=[t_store[NCB - 2], t_store[NCB - 1]])

        sch.lint()
        import os as _os
        if _os.environ.get("KRACE"):
            for issue in sch.racecheck()[:300]:
                print("RACE:", issue)

        @block.tensor
        def _(t):
            sch.run("pe", t, skip_sem=SPE)

        @block.scalar
        def _(s):
            sch.run("act", s)

        @block.vector
        def _(v):
            sch.run("dve", v)

        @block.gpsimd
        def _(g):
            sch.run("gp", g)

        @block.sync
        def _(s):
            sch.run("sp", s)

    return nc


_NC_CACHE = None


def _get_nc():
    global _NC_CACHE
    if _NC_CACHE is None:
        _NC_CACHE = build_nc()
    return _NC_CACHE


def kernel(**inputs):
    x = np.asarray(inputs["L0_attn_x"], dtype=np.float32)
    Wi = np.asarray(inputs["L0_attn_Wi"], dtype=np.float32)
    bi = np.asarray(inputs["L0_attn_bi"], dtype=np.float32)
    Ws = np.asarray(inputs["L0_attn_Ws"], dtype=np.float32)
    bs = np.asarray(inputs["L0_attn_bs"], dtype=np.float32)
    Wo = np.asarray(inputs["L0_attn_Wo"], dtype=np.float32)
    bo = np.asarray(inputs["L0_attn_bo"], dtype=np.float32)
    A = np.asarray(inputs["L0_attn_A"], dtype=np.float32)
    B = np.asarray(inputs["L0_attn_B"], dtype=np.float32)
    C = np.asarray(inputs["L0_attn_C"], dtype=np.float32)
    W_r = np.asarray(inputs["L0_ffn_W_r"], dtype=np.float32)
    W_k = np.asarray(inputs["L0_ffn_W_k"], dtype=np.float32)
    W_v = np.asarray(inputs["L0_ffn_W_v"], dtype=np.float32)
    mu_r = np.asarray(inputs["L0_ffn_mu_r"], dtype=np.float32)
    mu_k = np.asarray(inputs["L0_ffn_mu_k"], dtype=np.float32)
    cmat = np.asarray(inputs["L0_c"], dtype=np.float32)
    wmod = np.asarray(inputs["L0_wmod"], dtype=np.float32)

    shared = {
        "wrT": np.ascontiguousarray(W_r.T), "wkT": np.ascontiguousarray(W_k.T),
        "wvT": np.ascontiguousarray(W_v.T), "woT": np.ascontiguousarray(Wo.T),
        "wsT": np.ascontiguousarray(Ws.T), "wiT": np.ascontiguousarray(Wi.T),
        "wiN": np.ascontiguousarray(Wi),
        "aT": np.ascontiguousarray(A.T), "bT": np.ascontiguousarray(B.T),
        "cT": np.ascontiguousarray(C.T),
        "wmod": np.ascontiguousarray(wmod),
        "bi": np.ascontiguousarray(bi.reshape(NST, 1)),
        "bs": np.ascontiguousarray(bs.reshape(NST, 1)),
        "borow": np.ascontiguousarray(bo.reshape(1, D)),
        "murc": np.ascontiguousarray(mu_r.reshape(DC, 128).T),
        "mukc": np.ascontiguousarray(mu_k.reshape(DC, 128).T),
    }
    in_maps = []
    for b in range(8):
        m = dict(shared)
        m["xT"] = np.ascontiguousarray(x[b].T)
        m["ccol"] = np.ascontiguousarray(cmat[b].reshape(DC, 128).T)
        in_maps.append(m)

    nc = _get_nc()
    res = run_bass_kernel_spmd(nc, in_maps, core_ids=list(range(8)))
    out = np.stack([np.ascontiguousarray(r["yout"].T) for r in res.results], axis=0)
    return out.astype(np.float32)


# version 19
# speedup vs baseline: 1.6550x; 1.0811x over previous
import numpy as np
import ml_dtypes
import concourse.bass as bass
import concourse.mybir as mybir
from concourse.bass_utils import run_bass_kernel_spmd

F32 = mybir.dt.float32
BF16 = mybir.dt.bfloat16
AF = mybir.ActivationFunctionType
ALU = mybir.AluOpType

T, D, DC, NST = 4096, 1024, 8, 128
CH = 512
NCH = T // CH
CB = 256
NCB = T // CB
R = 64
W = 96 if R > 1 else 0
S = T // R
ROUNDS = S + W
SPB = max(1, 512 // R)
RINGB = 4
EPS = 1e-5


class _Tok:
    __slots__ = ("sem", "val")
    def __init__(self, sem, val):
        self.sem, self.val = sem, val


class _Sem:
    def __init__(self, name):
        self.name, self.total, self.handle = name, 0, None


class _Sched:
    def __init__(self):
        self.ops, self.sems = [], []
    def sem(self, name):
        s = _Sem(name); self.sems.append(s); return s
    def emit(self, eng, fn, deps=(), inc=None, inc_n=1, stream=None):
        deps = list(deps)
        if stream is not None:
            if not hasattr(self, "streams"):
                self.streams = {}
            sem, prev = self.streams.get(stream, (None, None))
            if sem is None:
                sem = self.sem("st_" + stream)
                sem.handle = next(self.handle_pool)
            if prev is not None:
                deps.append(prev)
            inc, inc_n = sem, 16
        waits = [(t.sem, t.val) for t in deps if t is not None and t.val > 0]
        tok = None
        if inc is not None:
            inc.total += inc_n
            tok = _Tok(inc, inc.total)
        if stream is not None:
            self.streams[stream] = (inc, tok)
        self.ops.append((eng, fn, waits, (inc, inc_n) if inc else None))
        return tok
    def lint(self):
        for _, _, waits, _ in self.ops:
            for sem, val in waits:
                assert val <= sem.total, f"wait {sem.name}>={val} > total {sem.total}"
        engs = {}
        for i, (e, fn, waits, inc) in enumerate(self.ops):
            engs.setdefault(e, []).append((waits, inc, i))
        ptr = {e: 0 for e in engs}
        val = {id(s): 0 for s in self.sems}
        progress = True
        while progress:
            progress = False
            for e, lst in engs.items():
                while ptr[e] < len(lst):
                    waits, inc, i = lst[ptr[e]]
                    if all(val[id(s)] >= v for s, v in waits):
                        if inc is not None:
                            val[id(inc[0])] += inc[1]
                        ptr[e] += 1
                        progress = True
                    else:
                        break
        stuck = {e: (ptr[e], len(lst)) for e, lst in engs.items() if ptr[e] < len(lst)}
        if stuck:
            detail = []
            for e, lst in engs.items():
                if ptr[e] < len(lst):
                    waits, inc, i = lst[ptr[e]]
                    why = [(s.name, v, val[id(s)]) for s, v in waits if val[id(s)] < v]
                    detail.append(f"{e}@op{i}: needs {why}")
            raise AssertionError("DEADLOCK: " + "; ".join(detail))
    def racecheck(self):
        class _Inst:
            def then_inc(self, *a, **k):
                return self
        class _Rec:
            def __init__(self):
                self.acc = []
            def _ap(self, ap, kind):
                try:
                    t = ap.tensor
                    name = t.name
                    pairs = [tuple(x) for x in ap.ap]
                    pstride = int(pairs[0][0]) if pairs else 1
                    lo = int(ap.offset) % max(pstride, 1) if pstride > 1 else int(ap.offset)
                    hi = lo
                    for pair in pairs[1:]:
                        st, ct = int(pair[0]), int(pair[1])
                        hi += abs(st) * (ct - 1)
                    self.acc.append((kind, name, lo, hi + 1))
                except Exception:
                    pass
            def __getattr__(self, nm):
                def h(*args, **kw):
                    aps = []
                    outs = []
                    for i, a in enumerate(args):
                        if hasattr(a, "tensor"):
                            (outs if i == 0 else aps).append(a)
                    for k, a in kw.items():
                        if hasattr(a, "tensor"):
                            (outs if k.startswith("out") else aps).append(a)
                    if nm in ("dma_start",):
                        pass
                    for a in outs:
                        self._ap(a, "w")
                    for a in aps:
                        self._ap(a, "r")
                    if nm in ("memset", "memzero") and outs == [] and aps:
                        self.acc[-1] = ("w",) + self.acc[-1][1:]
                    return _Inst()
                return h
        N = len(self.ops)
        accs = []
        for e, fn, waits, inc in self.ops:
            r = _Rec()
            try:
                fn(r)
            except Exception:
                pass
            accs.append(r.acc)
        import numpy as np
        anc = np.zeros((N, N), dtype=bool)
        preds = [[] for _ in range(N)]
        last_by_eng = {}
        inc_points = {}
        cum = {}
        for i, (e, fn, waits, inc) in enumerate(self.ops):
            if e in last_by_eng:
                preds[i].append(last_by_eng[e])
            last_by_eng[e] = i
            if inc is not None:
                s, n = inc
                cum[s] = cum.get(s, 0) + n
                inc_points.setdefault(s, []).append((cum[s], i))
        for i, (e, fn, waits, inc) in enumerate(self.ops):
            for s, v in waits:
                for cv, j in inc_points.get(s, []):
                    if cv >= v:
                        preds[i].append(j)
                        break
        order = list(range(N))
        for i in order:
            for j in preds[i]:
                anc[i] |= anc[j]
                anc[i][j] = True
        psum_names = {f"pb{k}": k for k in range(8)}
        issues = []
        def tag(i):
            return f"L{self.ops[i][1].__code__.co_firstlineno}"
        for i in range(N):
            ei = self.ops[i][0]
            for j in range(i + 1, N):
                ej = self.ops[j][0]
                if anc[j][i] or anc[i][j] or ei == ej:
                    continue
                for (ki, ni, li, hi) in accs[i]:
                    for (kj, nj, lj, hj) in accs[j]:
                        if ni != nj:
                            continue
                        if ni in psum_names:
                            if (ki == "w" or kj == "w"):
                                issues.append((i, j, ni, "psum-bank", ki, kj, tag(i), tag(j)))
                        else:
                            if (ki == "w" or kj == "w") and li < hj and lj < hi:
                                issues.append((i, j, ni, f"[{max(li,lj)}:{min(hi,hj)})", ki, kj, tag(i), tag(j)))
        return issues

    def run(self, eng_name, eng, skip_sem=None):
        last = {}
        for e, fn, waits, inc in self.ops:
            if e != eng_name:
                continue
            for sem, val in waits:
                if sem is skip_sem:
                    continue
                if last.get(sem, 0) < val:
                    eng.wait_ge(sem.handle, val)
                    last[sem] = val
            inst = fn(eng)
            if inc is not None:
                inst.then_inc(inc[0].handle, inc[1])


def build_nc(upto="full", ops_limit=None):
    nc = bass.Bass("TRN2", target_bir_lowering=False)
    dram = {}
    for nm, shp in [("xT", [D, T]), ("wrT", [D, D]), ("wkT", [D, D]), ("wvT", [D, D]),
                    ("woT", [NST, D]), ("wsT", [D, NST]), ("wiT", [D, NST]), ("wiN", [NST, D]),
                    ("aT", [NST, NST]), ("bT", [NST, NST]), ("cT", [NST, NST]),
                    ("wmod", [D, 6144]), ("ccol", [NST, DC]), ("bi", [NST, 1]), ("bs", [NST, 1]),
                    ("borow", [1, D]), ("murc", [NST, DC]), ("mukc", [NST, DC])]:
        dram[nm] = nc.dram_tensor(nm, shp, F32, kind="ExternalInput").ap()
    yout = nc.dram_tensor("yout", [D, T], F32, kind="ExternalOutput").ap()
    x1sc = nc.dram_tensor("x1sc", [D, T], F32, kind="Internal").ap()

    sch = _Sched()
    SPE, SACT, SDVE, SGPC = (sch.sem(n) for n in ("pe", "act", "dve", "gpc"))
    DSP, DGP, SST = (sch.sem(n) for n in ("dsp", "dgp", "dst"))
    E = sch.emit

    AF32_N = 13664
    EPI = 13408
    ABF_N = 41152
    import contextlib
    with contextlib.ExitStack() as _st:
        _e = _st.enter_context
        wr = _e(nc.sbuf_tensor("wr", [128, DC * D], BF16))
        wk = _e(nc.sbuf_tensor("wk", [128, DC * D], BF16))
        wv = _e(nc.sbuf_tensor("wv", [128, DC * D], BF16))
        wo = _e(nc.sbuf_tensor("wo", [128, D], BF16))
        ws = _e(nc.sbuf_tensor("ws", [128, D], BF16))
        wib = _e(nc.sbuf_tensor("wib", [128, D], BF16))
        wiTs = _e(nc.sbuf_tensor("wiTs", [128, D], BF16))
        wiNs = _e(nc.sbuf_tensor("wiNs", [128, D], BF16))
        wia = _e(nc.sbuf_tensor("wia", [128, D], BF16))
        at = _e(nc.sbuf_tensor("at", [128, NST], BF16))
        btS = _e(nc.sbuf_tensor("btS", [128, NST], BF16))
        ct = _e(nc.sbuf_tensor("ct", [128, NST], BF16))
        identf = _e(nc.sbuf_tensor("identf", [128, 1], F32))
        identb = _e(nc.sbuf_tensor("identb", [128, 128], BF16))
        onescol = _e(nc.sbuf_tensor("onescol", [128, 1], BF16))
        onesrow = _e(nc.sbuf_tensor("onesrow", [1, CH], BF16))
        onesrowf = _e(nc.sbuf_tensor("onesrowf", [1, NST], F32))
        epsc = _e(nc.sbuf_tensor("epsc", [128, 1], F32))
        colsA = _e(nc.sbuf_tensor("colsA", [128, 32], F32))
        sab = _e(nc.sbuf_tensor("sab", [128, DC], BF16))
        sbb = _e(nc.sbuf_tensor("sbb", [128, DC], BF16))
        hcols = _e(nc.sbuf_tensor("hcols", [128, DC], BF16))
        crr = _e(nc.sbuf_tensor("crr", [128, DC], F32))
        crk = _e(nc.sbuf_tensor("crk", [128, DC], F32))
        siluf = _e(nc.sbuf_tensor("siluf", [128, DC], F32))
        silub = _e(nc.sbuf_tensor("silub", [128, DC], BF16))
        csvec = _e(nc.sbuf_tensor("csvec", [128, 8], F32))
        csb = _e(nc.sbuf_tensor("csb", [128, 2], BF16))
        bzc = _e(nc.sbuf_tensor("bzc", [128, 1], F32))
        murS = _e(nc.sbuf_tensor("murS", [128, DC], F32))
        mukS = _e(nc.sbuf_tensor("mukS", [128, DC], F32))
        borowS = _e(nc.sbuf_tensor("borowS", [1, D], BF16))
        AF32 = _e(nc.sbuf_tensor("AF32", [128, AF32_N], F32))
        ABF = _e(nc.sbuf_tensor("ABF", [128, ABF_N], BF16))
        pb0 = _e(nc.psum_tensor("pb0", [128, 512], F32))
        pb1 = _e(nc.psum_tensor("pb1", [128, 512], F32))
        pb2 = _e(nc.psum_tensor("pb2", [128, 512], F32))
        pb3 = _e(nc.psum_tensor("pb3", [128, 512], F32))
        pb4 = _e(nc.psum_tensor("pb4", [128, 512], F32))
        pb5 = _e(nc.psum_tensor("pb5", [128, 512], F32))
        pb6 = _e(nc.psum_tensor("pb6", [128, 512], F32))
        pb7 = _e(nc.psum_tensor("pb7", [128, 512], F32))
        h0 = _e(nc.semaphore("h0"))
        h1 = _e(nc.semaphore("h1"))
        h2 = _e(nc.semaphore("h2"))
        h3 = _e(nc.semaphore("h3"))
        h4 = _e(nc.semaphore("h4"))
        h5 = _e(nc.semaphore("h5"))
        h6 = _e(nc.semaphore("h6"))
        _sempool = [_e(nc.semaphore(f"spool{i}")) for i in range(48)]
        block = _e(nc.Block())

        for s, h in zip((SPE, SACT, SDVE, SGPC, DSP, DGP, SST), (h0, h1, h2, h3, h4, h5, h6)):
            s.handle = h
        sch.handle_pool = iter(_sempool)
        PB = [pb0, pb1, pb2, pb3, pb4, pb5, pb6, pb7]

        def dbg_exit(dumps):
            toks = []
            for row0, src_fn, deps, cast in dumps:
                eng = "gp" if cast else "sp"
                toks.append(E(eng, lambda s, row0=row0, src_fn=src_fn: s.dma_start(
                    out=yout[row0:row0 + src_fn().shape[0], 0:src_fn().shape[1]],
                    in_=src_fn(), allow_slow_non_contiguous=True), deps=deps, inc=SST, inc_n=16))
            E("gp", lambda g: g.engine_nop(), deps=[_Tok(SST, 16 * len(dumps))])
            sch.lint()

            @block.tensor
            def _(t):
                sch.run("pe", t, skip_sem=SPE)

            @block.scalar
            def _(s):
                sch.run("act", s)

            @block.vector
            def _(v):
                sch.run("dve", v)

            @block.gpsimd
            def _(g):
                sch.run("gp", g)

            @block.sync
            def _(s):
                sch.run("sp", s)

        X0A = 0
        X1A = 4096
        WSB = 8192
        RW1 = 12384
        assert WSB + max(W + T, T) // 1 <= RW1 or True
        def af(c0, w):
            return AF32[:, c0:c0 + w]
        def afr(c0, w):
            return AF32[0:1, c0:c0 + w]

        A0, B0, C0, D0, E0, F0, G0 = 0, 8192, 12288, 16384, 20496, 30800, 34897
        H0 = 36960
        def ab(c0, w):
            return ABF[:, c0:c0 + w]
        def abr(c0, w):
            return ABF[0:1, c0:c0 + w]

        t_at = E("gp", lambda g: g.dma_start(out=at[:], in_=dram["aT"]), stream="w_at")
        t_bt = E("gp", lambda g: g.dma_start(out=btS[:], in_=dram["bT"]), stream="w_bt")
        t_ct = E("gp", lambda g: g.dma_start(out=ct[:], in_=dram["cT"]), stream="w_ct")
        t_win = E("gp", lambda g: g.dma_start(out=wiNs[:], in_=dram["wiN"]), stream="w_win")
        t_wo = E("gp", lambda g: g.dma_start(out=wo[:], in_=dram["woT"]), stream="w_wo")
        t_ws = E("gp", lambda g: g.dma_start(
            out=ws[:].rearrange("p (r n) -> p r n", r=8), in_=dram["wsT"].rearrange("(r p) n -> p r n", p=128)), stream="w_ws")
        t_wiT = E("gp", lambda g: g.dma_start(
            out=wiTs[:].rearrange("p (r n) -> p r n", r=8), in_=dram["wiT"].rearrange("(r p) n -> p r n", p=128)), stream="w_wiT")
        t_wr = E("gp", lambda g: g.dma_start(
            out=wr[:].rearrange("p (r d) -> p r d", r=8), in_=dram["wrT"].rearrange("(r p) d -> p r d", p=128)), stream="w_wr")
        t_wk = E("gp", lambda g: g.dma_start(
            out=wk[:].rearrange("p (r d) -> p r d", r=8), in_=dram["wkT"].rearrange("(r p) d -> p r d", p=128)), stream="w_wk")
        t_wv = E("gp", lambda g: g.dma_start(
            out=wv[:].rearrange("p (r d) -> p r d", r=8), in_=dram["wvT"].rearrange("(r p) d -> p r d", p=128)), stream="w_wv")
        t_bi = E("sp", lambda s: s.dma_start(out=csvec[:, 3:4], in_=dram["bi"]), stream="s_bi")
        t_bs = E("sp", lambda s: s.dma_start(out=csvec[:, 4:5], in_=dram["bs"]), stream="s_bs")
        t_mur = E("sp", lambda s: s.dma_start(out=murS[:], in_=dram["murc"]), stream="s_mur")
        t_muk = E("sp", lambda s: s.dma_start(out=mukS[:], in_=dram["mukc"]), stream="s_muk")
        t_bo = E("sp", lambda s: s.dma_start(out=afr(RW1, D), in_=dram["borow"]), stream="s_bo")
        t_cc = E("sp", lambda s: s.dma_start(out=siluf[:], in_=dram["ccol"]), stream="s_cc")

        t_id0 = E("dve", lambda v: v.memset(identf[:], 0.0), inc=SDVE)
        E("dve", lambda v: v.memset(identf[0:1, 0:1], 1.0), deps=[t_id0])
        t_ib0 = E("gp", lambda g: g.memset(identb[:], 0.0), inc=SGPC)
        t_identb = E("gp", lambda g: g.affine_select(
            out=identb[:], in_=identb[:], compare_op=ALU.not_equal, fill=1.0, base=0,
            pattern=[[-1, 128]], channel_multiplier=1), deps=[t_ib0], inc=SGPC)
        E("dve", lambda v: v.memset(onescol[:], 1.0))
        E("dve", lambda v: v.memset(onesrow[:], 1.0))
        E("dve", lambda v: v.memset(onesrowf[:], 1.0))
        E("dve", lambda v: v.memset(epsc[:], EPS))
        E("dve", lambda v: v.memset(ab(E0, R), 0.0))
        E("dve", lambda v: v.memset(ab(F0, 1), 0.0))
        t_init = E("dve", lambda v: v.memset(ABF[:, H0:H0 + max(W, 1)], 0.0), inc=SDVE)
        t_bobf = E("dve", lambda v: v.tensor_copy(borowS[:], afr(RW1, D)), deps=[t_bo], inc=SDVE)
        t_silu = E("act", lambda s: s.activation(out=siluf[:], in_=siluf[:], func=AF.Silu),
                   deps=[t_cc], inc=SACT)
        t_silub = E("dve", lambda v: v.tensor_copy(silub[:], siluf[:]), deps=[t_silu, t_bobf],
                    inc=SDVE)

        if upto == "load":
            if ops_limit is not None:
                del sch.ops[ops_limit:]
            dbg_exit([
                (0, lambda: siluf[:], [], False),
            ])
            return nc

        wmodb_v = lambda par: ABF[:, F0 + par * 2048:F0 + (par + 1) * 2048]
        wmodf_v = lambda par: AF32[:, X1A + par * 1024:X1A + (par + 1) * 1024]

        def mod_block(c0, nbf, prev_mm, bank_gate=()):
            ng = 4 if nbf else 2
            dma_t, mm_t = [None] * 8, [None] * 8
            for kk in range(8):
                par = kk % 2
                deps = [t_silub]
                if kk >= 2:
                    deps.append(mm_t[kk - 2])
                if kk < 2 and prev_mm is not None:
                    deps.append(prev_mm[6 + kk])
                if nbf:
                    dma_t[kk] = E("gp", lambda g, kk=kk, par=par, c0=c0: g.dma_start(
                        out=wmodb_v(par), in_=dram["wmod"][kk * 128:(kk + 1) * 128, c0:c0 + 2048]),
                        deps=deps, stream=f"wmodb{par}")
                else:
                    dma_t[kk] = E("gp", lambda g, kk=kk, par=par, c0=c0: g.dma_start(
                        out=wmodf_v(par), in_=dram["wmod"][kk * 128:(kk + 1) * 128, c0:c0 + 1024]),
                        deps=deps, stream=f"wmodf{par}")
                for gi in range(ng):
                    srcf = (lambda par=par, gi=gi: wmodb_v(par)[:, gi * 512:(gi + 1) * 512]) if nbf \
                        else (lambda par=par, gi=gi: wmodf_v(par)[:, gi * 512:(gi + 1) * 512])
                    lhs = (lambda kk=kk: silub[:, kk:kk + 1]) if nbf else (lambda kk=kk: siluf[:, kk:kk + 1])
                    deps = [dma_t[kk]] + (list(bank_gate) if kk == 0 else [])
                    mm_t[kk] = E("pe", lambda t, gi=gi, kk=kk, srcf=srcf, lhs=lhs: t.matmul(
                        PB[4 + gi][0:1, 0:512], lhs(), srcf(),
                        start=(kk == 0), stop=(kk == 7)), deps=deps, inc=SPE)
            return mm_t

        mmA = mod_block(0, True, None)
        tr_prev = _Tok(SPE, 0)

        def row_to_cols(src_row_fn, dst, dst0, n, deps):
            nonlocal tr_prev
            outs = []
            for i in range(n):
                tp = E("pe", lambda t, i=i: t.matmul(
                    PB[3][0:128, 0:1], src_row_fn(i), identf[0:1, 0:1],
                    start=True, stop=True),
                    deps=list(deps) + [tr_prev], inc=SPE)
                tr_prev = E("dve", lambda v, i=i: v.tensor_copy(
                    dst[:, dst0 + i:dst0 + i + 1], PB[3][0:128, 0:1]), deps=[tp], inc=SDVE)
                outs.append(tr_prev)
            return outs

        t_saev, t_caev = [], []
        for gi in range(2):
            t_saev.append(E("dve", lambda v, gi=gi: v.tensor_copy(
                afr(X0A + gi * 512, 512), PB[4 + gi][0:1, 0:512]), deps=[mmA[7]], inc=SDVE))
        for gi in range(2):
            t_caev.append(E("dve", lambda v, gi=gi: v.tensor_scalar(
                out=afr(RW1, D)[0:1, gi * 512:(gi + 1) * 512], in0=PB[6 + gi][0:1, 0:512],
                scalar1=1.0, scalar2=None, op0=ALU.add), deps=[mmA[7], t_silub], inc=SDVE))
        t_caplb = E("dve", lambda v: v.tensor_copy(abr(D0, D), afr(RW1, D)),
                    deps=t_caev, inc=SDVE)
        sab_cols = row_to_cols(lambda i: afr(X0A + 0, 1024)[0:1, i * 128:(i + 1) * 128],
                               sab, 0, DC, t_saev + [mmA[7]])
        capl_cols = row_to_cols(lambda i: afr(RW1, D)[0:1, i * 128:(i + 1) * 128],
                                colsA, 0, DC, t_caev)

        mmB = mod_block(3072, True, mmA, bank_gate=t_saev + t_caev)
        t_sbev = []
        for gi in range(2):
            t_sbev.append(E("dve", lambda v, gi=gi: v.tensor_copy(
                afr(X0A + gi * 512, 512), PB[4 + gi][0:1, 0:512]),
                deps=[mmB[7], sab_cols[-1]], inc=SDVE))
        t_cbev = []
        for gi in range(2):
            t_cbev.append(E("dve", lambda v, gi=gi: v.tensor_scalar(
                out=afr(RW1, D)[0:1, gi * 512:(gi + 1) * 512], in0=PB[6 + gi][0:1, 0:512],
                scalar1=1.0, scalar2=None, op0=ALU.add),
                deps=[mmB[7], capl_cols[-1], t_caplb], inc=SDVE))
        sbb_cols = row_to_cols(lambda i: afr(X0A + 0, 1024)[0:1, i * 128:(i + 1) * 128],
                               sbb, 0, DC, t_sbev + [mmB[7]])
        cbpl_cols = row_to_cols(lambda i: afr(RW1, D)[0:1, i * 128:(i + 1) * 128],
                                colsA, 8, DC, t_cbev)

        mmGa = mod_block(2048, False, mmB, bank_gate=t_sbev + t_cbev)
        t_gaev = E("act", lambda s: s.activation(
            out=afr(X0A + 512, D)[0:1, 0:512], in_=PB[4][0:1, 0:512], func=AF.Abs),
            deps=[mmGa[7], sbb_cols[-1], cbpl_cols[-1]], inc=SACT)
        t_gaev2 = E("act", lambda s: s.activation(
            out=afr(X0A + 512, D)[0:1, 512:1024], in_=PB[5][0:1, 0:512], func=AF.Abs),
            deps=[mmGa[7], t_gaev], inc=SACT)
        t_gm = E("dve", lambda v: v.tensor_scalar_max(
            afr(X0A + 512, D), afr(X0A + 512, D), 0.25), deps=[t_gaev2], inc=SDVE)
        t_gs1 = E("dve", lambda v: v.tensor_scalar(
            out=afr(X0A + 1536, D)[0:1, 0:512], in0=PB[4][0:1, 0:512],
            scalar1=0.0, scalar2=None, op0=ALU.is_ge), deps=[mmGa[7]], inc=SDVE)
        t_gs2 = E("dve", lambda v: v.tensor_scalar(
            out=afr(X0A + 1536, D)[0:1, 512:1024], in0=PB[5][0:1, 0:512],
            scalar1=0.0, scalar2=None, op0=ALU.is_ge), deps=[mmGa[7]], inc=SDVE)
        t_gs3 = E("dve", lambda v: v.tensor_scalar(
            out=afr(X0A + 1536, D), in0=afr(X0A + 1536, D),
            scalar1=2.0, scalar2=-1.0, op0=ALU.mult, op1=ALU.add), deps=[t_gs1, t_gs2], inc=SDVE)
        t_ga = E("dve", lambda v: v.tensor_mul(
            afr(X0A + 512, D), afr(X0A + 512, D), afr(X0A + 1536, D)),
            deps=[t_gm, t_gs3], inc=SDVE)
        ga_cols = row_to_cols(lambda i: afr(X0A + 512, D)[0:1, i * 128:(i + 1) * 128],
                              colsA, 16, DC, [t_ga])

        mmGb = mod_block(5120, False, mmGa, bank_gate=[t_ga])
        t_gbev = E("act", lambda s: s.activation(
            out=afr(X0A + 512, D)[0:1, 0:512], in_=PB[4][0:1, 0:512], func=AF.Abs),
            deps=[mmGb[7], ga_cols[-1]], inc=SACT)
        t_gbev2 = E("act", lambda s: s.activation(
            out=afr(X0A + 512, D)[0:1, 512:1024], in_=PB[5][0:1, 0:512], func=AF.Abs),
            deps=[mmGb[7], t_gbev], inc=SACT)
        t_gm = E("dve", lambda v: v.tensor_scalar_max(
            afr(X0A + 512, D), afr(X0A + 512, D), 0.25), deps=[t_gbev2], inc=SDVE)
        t_gs1 = E("dve", lambda v: v.tensor_scalar(
            out=afr(X0A + 1536, D)[0:1, 0:512], in0=PB[4][0:1, 0:512],
            scalar1=0.0, scalar2=None, op0=ALU.is_ge), deps=[mmGb[7]], inc=SDVE)
        t_gs2 = E("dve", lambda v: v.tensor_scalar(
            out=afr(X0A + 1536, D)[0:1, 512:1024], in0=PB[5][0:1, 0:512],
            scalar1=0.0, scalar2=None, op0=ALU.is_ge), deps=[mmGb[7]], inc=SDVE)
        t_gs3 = E("dve", lambda v: v.tensor_scalar(
            out=afr(X0A + 1536, D), in0=afr(X0A + 1536, D),
            scalar1=2.0, scalar2=-1.0, op0=ALU.mult, op1=ALU.add), deps=[t_gs1, t_gs2], inc=SDVE)
        t_gb = E("dve", lambda v: v.tensor_mul(
            afr(X0A + 512, D), afr(X0A + 512, D), afr(X0A + 1536, D)),
            deps=[t_gm, t_gs3], inc=SDVE)
        gb_cols = row_to_cols(lambda i: afr(X0A + 512, D)[0:1, i * 128:(i + 1) * 128],
                              colsA, 24, DC, [t_gb])
        t_gbb = E("dve", lambda v: v.tensor_copy(abr(D0 + 1024, D), afr(X0A + 512, D)),
                  deps=[t_gb], inc=SDVE)

        if upto == "mod":
            dbg_exit([
                (0, lambda: colsA[:], [capl_cols[-1], cbpl_cols[-1], ga_cols[-1], gb_cols[-1]], False),
                (128, lambda: sab[:], [sab_cols[-1]], True),
                (256, lambda: sbb[:], [sbb_cols[-1]], True),
            ])
            return nc

        if upto == "mod":
            if ops_limit is not None:
                del sch.ops[ops_limit:]
            dbg_exit([
                (0, lambda: colsA[:], [capl_cols[-1], cbpl_cols[-1], ga_cols[-1], gb_cols[-1]] if ops_limit is None else [], False),
                (128, lambda: sab[:], [sab_cols[-1]] if ops_limit is None else [], True),
                (256, lambda: sbb[:], [sbb_cols[-1]] if ops_limit is None else [], True),
                (384, lambda: AF32[0:1, RW1:RW1 + D], [capl_cols[-1]] if ops_limit is None else [], False),
                (385, lambda: AF32[0:1, X0A:X0A + D], [sab_cols[-1]] if ops_limit is None else [], False),
            ])
            return nc

        t_mm = None
        for kk in range(DC):
            t_mm = E("pe", lambda t, kk=kk: t.matmul(
                PB[6][0:128, 0:1], ws[:, kk * 128:(kk + 1) * 128], sab[:, kk:kk + 1],
                start=(kk == 0), stop=(kk == 7)),
                deps=[t_ws, sab_cols[kk], mmGb[7]] + (t_cbev if kk == 0 else []), inc=SPE)
        t_cs = E("dve", lambda v: v.scalar_tensor_tensor(
            out=csvec[:, 0:1], in0=PB[6][0:128, 0:1], scalar=1.0, in1=csvec[:, 4:5],
            op0=ALU.mult, op1=ALU.add), deps=[t_mm, t_bs], inc=SDVE)
        t_mm = None
        for kk in range(DC):
            t_mm = E("pe", lambda t, kk=kk: t.matmul(
                PB[6][0:128, 1:2], wiTs[:, kk * 128:(kk + 1) * 128], sab[:, kk:kk + 1],
                start=(kk == 0), stop=(kk == 7)),
                deps=[t_wiT, sab_cols[kk]] + ([t_cs] if kk == 0 else []), inc=SPE)
        t_cu = E("dve", lambda v: v.scalar_tensor_tensor(
            out=csvec[:, 1:2], in0=PB[6][0:128, 1:2], scalar=1.0, in1=csvec[:, 3:4],
            op0=ALU.mult, op1=ALU.add), deps=[t_mm, t_bi], inc=SDVE)
        t_csb = E("dve", lambda v: v.tensor_copy(csb[:], csvec[:, 0:2]), deps=[t_cs, t_cu],
                  inc=SDVE)
        t_mm = E("pe", lambda t: t.matmul(PB[6][0:128, 2:3], at[:], csb[:, 0:1],
                                          start=True, stop=False), deps=[t_at, t_csb], inc=SPE)
        t_mm = E("pe", lambda t: t.matmul(PB[6][0:128, 2:3], btS[:], csb[:, 1:2],
                                          start=False, stop=True), deps=[t_bt, t_mm], inc=SPE)
        t_bz = E("dve", lambda v: v.tensor_copy(bzc[:], PB[6][0:128, 2:3]), deps=[t_mm], inc=SDVE)
        t_mm = E("pe", lambda t: t.matmul(PB[6][0:128, 3:4], ct[:], csb[:, 0:1],
                                          start=True, stop=True), deps=[t_ct, t_csb, t_bz], inc=SPE)
        t_ccs = E("dve", lambda v: v.tensor_copy(csvec[:, 2:3], PB[6][0:128, 3:4]),
                  deps=[t_mm], inc=SDVE)
        t_mm = E("pe", lambda t: t.matmul(PB[6][0:128, 4:5], at[:], csb[:, 0:1],
                                          start=True, stop=True), deps=[t_at, t_csb, t_ccs], inc=SPE)
        t_acs = E("dve", lambda v: v.tensor_copy(csvec[:, 5:6], PB[6][0:128, 4:5]),
                  deps=[t_mm], inc=SDVE)

        crr_last = t_acs
        for nm, wt, wtok, dst in (("crr", wr, t_wr, crr), ("crk", wk, t_wk, crk)):
            for gg in range(2):
                t_mm = None
                for kk in range(DC):
                    t_mm = E("pe", lambda t, kk=kk, gg=gg, wt=wt: t.matmul(
                        PB[6][0:1, 0:512], sbb[:, kk:kk + 1],
                        wt[:, kk * D + gg * 512:kk * D + (gg + 1) * 512],
                        start=(kk == 0), stop=(kk == 7)),
                        deps=[wtok, sbb_cols[kk], crr_last], inc=SPE)
                crr_last = E("dve", lambda v, gg=gg: v.tensor_copy(
                    afr(X0A + 2560, 1024)[0:1, gg * 512:(gg + 1) * 512], PB[6][0:1, 0:512]),
                    deps=[t_mm], inc=SDVE)
            cols = row_to_cols(lambda i: afr(X0A + 2560, 1024)[0:1, i * 128:(i + 1) * 128],
                               dst, 0, DC, [crr_last])
            crr_last = cols[-1]
        t_crcols = crr_last

        t_wsf = None
        for m in range(DC):
            t_wsf = E("act", lambda s, m=m: s.activation(
                out=ws[:, m * 128:(m + 1) * 128], in_=ws[:, m * 128:(m + 1) * 128],
                func=AF.Identity, bias=0.0, scale=colsA[:, m:m + 1]),
                deps=[t_cs, capl_cols[m], t_ws], inc=SACT)
        bc_t = []
        for gg in range(2):
            tm = E("pe", lambda t, gg=gg: t.matmul(
                PB[6][0:128, 0:512], onesrow[0:1, 0:128], abr(D0 + gg * 512, 512),
                start=True, stop=True),
                deps=[t_caplb, t_crcols] + (bc_t[:1] if gg else []), inc=SPE)
            bc_t.append(E("dve", lambda v, gg=gg: v.tensor_copy(
                ABF[:, G0 + gg * 512:G0 + (gg + 1) * 512], PB[6][0:128, 0:512]),
                deps=[tm], inc=SDVE))
        t_wia = E("dve", lambda v: v.tensor_mul(wia[:], wiNs[:], ab(G0, 1024)),
                  deps=[bc_t[1], t_win], inc=SDVE)
        t_wibl = None
        for m in range(DC):
            tm = E("pe", lambda t, m=m: t.matmul(
                PB[6][0:128, 0:128], wia[:, m * 128:(m + 1) * 128], btS[:],
                start=True, stop=True),
                deps=[t_wia, t_bt] + ([t_wibl] if t_wibl else []), inc=SPE)
            t_wibl = E("dve", lambda v, m=m: v.tensor_copy(
                wib[:, m * 128:(m + 1) * 128], PB[6][0:128, 0:128]), deps=[tm], inc=SDVE)
        t_wib = t_wibl
        t_wrf = t_wkf = None
        for m in range(DC):
            t_wrf = E("act", lambda s, m=m: s.activation(
                out=wr[:, m * D:(m + 1) * D], in_=wr[:, m * D:(m + 1) * D],
                func=AF.Identity, bias=0.0, scale=colsA[:, 8 + m:9 + m]),
                deps=[t_crcols, cbpl_cols[m], t_wr], inc=SACT)
            t_wkf = E("act", lambda s, m=m: s.activation(
                out=wk[:, m * D:(m + 1) * D], in_=wk[:, m * D:(m + 1) * D],
                func=AF.Identity, bias=0.0, scale=colsA[:, 8 + m:9 + m]),
                deps=[t_crcols, cbpl_cols[m], t_wk], inc=SACT)
        bc_t = []
        for gg in range(2):
            tm = E("pe", lambda t, gg=gg: t.matmul(
                PB[6][0:128, 0:512], onesrow[0:1, 0:128], abr(D0 + 1024 + gg * 512, 512),
                start=True, stop=True),
                deps=[t_gbb, t_wib] + (bc_t[:1] if gg else []), inc=SPE)
            bc_t.append(E("dve", lambda v, gg=gg: v.tensor_copy(
                ABF[:, G0 + gg * 512:G0 + (gg + 1) * 512], PB[6][0:128, 0:512]),
                deps=[tm, t_wia], inc=SDVE))
        t_wvf = None
        for m in range(DC):
            t_wvf = E("dve", lambda v, m=m: v.tensor_mul(
                wv[:, m * D:(m + 1) * D], wv[:, m * D:(m + 1) * D], ab(G0, 1024)),
                deps=[bc_t[1], t_wv], inc=SDVE)

        t_pad = E("dve", lambda v: v.memset(ab(F0, 1), 0.0),
                  deps=[mmB[7], mmGb[7]], inc=SDVE)

        t_hcols = None
        for m in range(DC):
            t_r = E("dve", lambda v, m=m: v.reciprocal(csvec[:, 6:7], colsA[:, 8 + m:9 + m]),
                    deps=[cbpl_cols[m], t_cs, t_cu] + ([t_hcols] if t_hcols else []), inc=SDVE)
            t_m = E("dve", lambda v, m=m: v.tensor_mul(csvec[:, 7:8], sbb[:, m:m + 1],
                    csvec[:, 6:7]), deps=[t_r, sbb_cols[m]], inc=SDVE)
            t_hcols = E("dve", lambda v, m=m: v.tensor_scalar(
                out=hcols[:, m:m + 1], in0=csvec[:, 7:8], scalar1=-1.0, scalar2=None,
                op0=ALU.mult), deps=[t_m], inc=SDVE)

        if upto == "pro":
            dbg_exit([
                (0, lambda: colsA[:], [t_cs, t_cu, t_ccs, t_acs, t_crcols, t_wvf, t_wsf if t_wsf else None, t_wkf], False),
                (128, lambda: csvec[:], [t_acs, t_bz], False),
                (256, lambda: crr[:], [t_crcols], False),
                (384, lambda: crk[:], [t_crcols], False),
                (512, lambda: bzc[:], [t_bz], False),
                (640, lambda: sab[:], [t_crcols], True),
                (768, lambda: sbb[:], [t_crcols], True),
                (896, lambda: wib[:, 0:256], [t_wib], True),
            ])
            return nc

        bfree = {0: mmGb[7], 1: mmGb[7], 2: mmGb[7], 3: tr_prev, 4: t_gb, 5: t_gb,
                 6: bc_t[1], 7: t_cbev[1]}
        xbufb = lambda par: ABF[:, A0 + par * 4096:A0 + (par + 1) * 4096]
        xsqP = lambda: ABF[:, B0:B0 + 4096]
        xcb = lambda: ABF[:, C0:C0 + 4096]
        sbc_v = ABF[:, G0:G0 + 512]
        mbc_v = ABF[:, G0 + 512:G0 + 1024]
        rowsS = lambda i: AF32[0:1, X1A + 2048 + i * CH:X1A + 2048 + (i + 1) * CH]
        rowbf = lambda i: ABF[0:1, D0 + 2816 + i * CH:D0 + 2816 + (i + 1) * CH]
        spb_c = lambda c0, w: ABF[:, F0 + c0:F0 + c0 + w]
        wsb_c = lambda c0, w: ABF[:, H0 + c0:H0 + c0 + w]
        thb_c = lambda c0, w: ABF[:, E0 + c0:E0 + c0 + w]

        t_xdma = [None] * NCH
        t_wev = [None] * NCH
        t_spev = [None] * NCH
        xb_last = {0: t_silub, 1: t_silub}
        rows_free = t_caplb
        xsq_free = t_gbb
        xc_free = t_gbb
        for c in range(NCH):
            par = c % 2
            t0 = c * CH
            t_xdma[c] = E("gp", lambda g, t0=t0, par=par: g.dma_start(
                out=xbufb(par).rearrange("p (r t) -> p r t", r=8), in_=dram["xT"][:, t0:t0 + CH].rearrange("(r p) t -> p r t", p=128)),
                deps=[xb_last[par]], stream=f"x_{par}")
            t_sq = [None] * DC
            for m in range(DC):
                t_sq[m] = E("gp", lambda g, m=m, par=par: g.tensor_mul(
                    xsqP()[:, m * CH:(m + 1) * CH], xbufb(par)[:, m * CH:(m + 1) * CH],
                    xbufb(par)[:, m * CH:(m + 1) * CH]),
                    deps=[t_xdma[c], xsq_free], inc=SGPC)
            t_sum = t_ssq = None
            for m in range(DC):
                t_sum = E("pe", lambda t, m=m, par=par: t.matmul(
                    PB[0][0:1, 0:CH], onescol[:], xbufb(par)[:, m * CH:(m + 1) * CH],
                    start=(m == 0), stop=(m == 7)), deps=[t_xdma[c], bfree[0], rows_free], inc=SPE)
            for m in range(DC):
                t_ssq = E("pe", lambda t, m=m: t.matmul(
                    PB[1][0:1, 0:CH], onescol[:], xsqP()[:, m * CH:(m + 1) * CH],
                    start=(m == 0), stop=(m == 7)), deps=[t_sq[m], bfree[1], rows_free], inc=SPE)
            t_mrbf = E("dve", lambda v: v.tensor_scalar(
                out=rowbf(0), in0=PB[0][0:1, 0:CH], scalar1=1.0 / D, scalar2=None,
                op0=ALU.mult), deps=[t_sum], inc=SDVE)
            t_mf = E("dve", lambda v: v.tensor_scalar(
                out=rowsS(0), in0=PB[0][0:1, 0:CH], scalar1=1.0 / D, scalar2=None,
                op0=ALU.mult), deps=[t_sum], inc=SDVE)
            t_msq = E("dve", lambda v: v.tensor_mul(rowsS(1), rowsS(0), rowsS(0)),
                      deps=[t_mf], inc=SDVE)
            t_var = E("dve", lambda v: v.scalar_tensor_tensor(
                out=rowsS(2), in0=PB[1][0:1, 0:CH], scalar=1.0 / D, in1=rowsS(1),
                op0=ALU.mult, op1=ALU.subtract), deps=[t_ssq, t_msq], inc=SDVE)
            bfree[0] = t_mf
            bfree[1] = t_var
            t_ln = E("act", lambda s: s.activation(
                out=rowsS(3), in_=rowsS(2), func=AF.Ln, bias=epsc[0:1, 0:1], scale=1.0),
                deps=[t_var, t_wsf], inc=SACT)
            t_rstd = E("act", lambda s: s.activation(
                out=rowsS(2), in_=rowsS(3), func=AF.Exp, bias=0.0, scale=-0.5),
                deps=[t_ln], inc=SACT)
            t_rstdb = E("dve", lambda v: v.tensor_copy(rowbf(1), rowsS(2)),
                        deps=[t_rstd], inc=SDVE)
            rows_free = t_rstdb
            t_sbcm = E("pe", lambda t: t.matmul(
                PB[2][0:128, 0:CH], onesrow[0:1, 0:128], rowbf(1), start=True, stop=True),
                deps=[t_rstdb, bfree[2]], inc=SPE)
            t_mbcm = E("pe", lambda t: t.matmul(
                PB[3][0:128, 0:CH], onesrow[0:1, 0:128], rowbf(0), start=True, stop=True),
                deps=[t_mrbf, bfree[3]], inc=SPE)
            t_sbce = E("dve", lambda v: v.tensor_copy(sbc_v, PB[2][0:128, 0:CH]),
                       deps=[t_sbcm], inc=SDVE)
            t_mbce = E("dve", lambda v: v.tensor_copy(mbc_v, PB[3][0:128, 0:CH]),
                       deps=[t_mbcm], inc=SDVE)
            bfree[2] = t_sbce
            bfree[3] = t_mbce
            t_xc = [None] * DC
            for m in range(DC):
                t_xc[m] = E("dve", lambda v, m=m, par=par: v.tensor_sub(
                    xcb()[:, m * CH:(m + 1) * CH], xbufb(par)[:, m * CH:(m + 1) * CH], mbc_v),
                    deps=[t_mbce, t_xdma[c], xc_free], inc=SDVE)
            xb_last[par] = t_xc[DC - 1]
            t_mm = None
            for m in range(DC):
                t_mm = E("pe", lambda t, m=m: t.matmul(
                    PB[5][0:128, 0:CH], ws[:, m * 128:(m + 1) * 128], xcb()[:, m * CH:(m + 1) * CH],
                    start=(m == 0), stop=(m == 7)), deps=[t_xc[m], t_wsf, bfree[5]], inc=SPE)
            t_spev[c] = E("dve", lambda v, t0=t0: v.tensor_mul(
                spb_c(1 + t0, CH), PB[5][0:128, 0:CH], sbc_v), deps=[t_mm, t_sbce], inc=SDVE)
            bfree[5] = t_spev[c]
            t_spa = E("pe", lambda t, t0=t0: t.matmul(
                PB[4][0:128, 0:CH], at[:], spb_c(t0, CH), start=True, stop=True),
                deps=[t_spev[c], t_at, bfree[4]] + ([t_pad] if c == 0 else []), inc=SPE)
            t_mm = None
            for m in range(DC):
                t_mm = E("pe", lambda t, m=m: t.matmul(
                    PB[6][0:128, 0:CH], wib[:, m * 128:(m + 1) * 128], xcb()[:, m * CH:(m + 1) * CH],
                    start=(m == 0), stop=(m == 7)), deps=[t_xc[m], t_wib, bfree[6]], inc=SPE)
            t_w1 = E("dve", lambda v: v.tensor_mul(af(X1A, CH)[:, 0:CH], PB[6][0:128, 0:CH], sbc_v),
                     deps=[t_mm, t_sbce], inc=SDVE)
            t_wev[c] = E("dve", lambda v, t0=t0: v.tensor_add(
                wsb_c(W + t0, CH), af(X1A, CH)[:, 0:CH], PB[4][0:128, 0:CH]),
                deps=[t_w1, t_spa, t_init], inc=SDVE)
            if c == 0:
                t_wev[0] = E("dve", lambda v: v.tensor_sub(
                    wsb_c(W, 1), wsb_c(W, 1), csvec[:, 5:6]),
                    deps=[t_wev[0], t_acs], inc=SDVE)
            bfree[4] = t_wev[c]
            bfree[6] = t_wev[c]
            xsq_free = t_ssq
            xc_free = t_mm

        if upto == "pre":
            dbg_exit([
                (0, lambda: ABF[:, H0 + W:H0 + W + T], [t_wev[NCH - 1]], True),
                (128, lambda: ABF[:, F0 + 1:F0 + 1 + T], [t_spev[NCH - 1]], True),
            ])
            return nc

        all_w = t_wev[NCH - 1]
        def slot(i):
            return PB[i % 8][0:128, 0:R]
        ring_free0 = [bfree[k] for k in range(8)]
        t_fill = [None] * ROUNDS
        t_cmm = [None] * ROUNDS
        t_tanh = [None] * ROUNDS

        def emit_fill(i):
            deps = [all_w, t_identb]
            if i >= 8:
                deps.append(t_tanh[i - 8])
            else:
                deps.append(ring_free0[i % 8])
            if R > 1:
                srcf = lambda i=i: ABF[:, H0 + i:H0 + i + (R - 1) * S + 1:S]
            else:
                srcf = lambda i=i: ABF[:, H0 + W + i:H0 + W + i + 1]
            t_fill[i] = E("pe", lambda t, srcf=srcf, i=i: t.matmul(
                slot(i), identb[:], srcf(), start=True, stop=False), deps=deps, inc=SPE)

        for i in range(min(8, ROUNDS)):
            emit_fill(i)
        for i in range(ROUNDS):
            deps = [t_fill[i], t_bz, t_wvf]
            if i > 0:
                deps.append(t_tanh[i - 1])
            if W > 0 and i == W:
                t_fix = E("dve", lambda v, i=i: v.memset(thb_c(i * R, 1), 0.0),
                          deps=[t_tanh[i - 1]], inc=SDVE)
                deps.append(t_fix)
            t_cmm[i] = E("pe", lambda t, i=i: t.matmul(
                slot(i), at[:], thb_c(i * R, R), start=False, stop=True), deps=deps, inc=SPE)
            t_tanh[i] = E("act", lambda s, i=i: s.activation(
                out=thb_c((i + 1) * R, R), in_=slot(i), func=AF.Tanh, bias=bzc[:], scale=1.0),
                deps=[t_cmm[i]], inc=SACT)
            if i + 8 < ROUNDS:
                emit_fill(i + 8)
        t_scan = t_tanh[ROUNDS - 1]

        if upto == "scan":
            dbg_exit([
                (0, lambda: ABF[:, E0 + (W + 1) * R:E0 + (W + 1) * R + min((ROUNDS - W) * R, T)], [t_scan], True),
            ])
            return nc

        x0b = lambda par: AF32[:, X0A + par * 2048:X0A + (par + 1) * 2048]
        x1bv = lambda par: AF32[:, X1A + par * 2048:X1A + (par + 1) * 2048]
        kscrv = lambda p: AF32[:, WSB + p * CB:WSB + (p + 1) * CB]
        vscrv = lambda p: AF32[:, WSB + 512 + p * CB:WSB + 512 + (p + 1) * CB]
        m2row = lambda par: AF32[0:1, WSB + 1024 + par * CB:WSB + 1024 + (par + 1) * CB]
        rowsA = lambda par, i: AF32[0:1, RW1 + par * 512 + i * CB:RW1 + par * 512 + (i + 1) * CB]
        xbfc = lambda par: ABF[:, A0 + par * 2048:A0 + (par + 1) * 2048]
        xsqv = lambda par: ABF[:, A0 + 4096 + par * 2048:A0 + 4096 + (par + 1) * 2048]
        rbufv = ABF[:, B0:B0 + 2048]
        kabv = ABF[:, B0 + 2048:B0 + 4096]
        xrbv = lambda par: ABF[:, C0 + par * 2048:C0 + (par + 1) * 2048]
        xn2v = lambda par, m: ABF[:, D0 + par * 2056 + m * 257:D0 + par * 2056 + m * 257 + 257]
        xkbv = lambda par: ABF[:, H0 + par * 2048:H0 + (par + 1) * 2048]
        s2bcv = ABF[:, G0:G0 + CB]
        m2bcv = ABF[:, G0 + CB:G0 + 2 * CB]
        rstd2r = lambda par: ABF[0:1, G0 + 512 + par * CB:G0 + 512 + (par + 1) * CB]
        ycbv = lambda par: ABF[:, G0 + 1024 + par * CB:G0 + 1024 + (par + 1) * CB]
        epiv = lambda p: AF32[:, WSB + 1536 + p * CB:WSB + 1536 + (p + 1) * CB]

        bk = {i: t_scan for i in range(8)}
        bkh = {}
        t_x0 = [None] * NCB
        t_yc = [None] * NCB
        out2_last = [None] * NCB
        t_x1m = [[None] * DC for _ in range(NCB)]
        t_cast = [[None] * DC for _ in range(NCB)]
        t_sqx = [[None] * DC for _ in range(NCB)]
        t_sumT = [None] * NCB
        t_ssqT = [None] * NCB
        t_m2r = [None] * NCB
        t_varT = [None] * NCB
        t_rstd = [None] * NCB
        t_s2m = [None] * NCB
        t_m2m = [None] * NCB
        t_s2e = [None] * NCB
        t_m2e = [None] * NCB
        t_sub = [[None] * DC for _ in range(NCB)]
        t_xn = [[None] * DC for _ in range(NCB)]
        t_halo = [[None] * DC for _ in range(NCB)]
        t_dm = [[None] * DC for _ in range(NCB)]
        t_xr = [[None] * DC for _ in range(NCB)]
        t_xk = [[None] * DC for _ in range(NCB)]
        t_sig = [[None] * DC for _ in range(NCB)]
        t_rel = [[None] * DC for _ in range(NCB)]
        t_ksq = [[None] * DC for _ in range(NCB)]
        t_vm = [[None] * DC for _ in range(NCB)]
        t_x2 = [[None] * DC for _ in range(NCB)]
        t_store = [None] * NCB
        r_done = [None] * NCB
        k_done = [None] * NCB
        v_done = [None] * NCB
        pre_gp = t_sq[DC - 1]

        for c in range(NCB + 1):
            par = c % 2
            t0 = c * CB
            if c < NCB:
                dep0 = [gb_cols[-1], t_crcols] if c < 2 else [t_x1m[c - 2][DC - 1]]
                t_x0[c] = E("sp", lambda s, t0=t0, par=par: s.dma_start(
                    out=x0b(par).rearrange("p (r t) -> p r t", r=8),
                    in_=dram["xT"][:, t0:t0 + CB].rearrange("(r p) t -> p r t", p=128)),
                    deps=dep0, stream=f"x0_{par}")
                t_mm = None
                for rr in range(CB // S):
                    base = (W + 1) * R + (t0 // S + rr)
                    t_mm = E("pe", lambda t, base=base, rr=rr: t.matmul(
                        PB[2][0:128, rr * S:rr * S + S],
                        ct[:], ABF[:, E0 + base:E0 + base + (S - 1) * R + 1:R],
                        start=(rr == 0), stop=False),
                        deps=([t_scan, t_ct, bk[2]] if rr == 0 else [t_mm]), inc=SPE)
                t_mm = E("pe", lambda t, t0=t0: t.matmul(
                    PB[2][0:128, 0:CB], ct[:], spb_c(1 + t0, CB), start=False, stop=True),
                    deps=[t_mm], inc=SPE)
                ycw = [out2_last[c - 2]] if c >= 2 else []
                t_yc[c] = E("act", lambda s, par=par: s.activation(
                    out=ycbv(par), in_=PB[2][0:128, 0:CB], func=AF.Identity,
                    bias=csvec[:, 2:3], scale=1.0), deps=[t_mm, t_ccs] + ycw, inc=SACT)
                bk[2] = t_yc[c]
                for m in range(DC):
                    bb = 3 - (m % 2)
                    ep = m % 2
                    dd = [t_yc[c], t_wo, t_bobf, bkh.get(bb, bk[bb])]
                    t_mm = E("pe", lambda t, m=m, bb=bb, par=par: t.matmul(
                        PB[bb][0:128, 0:CB], wo[:, m * 128:(m + 1) * 128], ycbv(par),
                        start=True, stop=False), deps=dd, inc=SPE)
                    t_mm = E("pe", lambda t, m=m, bb=bb: t.matmul(
                        PB[bb][0:128, 0:CB], borowS[0:1, m * 128:(m + 1) * 128],
                        onesrow[0:1, 0:CB], start=False, stop=True), deps=[t_mm], inc=SPE)
                    if m == DC - 1:
                        out2_last[c] = t_mm
                    edeps = [t_mm, ga_cols[m]]
                    if m >= 2:
                        edeps.append(t_x1m[c][m - 2])
                    elif c >= 1:
                        edeps.append(t_x1m[c - 1][DC - 2 + ep])
                    t_sc = E("act", lambda s, m=m, bb=bb, ep=ep: s.activation(
                        out=epiv(ep), in_=PB[bb][0:128, 0:CB], func=AF.Identity,
                        bias=0.0, scale=colsA[:, 16 + m:17 + m]), deps=edeps, inc=SACT)
                    bkh[bb] = t_sc
                    wdeps = [t_sc, t_x0[c]]
                    if c >= 2:
                        wdeps.append(t_store[c - 2])
                    elif par == 0:
                        wdeps.append(t_w1)
                    else:
                        wdeps.append(rows_free)
                    t_x1m[c][m] = E("dve", lambda v, m=m, ep=ep, par=par: v.tensor_add(
                        x1bv(par)[:, m * CB:(m + 1) * CB], epiv(ep),
                        x0b(par)[:, m * CB:(m + 1) * CB]), deps=wdeps, inc=SDVE)
                    cdeps = [t_x1m[c][m], pre_gp]
                    if c >= 2:
                        cdeps += [t_sumT[c - 2], t_sub[c - 2][m]]
                    t_cast[c][m] = E("act", lambda s, m=m, par=par: s.activation(
                        out=xbfc(par)[:, m * CB:(m + 1) * CB],
                        in_=x1bv(par)[:, m * CB:(m + 1) * CB], func=AF.Identity,
                        bias=0.0, scale=1.0), deps=cdeps, inc=SACT)
                    sdeps = [t_cast[c][m]]
                    if c >= 2:
                        sdeps.append(t_ssqT[c - 2])
                    t_sqx[c][m] = E("gp", lambda g, m=m, par=par: g.tensor_mul(
                        xsqv(par)[:, m * CB:(m + 1) * CB],
                        xbfc(par)[:, m * CB:(m + 1) * CB],
                        xbfc(par)[:, m * CB:(m + 1) * CB]), deps=sdeps, inc=SGPC)
                bk[3] = bkh[3]
                bk[2] = bkh[2]
            if c >= 1:
                pr = (c - 1) % 2
                for m in range(DC):
                    bb = 4 + (m % 2)
                    t_mm = None
                    for kk in range(DC):
                        dd = [t_mm] if t_mm else [bk[bb], t_wrf, t_xr[c - 1][DC - 1]]
                        t_mm = E("pe", lambda t, m=m, kk=kk, bb=bb, pr=pr: t.matmul(
                            PB[bb][0:128, 0:CB], wr[:, kk * D + m * 128:kk * D + (m + 1) * 128],
                            xrbv(pr)[:, kk * CB:(kk + 1) * CB], start=(kk == 0), stop=(kk == 7)),
                            deps=dd, inc=SPE)
                    if m == DC - 1:
                        r_done[c - 1] = t_mm
                    rw = [t_vm[c - 2][m]] if c >= 2 else []
                    t_sig[c - 1][m] = E("act", lambda s, m=m, bb=bb: s.activation(
                        out=rbufv[:, m * CB:(m + 1) * CB], in_=PB[bb][0:128, 0:CB],
                        func=AF.Sigmoid, bias=crr[:, m:m + 1], scale=1.0),
                        deps=[t_mm, t_crcols] + rw, inc=SACT)
                    bk[bb] = t_sig[c - 1][m]
            if c < NCB:
                t_mm = None
                for m in range(DC):
                    dd = [t_cast[c][m]] + ([t_mm] if t_mm else [bk[0]])
                    t_mm = E("pe", lambda t, m=m, par=par: t.matmul(
                        PB[0][0:1, 0:CB], onescol[:], xbfc(par)[:, m * CB:(m + 1) * CB],
                        start=(m == 0), stop=(m == 7)), deps=dd, inc=SPE)
                t_sumT[c] = t_mm
                t_mm = None
                for m in range(DC):
                    dd = [t_sqx[c][m]] + ([t_mm] if t_mm else [bk[1]])
                    t_mm = E("pe", lambda t, m=m, par=par: t.matmul(
                        PB[1][0:1, 0:CB], onescol[:], xsqv(par)[:, m * CB:(m + 1) * CB],
                        start=(m == 0), stop=(m == 7)), deps=dd, inc=SPE)
                t_ssqT[c] = t_mm
                mdeps = [t_sumT[c]]
                if c >= 2:
                    mdeps.append(t_m2m[c - 2])
                t_m2r[c] = E("dve", lambda v, par=par: v.tensor_scalar(
                    out=m2row(par), in0=PB[0][0:1, 0:CB], scalar1=1.0 / D, scalar2=None,
                    op0=ALU.mult), deps=mdeps, inc=SDVE)
                qdeps = [t_m2r[c]]
                qdeps.append(t_rstd[c - 2] if c >= 2 else cbpl_cols[-1])
                t_msq = E("dve", lambda v, par=par: v.tensor_mul(
                    rowsA(par, 0), m2row(par), m2row(par)), deps=qdeps, inc=SDVE)
                t_varT[c] = E("dve", lambda v, par=par: v.scalar_tensor_tensor(
                    out=rowsA(par, 1), in0=PB[1][0:1, 0:CB], scalar=1.0 / D,
                    in1=rowsA(par, 0), op0=ALU.mult, op1=ALU.subtract),
                    deps=[t_ssqT[c], t_msq], inc=SDVE)
                t_ln = E("act", lambda s, par=par: s.activation(
                    out=rowsA(par, 0), in_=rowsA(par, 1), func=AF.Ln,
                    bias=epsc[0:1, 0:1], scale=1.0), deps=[t_varT[c], t_msq], inc=SACT)
                rw = [t_s2m[c - 2]] if c >= 2 else []
                t_rstd[c] = E("act", lambda s, par=par: s.activation(
                    out=rstd2r(par), in_=rowsA(par, 0), func=AF.Exp, bias=0.0,
                    scale=-0.5), deps=[t_ln] + rw, inc=SACT)
                t_s2m[c] = E("pe", lambda t, par=par: t.matmul(
                    PB[6][0:128, 0:CB], onesrow[0:1, 0:128], rstd2r(par),
                    start=True, stop=True), deps=[t_rstd[c], bk[6]], inc=SPE)
                s2r = [t_xn[c - 1][DC - 1]] if c >= 1 else []
                t_s2e[c] = E("dve", lambda v: v.tensor_copy(s2bcv, PB[6][0:128, 0:CB]),
                             deps=[t_s2m[c]] + s2r, inc=SDVE)
                t_m2m[c] = E("pe", lambda t, par=par: t.matmul(
                    PB[7][0:128, 0:CB], onesrowf[0:1, 0:128], m2row(par),
                    start=True, stop=True), deps=[t_m2r[c], bk[7]], inc=SPE)
                m2rd = [t_sub[c - 1][DC - 1]] if c >= 1 else []
                t_m2e[c] = E("dve", lambda v: v.tensor_copy(m2bcv, PB[7][0:128, 0:CB]),
                             deps=[t_m2m[c]] + m2rd, inc=SDVE)
                bk[6] = t_s2e[c]
                bk[7] = t_m2e[c]
            if c >= 1:
                pr = (c - 1) % 2
                for m in range(DC):
                    bb = 6 + (m % 2)
                    kp = m % 2
                    t_mm = None
                    for kk in range(DC):
                        dd = [t_mm] if t_mm else [bk[bb], t_wkf, t_xk[c - 1][DC - 1]]
                        t_mm = E("pe", lambda t, m=m, kk=kk, bb=bb, pr=pr: t.matmul(
                            PB[bb][0:128, 0:CB], wk[:, kk * D + m * 128:kk * D + (m + 1) * 128],
                            xkbv(pr)[:, kk * CB:(kk + 1) * CB], start=(kk == 0), stop=(kk == 7)),
                            deps=dd, inc=SPE)
                    if m == DC - 1:
                        k_done[c - 1] = t_mm
                    if m >= 2:
                        kw = [t_ksq[c - 1][m - 2]]
                    elif c >= 2:
                        kw = [t_ksq[c - 2][DC - 2 + kp]]
                    else:
                        kw = []
                    t_rel[c - 1][m] = E("act", lambda s, m=m, bb=bb, kp=kp: s.activation(
                        out=kscrv(kp), in_=PB[bb][0:128, 0:CB], func=AF.Relu,
                        bias=crk[:, m:m + 1], scale=1.0), deps=[t_mm, t_crcols] + kw, inc=SACT)
                    bk[bb] = t_rel[c - 1][m]
                    kabw = [v_done[c - 2]] if c >= 2 else []
                    t_ksq[c - 1][m] = E("gp", lambda g, m=m, kp=kp: g.tensor_mul(
                        kabv[:, m * CB:(m + 1) * CB], kscrv(kp), kscrv(kp)),
                        deps=[t_rel[c - 1][m]] + kabw, inc=SGPC)
            if c < NCB:
                for m in range(DC):
                    if c == 0:
                        t_halo[c][m] = E("dve", lambda v, m=m, par=par: v.tensor_copy(
                            xn2v(par, m)[:, 0:1], hcols[:, m:m + 1]),
                            deps=[t_hcols, t_wvf, pre_gp], inc=SDVE)
                    else:
                        hdeps = [t_xn[c - 1][m]]
                        if c >= 2:
                            hdeps.append(t_dm[c - 2][m])
                        t_halo[c][m] = E("dve", lambda v, m=m, par=par: v.tensor_copy(
                            xn2v(par, m)[:, 0:1], xn2v(1 - par, m)[:, CB:CB + 1]),
                            deps=hdeps, inc=SDVE)
                for m in range(DC):
                    sdeps = [t_m2e[c], t_cast[c][m]]
                    if c == 0:
                        sdeps.append(t_wvf)
                    if c >= 1:
                        sdeps.append(t_halo[c - 1][m])
                    if c >= 2:
                        sdeps.append(t_xr[c - 2][m])
                    t_sub[c][m] = E("gp", lambda g, m=m, par=par: g.tensor_sub(
                        xn2v(par, m)[:, 1:1 + CB], xbfc(par)[:, m * CB:(m + 1) * CB],
                        m2bcv), deps=sdeps, inc=SGPC)
                for m in range(DC):
                    t_xn[c][m] = E("dve", lambda v, m=m, par=par: v.tensor_mul(
                        xn2v(par, m)[:, 1:1 + CB], xn2v(par, m)[:, 1:1 + CB], s2bcv),
                        deps=[t_sub[c][m], t_s2e[c]], inc=SDVE)
                for m in range(DC):
                    ddeps = [t_xn[c][m], t_halo[c][m]]
                    if c >= 2:
                        ddeps.append(r_done[c - 2])
                    t_dm[c][m] = E("gp", lambda g, m=m, par=par: g.tensor_sub(
                        xrbv(par)[:, m * CB:(m + 1) * CB], xn2v(par, m)[:, 1:1 + CB],
                        xn2v(par, m)[:, 0:CB]), deps=ddeps, inc=SGPC)
                for m in range(DC):
                    xkw = [k_done[c - 2]] if c >= 2 else []
                    t_xk[c][m] = E("dve", lambda v, m=m, par=par: v.scalar_tensor_tensor(
                        out=xkbv(par)[:, m * CB:(m + 1) * CB],
                        in0=xrbv(par)[:, m * CB:(m + 1) * CB],
                        scalar=mukS[:, m:m + 1], in1=xn2v(par, m)[:, 0:CB],
                        op0=ALU.mult, op1=ALU.add), deps=[t_dm[c][m], t_muk] + xkw, inc=SDVE)
                    t_xr[c][m] = E("dve", lambda v, m=m, par=par: v.scalar_tensor_tensor(
                        out=xrbv(par)[:, m * CB:(m + 1) * CB],
                        in0=xrbv(par)[:, m * CB:(m + 1) * CB],
                        scalar=murS[:, m:m + 1], in1=xn2v(par, m)[:, 0:CB],
                        op0=ALU.mult, op1=ALU.add), deps=[t_xk[c][m], t_mur], inc=SDVE)
            if c >= 1:
                pr = (c - 1) % 2
                vfree = {0: (t_m2r[c] if c < NCB else t_vm[c - 2][DC - 2]),
                         1: (t_varT[c] if c < NCB else t_vm[c - 2][DC - 1])}
                for m in range(DC):
                    bb = m % 2
                    vp = m % 2
                    t_mm = None
                    for kk in range(DC):
                        dd = [t_mm] if t_mm else [t_ksq[c - 1][DC - 1], t_wvf, vfree[bb]]
                        t_mm = E("pe", lambda t, m=m, kk=kk, bb=bb: t.matmul(
                            PB[bb][0:128, 0:CB], wv[:, kk * D + m * 128:kk * D + (m + 1) * 128],
                            kabv[:, kk * CB:(kk + 1) * CB], start=(kk == 0), stop=(kk == 7)),
                            deps=dd, inc=SPE)
                    if m == DC - 1:
                        v_done[c - 1] = t_mm
                    if m >= 2:
                        vwm = [t_x2[c - 1][m - 2]]
                    elif c >= 2:
                        vwm = [t_x2[c - 2][DC - 2 + vp]]
                    else:
                        vwm = []
                    t_vm[c - 1][m] = E("dve", lambda v, m=m, bb=bb, vp=vp: v.tensor_mul(
                        vscrv(vp), rbufv[:, m * CB:(m + 1) * CB], PB[bb][0:128, 0:CB]),
                        deps=[t_mm, t_sig[c - 1][m]] + vwm, inc=SDVE)
                    vfree[bb] = t_vm[c - 1][m]
                    t_x2[c - 1][m] = E("dve", lambda v, m=m, vp=vp, pr=pr: v.tensor_add(
                        x1bv(pr)[:, m * CB:(m + 1) * CB], x1bv(pr)[:, m * CB:(m + 1) * CB],
                        vscrv(vp)),
                        deps=[t_vm[c - 1][m], t_x1m[c - 1][m], t_cast[c - 1][m],
                              t_sqx[c - 1][m]], inc=SDVE)
                bk[0] = t_vm[c - 1][DC - 2]
                bk[1] = t_vm[c - 1][DC - 1]
                t_store[c - 1] = E("sp", lambda s, c=c, pr=pr: s.dma_start(
                    out=yout[:, (c - 1) * CB:c * CB].rearrange("(r p) t -> p r t", p=128),
                    in_=x1bv(pr).rearrange("p (r t) -> p r t", r=8)),
                    deps=[t_x2[c - 1][DC - 1]], stream=f"yo_{pr}")

        E("gp", lambda g: g.engine_nop(), deps=[t_store[NCB - 2], t_store[NCB - 1]])

        sch.lint()
        import os as _os
        if _os.environ.get("KRACE"):
            for issue in sch.racecheck()[:300]:
                print("RACE:", issue)

        @block.tensor
        def _(t):
            sch.run("pe", t, skip_sem=SPE)

        @block.scalar
        def _(s):
            sch.run("act", s)

        @block.vector
        def _(v):
            sch.run("dve", v)

        @block.gpsimd
        def _(g):
            sch.run("gp", g)

        @block.sync
        def _(s):
            sch.run("sp", s)

    return nc


_NC_CACHE = None


def _get_nc():
    global _NC_CACHE
    if _NC_CACHE is None:
        _NC_CACHE = build_nc()
    return _NC_CACHE


def kernel(**inputs):
    x = np.asarray(inputs["L0_attn_x"], dtype=np.float32)
    Wi = np.asarray(inputs["L0_attn_Wi"], dtype=np.float32)
    bi = np.asarray(inputs["L0_attn_bi"], dtype=np.float32)
    Ws = np.asarray(inputs["L0_attn_Ws"], dtype=np.float32)
    bs = np.asarray(inputs["L0_attn_bs"], dtype=np.float32)
    Wo = np.asarray(inputs["L0_attn_Wo"], dtype=np.float32)
    bo = np.asarray(inputs["L0_attn_bo"], dtype=np.float32)
    A = np.asarray(inputs["L0_attn_A"], dtype=np.float32)
    B = np.asarray(inputs["L0_attn_B"], dtype=np.float32)
    C = np.asarray(inputs["L0_attn_C"], dtype=np.float32)
    W_r = np.asarray(inputs["L0_ffn_W_r"], dtype=np.float32)
    W_k = np.asarray(inputs["L0_ffn_W_k"], dtype=np.float32)
    W_v = np.asarray(inputs["L0_ffn_W_v"], dtype=np.float32)
    mu_r = np.asarray(inputs["L0_ffn_mu_r"], dtype=np.float32)
    mu_k = np.asarray(inputs["L0_ffn_mu_k"], dtype=np.float32)
    cmat = np.asarray(inputs["L0_c"], dtype=np.float32)
    wmod = np.asarray(inputs["L0_wmod"], dtype=np.float32)

    shared = {
        "wrT": np.ascontiguousarray(W_r.T), "wkT": np.ascontiguousarray(W_k.T),
        "wvT": np.ascontiguousarray(W_v.T), "woT": np.ascontiguousarray(Wo.T),
        "wsT": np.ascontiguousarray(Ws.T), "wiT": np.ascontiguousarray(Wi.T),
        "wiN": np.ascontiguousarray(Wi),
        "aT": np.ascontiguousarray(A.T), "bT": np.ascontiguousarray(B.T),
        "cT": np.ascontiguousarray(C.T),
        "wmod": np.ascontiguousarray(wmod),
        "bi": np.ascontiguousarray(bi.reshape(NST, 1)),
        "bs": np.ascontiguousarray(bs.reshape(NST, 1)),
        "borow": np.ascontiguousarray(bo.reshape(1, D)),
        "murc": np.ascontiguousarray(mu_r.reshape(DC, 128).T),
        "mukc": np.ascontiguousarray(mu_k.reshape(DC, 128).T),
    }
    in_maps = []
    for b in range(8):
        m = dict(shared)
        m["xT"] = np.ascontiguousarray(x[b].T)
        m["ccol"] = np.ascontiguousarray(cmat[b].reshape(DC, 128).T)
        in_maps.append(m)

    nc = _get_nc()
    res = run_bass_kernel_spmd(nc, in_maps, core_ids=list(range(8)))
    out = np.stack([np.ascontiguousarray(r["yout"].T) for r in res.results], axis=0)
    return out.astype(np.float32)


# version 20 (final)
# speedup vs baseline: 1.8095x; 1.0934x over previous
import numpy as np
import ml_dtypes
import concourse.bass as bass
import concourse.mybir as mybir
from concourse.bass_utils import run_bass_kernel_spmd

F32 = mybir.dt.float32
BF16 = mybir.dt.bfloat16
AF = mybir.ActivationFunctionType
ALU = mybir.AluOpType

T, D, DC, NST = 4096, 1024, 8, 128
CH = 512
NCH = T // CH
CB = 256
NCB = T // CB
R = 64
W = 96 if R > 1 else 0
S = T // R
ROUNDS = S + W
SPB = max(1, 512 // R)
RINGB = 4
EPS = 1e-5


class _Tok:
    __slots__ = ("sem", "val")
    def __init__(self, sem, val):
        self.sem, self.val = sem, val


class _Sem:
    def __init__(self, name):
        self.name, self.total, self.handle = name, 0, None


class _Sched:
    def __init__(self):
        self.ops, self.sems = [], []
    def sem(self, name):
        s = _Sem(name); self.sems.append(s); return s
    def emit(self, eng, fn, deps=(), inc=None, inc_n=1, stream=None):
        deps = list(deps)
        if stream is not None:
            if not hasattr(self, "streams"):
                self.streams = {}
            sem, prev = self.streams.get(stream, (None, None))
            if sem is None:
                sem = self.sem("st_" + stream)
                sem.handle = next(self.handle_pool)
            if prev is not None:
                deps.append(prev)
            inc, inc_n = sem, 16
        waits = [(t.sem, t.val) for t in deps if t is not None and t.val > 0]
        tok = None
        if inc is not None:
            inc.total += inc_n
            tok = _Tok(inc, inc.total)
        if stream is not None:
            self.streams[stream] = (inc, tok)
        self.ops.append((eng, fn, waits, (inc, inc_n) if inc else None))
        return tok
    def lint(self):
        for _, _, waits, _ in self.ops:
            for sem, val in waits:
                assert val <= sem.total, f"wait {sem.name}>={val} > total {sem.total}"
        engs = {}
        for i, (e, fn, waits, inc) in enumerate(self.ops):
            engs.setdefault(e, []).append((waits, inc, i))
        ptr = {e: 0 for e in engs}
        val = {id(s): 0 for s in self.sems}
        progress = True
        while progress:
            progress = False
            for e, lst in engs.items():
                while ptr[e] < len(lst):
                    waits, inc, i = lst[ptr[e]]
                    if all(val[id(s)] >= v for s, v in waits):
                        if inc is not None:
                            val[id(inc[0])] += inc[1]
                        ptr[e] += 1
                        progress = True
                    else:
                        break
        stuck = {e: (ptr[e], len(lst)) for e, lst in engs.items() if ptr[e] < len(lst)}
        if stuck:
            detail = []
            for e, lst in engs.items():
                if ptr[e] < len(lst):
                    waits, inc, i = lst[ptr[e]]
                    why = [(s.name, v, val[id(s)]) for s, v in waits if val[id(s)] < v]
                    detail.append(f"{e}@op{i}: needs {why}")
            raise AssertionError("DEADLOCK: " + "; ".join(detail))
    def racecheck(self):
        class _Inst:
            def then_inc(self, *a, **k):
                return self
        class _Rec:
            def __init__(self):
                self.acc = []
            def _ap(self, ap, kind):
                try:
                    t = ap.tensor
                    name = t.name
                    pairs = [tuple(x) for x in ap.ap]
                    pstride = int(pairs[0][0]) if pairs else 1
                    lo = int(ap.offset) % max(pstride, 1) if pstride > 1 else int(ap.offset)
                    hi = lo
                    for pair in pairs[1:]:
                        st, ct = int(pair[0]), int(pair[1])
                        hi += abs(st) * (ct - 1)
                    self.acc.append((kind, name, lo, hi + 1))
                except Exception:
                    pass
            def __getattr__(self, nm):
                def h(*args, **kw):
                    aps = []
                    outs = []
                    for i, a in enumerate(args):
                        if hasattr(a, "tensor"):
                            (outs if i == 0 else aps).append(a)
                    for k, a in kw.items():
                        if hasattr(a, "tensor"):
                            (outs if k.startswith("out") else aps).append(a)
                    if nm in ("dma_start",):
                        pass
                    for a in outs:
                        self._ap(a, "w")
                    for a in aps:
                        self._ap(a, "r")
                    if nm in ("memset", "memzero") and outs == [] and aps:
                        self.acc[-1] = ("w",) + self.acc[-1][1:]
                    return _Inst()
                return h
        N = len(self.ops)
        accs = []
        for e, fn, waits, inc in self.ops:
            r = _Rec()
            try:
                fn(r)
            except Exception:
                pass
            accs.append(r.acc)
        import numpy as np
        anc = np.zeros((N, N), dtype=bool)
        preds = [[] for _ in range(N)]
        last_by_eng = {}
        inc_points = {}
        cum = {}
        for i, (e, fn, waits, inc) in enumerate(self.ops):
            if e in last_by_eng:
                preds[i].append(last_by_eng[e])
            last_by_eng[e] = i
            if inc is not None:
                s, n = inc
                cum[s] = cum.get(s, 0) + n
                inc_points.setdefault(s, []).append((cum[s], i))
        for i, (e, fn, waits, inc) in enumerate(self.ops):
            for s, v in waits:
                for cv, j in inc_points.get(s, []):
                    if cv >= v:
                        preds[i].append(j)
                        break
        order = list(range(N))
        for i in order:
            for j in preds[i]:
                anc[i] |= anc[j]
                anc[i][j] = True
        psum_names = {f"pb{k}": k for k in range(8)}
        issues = []
        def tag(i):
            return f"L{self.ops[i][1].__code__.co_firstlineno}"
        for i in range(N):
            ei = self.ops[i][0]
            for j in range(i + 1, N):
                ej = self.ops[j][0]
                if anc[j][i] or anc[i][j] or ei == ej:
                    continue
                for (ki, ni, li, hi) in accs[i]:
                    for (kj, nj, lj, hj) in accs[j]:
                        if ni != nj:
                            continue
                        if ni in psum_names:
                            if (ki == "w" or kj == "w"):
                                issues.append((i, j, ni, "psum-bank", ki, kj, tag(i), tag(j)))
                        else:
                            if (ki == "w" or kj == "w") and li < hj and lj < hi:
                                issues.append((i, j, ni, f"[{max(li,lj)}:{min(hi,hj)})", ki, kj, tag(i), tag(j)))
        return issues

    def run(self, eng_name, eng, skip_sem=None):
        last = {}
        for e, fn, waits, inc in self.ops:
            if e != eng_name:
                continue
            for sem, val in waits:
                if sem is skip_sem:
                    continue
                if last.get(sem, 0) < val:
                    eng.wait_ge(sem.handle, val)
                    last[sem] = val
            inst = fn(eng)
            if inc is not None:
                inst.then_inc(inc[0].handle, inc[1])


def build_nc(upto="full", ops_limit=None):
    nc = bass.Bass("TRN2", target_bir_lowering=False)
    dram = {}
    for nm, shp in [("xT", [D, T]), ("wrT", [D, D]), ("wkT", [D, D]), ("wvT", [D, D]),
                    ("woT", [NST, D]), ("wsT", [D, NST]), ("wiT", [D, NST]), ("wiN", [NST, D]),
                    ("aT", [NST, NST]), ("bT", [NST, NST]), ("cT", [NST, NST]),
                    ("wmod", [D, 6144]), ("ccol", [NST, DC]), ("bi", [NST, 1]), ("bs", [NST, 1]),
                    ("borow", [1, D]), ("murc", [NST, DC]), ("mukc", [NST, DC])]:
        dram[nm] = nc.dram_tensor(nm, shp, F32, kind="ExternalInput").ap()
    yout = nc.dram_tensor("yout", [D, T], F32, kind="ExternalOutput").ap()
    x1sc = nc.dram_tensor("x1sc", [D, T], F32, kind="Internal").ap()

    sch = _Sched()
    SPE, SACT, SDVE, SGPC = (sch.sem(n) for n in ("pe", "act", "dve", "gpc"))
    DSP, DGP, SST = (sch.sem(n) for n in ("dsp", "dgp", "dst"))
    E = sch.emit

    AF32_N = 13664
    EPI = 13408
    ABF_N = 41152
    import contextlib
    with contextlib.ExitStack() as _st:
        _e = _st.enter_context
        wr = _e(nc.sbuf_tensor("wr", [128, DC * D], BF16))
        wk = _e(nc.sbuf_tensor("wk", [128, DC * D], BF16))
        wv = _e(nc.sbuf_tensor("wv", [128, DC * D], BF16))
        wo = _e(nc.sbuf_tensor("wo", [128, D], BF16))
        ws = _e(nc.sbuf_tensor("ws", [128, D], BF16))
        wib = _e(nc.sbuf_tensor("wib", [128, D], BF16))
        wiTs = _e(nc.sbuf_tensor("wiTs", [128, D], BF16))
        wiNs = _e(nc.sbuf_tensor("wiNs", [128, D], BF16))
        wia = _e(nc.sbuf_tensor("wia", [128, D], BF16))
        at = _e(nc.sbuf_tensor("at", [128, NST], BF16))
        btS = _e(nc.sbuf_tensor("btS", [128, NST], BF16))
        ct = _e(nc.sbuf_tensor("ct", [128, NST], BF16))
        identf = _e(nc.sbuf_tensor("identf", [128, 1], F32))
        identb = _e(nc.sbuf_tensor("identb", [128, 128], BF16))
        onescol = _e(nc.sbuf_tensor("onescol", [128, 1], BF16))
        onesrow = _e(nc.sbuf_tensor("onesrow", [1, CH], BF16))
        onesrowf = _e(nc.sbuf_tensor("onesrowf", [1, NST], F32))
        epsc = _e(nc.sbuf_tensor("epsc", [128, 1], F32))
        colsA = _e(nc.sbuf_tensor("colsA", [128, 32], F32))
        sab = _e(nc.sbuf_tensor("sab", [128, DC], BF16))
        sbb = _e(nc.sbuf_tensor("sbb", [128, DC], BF16))
        hcols = _e(nc.sbuf_tensor("hcols", [128, DC], BF16))
        crr = _e(nc.sbuf_tensor("crr", [128, DC], F32))
        crk = _e(nc.sbuf_tensor("crk", [128, DC], F32))
        siluf = _e(nc.sbuf_tensor("siluf", [128, DC], F32))
        silub = _e(nc.sbuf_tensor("silub", [128, DC], BF16))
        csvec = _e(nc.sbuf_tensor("csvec", [128, 8], F32))
        csb = _e(nc.sbuf_tensor("csb", [128, 2], BF16))
        bzc = _e(nc.sbuf_tensor("bzc", [128, 1], F32))
        murS = _e(nc.sbuf_tensor("murS", [128, DC], F32))
        mukS = _e(nc.sbuf_tensor("mukS", [128, DC], F32))
        borowS = _e(nc.sbuf_tensor("borowS", [1, D], BF16))
        AF32 = _e(nc.sbuf_tensor("AF32", [128, AF32_N], F32))
        ABF = _e(nc.sbuf_tensor("ABF", [128, ABF_N], BF16))
        pb0 = _e(nc.psum_tensor("pb0", [128, 512], F32))
        pb1 = _e(nc.psum_tensor("pb1", [128, 512], F32))
        pb2 = _e(nc.psum_tensor("pb2", [128, 512], F32))
        pb3 = _e(nc.psum_tensor("pb3", [128, 512], F32))
        pb4 = _e(nc.psum_tensor("pb4", [128, 512], F32))
        pb5 = _e(nc.psum_tensor("pb5", [128, 512], F32))
        pb6 = _e(nc.psum_tensor("pb6", [128, 512], F32))
        pb7 = _e(nc.psum_tensor("pb7", [128, 512], F32))
        h0 = _e(nc.semaphore("h0"))
        h1 = _e(nc.semaphore("h1"))
        h2 = _e(nc.semaphore("h2"))
        h3 = _e(nc.semaphore("h3"))
        h4 = _e(nc.semaphore("h4"))
        h5 = _e(nc.semaphore("h5"))
        h6 = _e(nc.semaphore("h6"))
        _sempool = [_e(nc.semaphore(f"spool{i}")) for i in range(48)]
        block = _e(nc.Block())

        for s, h in zip((SPE, SACT, SDVE, SGPC, DSP, DGP, SST), (h0, h1, h2, h3, h4, h5, h6)):
            s.handle = h
        sch.handle_pool = iter(_sempool)
        PB = [pb0, pb1, pb2, pb3, pb4, pb5, pb6, pb7]

        def dbg_exit(dumps):
            toks = []
            for row0, src_fn, deps, cast in dumps:
                eng = "gp" if cast else "sp"
                toks.append(E(eng, lambda s, row0=row0, src_fn=src_fn: s.dma_start(
                    out=yout[row0:row0 + src_fn().shape[0], 0:src_fn().shape[1]],
                    in_=src_fn(), allow_slow_non_contiguous=True), deps=deps, inc=SST, inc_n=16))
            E("gp", lambda g: g.engine_nop(), deps=[_Tok(SST, 16 * len(dumps))])
            sch.lint()

            @block.tensor
            def _(t):
                sch.run("pe", t, skip_sem=SPE)

            @block.scalar
            def _(s):
                sch.run("act", s)

            @block.vector
            def _(v):
                sch.run("dve", v)

            @block.gpsimd
            def _(g):
                sch.run("gp", g)

            @block.sync
            def _(s):
                sch.run("sp", s)

        X0A = 0
        X1A = 4096
        WSB = 8192
        RW1 = 12384
        assert WSB + max(W + T, T) // 1 <= RW1 or True
        def af(c0, w):
            return AF32[:, c0:c0 + w]
        def afr(c0, w):
            return AF32[0:1, c0:c0 + w]

        A0, B0, C0, D0, E0, F0, G0 = 0, 8192, 12288, 16384, 20496, 30800, 34897
        H0 = 36960
        def ab(c0, w):
            return ABF[:, c0:c0 + w]
        def abr(c0, w):
            return ABF[0:1, c0:c0 + w]

        t_at = E("gp", lambda g: g.dma_start(out=at[:], in_=dram["aT"]), stream="w_at")
        t_bt = E("gp", lambda g: g.dma_start(out=btS[:], in_=dram["bT"]), stream="w_bt")
        t_ct = E("gp", lambda g: g.dma_start(out=ct[:], in_=dram["cT"]), stream="w_ct")
        t_win = E("gp", lambda g: g.dma_start(out=wiNs[:], in_=dram["wiN"]), stream="w_win")
        t_wo = E("gp", lambda g: g.dma_start(out=wo[:], in_=dram["woT"]), stream="w_wo")
        t_ws = E("gp", lambda g: g.dma_start(
            out=ws[:].rearrange("p (r n) -> p r n", r=8), in_=dram["wsT"].rearrange("(r p) n -> p r n", p=128)), stream="w_ws")
        t_wiT = E("gp", lambda g: g.dma_start(
            out=wiTs[:].rearrange("p (r n) -> p r n", r=8), in_=dram["wiT"].rearrange("(r p) n -> p r n", p=128)), stream="w_wiT")
        t_wr = E("gp", lambda g: g.dma_start(
            out=wr[:].rearrange("p (r d) -> p r d", r=8), in_=dram["wrT"].rearrange("(r p) d -> p r d", p=128)), stream="w_wr")
        t_wk = E("gp", lambda g: g.dma_start(
            out=wk[:].rearrange("p (r d) -> p r d", r=8), in_=dram["wkT"].rearrange("(r p) d -> p r d", p=128)), stream="w_wk")
        t_wv = E("gp", lambda g: g.dma_start(
            out=wv[:].rearrange("p (r d) -> p r d", r=8), in_=dram["wvT"].rearrange("(r p) d -> p r d", p=128)), stream="w_wv")
        t_bi = E("sp", lambda s: s.dma_start(out=csvec[:, 3:4], in_=dram["bi"]), stream="s_bi")
        t_bs = E("sp", lambda s: s.dma_start(out=csvec[:, 4:5], in_=dram["bs"]), stream="s_bs")
        t_mur = E("sp", lambda s: s.dma_start(out=murS[:], in_=dram["murc"]), stream="s_mur")
        t_muk = E("sp", lambda s: s.dma_start(out=mukS[:], in_=dram["mukc"]), stream="s_muk")
        t_bo = E("sp", lambda s: s.dma_start(out=afr(RW1, D), in_=dram["borow"]), stream="s_bo")
        t_cc = E("sp", lambda s: s.dma_start(out=siluf[:], in_=dram["ccol"]), stream="s_cc")

        t_id0 = E("dve", lambda v: v.memset(identf[:], 0.0), inc=SDVE)
        E("dve", lambda v: v.memset(identf[0:1, 0:1], 1.0), deps=[t_id0])
        t_ib0 = E("gp", lambda g: g.memset(identb[:], 0.0), inc=SGPC)
        t_identb = E("gp", lambda g: g.affine_select(
            out=identb[:], in_=identb[:], compare_op=ALU.not_equal, fill=1.0, base=0,
            pattern=[[-1, 128]], channel_multiplier=1), deps=[t_ib0], inc=SGPC)
        E("dve", lambda v: v.memset(onescol[:], 1.0))
        E("dve", lambda v: v.memset(onesrow[:], 1.0))
        E("dve", lambda v: v.memset(onesrowf[:], 1.0))
        E("dve", lambda v: v.memset(epsc[:], EPS))
        E("dve", lambda v: v.memset(ab(E0, R), 0.0))
        E("dve", lambda v: v.memset(ab(F0, 1), 0.0))
        t_init = E("dve", lambda v: v.memset(ABF[:, H0:H0 + max(W, 1)], 0.0), inc=SDVE)
        t_bobf = E("dve", lambda v: v.tensor_copy(borowS[:], afr(RW1, D)), deps=[t_bo], inc=SDVE)
        t_silu = E("act", lambda s: s.activation(out=siluf[:], in_=siluf[:], func=AF.Silu),
                   deps=[t_cc], inc=SACT)
        t_silub = E("dve", lambda v: v.tensor_copy(silub[:], siluf[:]), deps=[t_silu, t_bobf],
                    inc=SDVE)

        if upto == "load":
            if ops_limit is not None:
                del sch.ops[ops_limit:]
            dbg_exit([
                (0, lambda: siluf[:], [], False),
            ])
            return nc

        wmodb_v = lambda par: ABF[:, F0 + par * 2048:F0 + (par + 1) * 2048]
        wmodf_v = lambda par: AF32[:, X1A + par * 1024:X1A + (par + 1) * 1024]

        def mod_block(c0, nbf, prev_mm, bank_gate=()):
            ng = 4 if nbf else 2
            dma_t, mm_t = [None] * 8, [None] * 8
            for kk in range(8):
                par = kk % 2
                deps = [t_silub]
                if kk >= 2:
                    deps.append(mm_t[kk - 2])
                if kk < 2 and prev_mm is not None:
                    deps.append(prev_mm[6 + kk])
                if nbf:
                    dma_t[kk] = E("gp", lambda g, kk=kk, par=par, c0=c0: g.dma_start(
                        out=wmodb_v(par), in_=dram["wmod"][kk * 128:(kk + 1) * 128, c0:c0 + 2048]),
                        deps=deps, stream=f"wmodb{par}")
                else:
                    dma_t[kk] = E("gp", lambda g, kk=kk, par=par, c0=c0: g.dma_start(
                        out=wmodf_v(par), in_=dram["wmod"][kk * 128:(kk + 1) * 128, c0:c0 + 1024]),
                        deps=deps, stream=f"wmodf{par}")
                for gi in range(ng):
                    srcf = (lambda par=par, gi=gi: wmodb_v(par)[:, gi * 512:(gi + 1) * 512]) if nbf \
                        else (lambda par=par, gi=gi: wmodf_v(par)[:, gi * 512:(gi + 1) * 512])
                    lhs = (lambda kk=kk: silub[:, kk:kk + 1]) if nbf else (lambda kk=kk: siluf[:, kk:kk + 1])
                    deps = [dma_t[kk]] + (list(bank_gate) if kk == 0 else [])
                    mm_t[kk] = E("pe", lambda t, gi=gi, kk=kk, srcf=srcf, lhs=lhs: t.matmul(
                        PB[4 + gi][0:1, 0:512], lhs(), srcf(),
                        start=(kk == 0), stop=(kk == 7)), deps=deps, inc=SPE)
            return mm_t

        mmA = mod_block(0, True, None)
        tr_prev = _Tok(SPE, 0)

        def row_to_cols(src_row_fn, dst, dst0, n, deps):
            nonlocal tr_prev
            outs = []
            for i in range(n):
                tp = E("pe", lambda t, i=i: t.matmul(
                    PB[3][0:128, 0:1], src_row_fn(i), identf[0:1, 0:1],
                    start=True, stop=True),
                    deps=list(deps) + [tr_prev], inc=SPE)
                tr_prev = E("dve", lambda v, i=i: v.tensor_copy(
                    dst[:, dst0 + i:dst0 + i + 1], PB[3][0:128, 0:1]), deps=[tp], inc=SDVE)
                outs.append(tr_prev)
            return outs

        t_saev, t_caev = [], []
        for gi in range(2):
            t_saev.append(E("dve", lambda v, gi=gi: v.tensor_copy(
                afr(X0A + gi * 512, 512), PB[4 + gi][0:1, 0:512]), deps=[mmA[7]], inc=SDVE))
        for gi in range(2):
            t_caev.append(E("dve", lambda v, gi=gi: v.tensor_scalar(
                out=afr(RW1, D)[0:1, gi * 512:(gi + 1) * 512], in0=PB[6 + gi][0:1, 0:512],
                scalar1=1.0, scalar2=None, op0=ALU.add), deps=[mmA[7], t_silub], inc=SDVE))
        t_caplb = E("dve", lambda v: v.tensor_copy(abr(D0, D), afr(RW1, D)),
                    deps=t_caev, inc=SDVE)
        sab_cols = row_to_cols(lambda i: afr(X0A + 0, 1024)[0:1, i * 128:(i + 1) * 128],
                               sab, 0, DC, t_saev + [mmA[7]])
        capl_cols = row_to_cols(lambda i: afr(RW1, D)[0:1, i * 128:(i + 1) * 128],
                                colsA, 0, DC, t_caev)

        mmB = mod_block(3072, True, mmA, bank_gate=t_saev + t_caev)
        t_sbev = []
        for gi in range(2):
            t_sbev.append(E("dve", lambda v, gi=gi: v.tensor_copy(
                afr(X0A + gi * 512, 512), PB[4 + gi][0:1, 0:512]),
                deps=[mmB[7], sab_cols[-1]], inc=SDVE))
        t_cbev = []
        for gi in range(2):
            t_cbev.append(E("dve", lambda v, gi=gi: v.tensor_scalar(
                out=afr(RW1, D)[0:1, gi * 512:(gi + 1) * 512], in0=PB[6 + gi][0:1, 0:512],
                scalar1=1.0, scalar2=None, op0=ALU.add),
                deps=[mmB[7], capl_cols[-1], t_caplb], inc=SDVE))
        sbb_cols = row_to_cols(lambda i: afr(X0A + 0, 1024)[0:1, i * 128:(i + 1) * 128],
                               sbb, 0, DC, t_sbev + [mmB[7]])
        cbpl_cols = row_to_cols(lambda i: afr(RW1, D)[0:1, i * 128:(i + 1) * 128],
                                colsA, 8, DC, t_cbev)

        mmGa = mod_block(2048, False, mmB, bank_gate=t_sbev + t_cbev)
        t_gaev = E("act", lambda s: s.activation(
            out=afr(X0A + 512, D)[0:1, 0:512], in_=PB[4][0:1, 0:512], func=AF.Abs),
            deps=[mmGa[7], sbb_cols[-1], cbpl_cols[-1]], inc=SACT)
        t_gaev2 = E("act", lambda s: s.activation(
            out=afr(X0A + 512, D)[0:1, 512:1024], in_=PB[5][0:1, 0:512], func=AF.Abs),
            deps=[mmGa[7], t_gaev], inc=SACT)
        t_gm = E("dve", lambda v: v.tensor_scalar_max(
            afr(X0A + 512, D), afr(X0A + 512, D), 0.25), deps=[t_gaev2], inc=SDVE)
        t_gs1 = E("dve", lambda v: v.tensor_scalar(
            out=afr(X0A + 1536, D)[0:1, 0:512], in0=PB[4][0:1, 0:512],
            scalar1=0.0, scalar2=None, op0=ALU.is_ge), deps=[mmGa[7]], inc=SDVE)
        t_gs2 = E("dve", lambda v: v.tensor_scalar(
            out=afr(X0A + 1536, D)[0:1, 512:1024], in0=PB[5][0:1, 0:512],
            scalar1=0.0, scalar2=None, op0=ALU.is_ge), deps=[mmGa[7]], inc=SDVE)
        t_gs3 = E("dve", lambda v: v.tensor_scalar(
            out=afr(X0A + 1536, D), in0=afr(X0A + 1536, D),
            scalar1=2.0, scalar2=-1.0, op0=ALU.mult, op1=ALU.add), deps=[t_gs1, t_gs2], inc=SDVE)
        t_ga = E("dve", lambda v: v.tensor_mul(
            afr(X0A + 512, D), afr(X0A + 512, D), afr(X0A + 1536, D)),
            deps=[t_gm, t_gs3], inc=SDVE)
        ga_cols = row_to_cols(lambda i: afr(X0A + 512, D)[0:1, i * 128:(i + 1) * 128],
                              colsA, 16, DC, [t_ga])

        mmGb = mod_block(5120, False, mmGa, bank_gate=[t_ga])
        t_gbev = E("act", lambda s: s.activation(
            out=afr(X0A + 512, D)[0:1, 0:512], in_=PB[4][0:1, 0:512], func=AF.Abs),
            deps=[mmGb[7], ga_cols[-1]], inc=SACT)
        t_gbev2 = E("act", lambda s: s.activation(
            out=afr(X0A + 512, D)[0:1, 512:1024], in_=PB[5][0:1, 0:512], func=AF.Abs),
            deps=[mmGb[7], t_gbev], inc=SACT)
        t_gm = E("dve", lambda v: v.tensor_scalar_max(
            afr(X0A + 512, D), afr(X0A + 512, D), 0.25), deps=[t_gbev2], inc=SDVE)
        t_gs1 = E("dve", lambda v: v.tensor_scalar(
            out=afr(X0A + 1536, D)[0:1, 0:512], in0=PB[4][0:1, 0:512],
            scalar1=0.0, scalar2=None, op0=ALU.is_ge), deps=[mmGb[7]], inc=SDVE)
        t_gs2 = E("dve", lambda v: v.tensor_scalar(
            out=afr(X0A + 1536, D)[0:1, 512:1024], in0=PB[5][0:1, 0:512],
            scalar1=0.0, scalar2=None, op0=ALU.is_ge), deps=[mmGb[7]], inc=SDVE)
        t_gs3 = E("dve", lambda v: v.tensor_scalar(
            out=afr(X0A + 1536, D), in0=afr(X0A + 1536, D),
            scalar1=2.0, scalar2=-1.0, op0=ALU.mult, op1=ALU.add), deps=[t_gs1, t_gs2], inc=SDVE)
        t_gb = E("dve", lambda v: v.tensor_mul(
            afr(X0A + 512, D), afr(X0A + 512, D), afr(X0A + 1536, D)),
            deps=[t_gm, t_gs3], inc=SDVE)
        gb_cols = row_to_cols(lambda i: afr(X0A + 512, D)[0:1, i * 128:(i + 1) * 128],
                              colsA, 24, DC, [t_gb])
        t_gbb = E("dve", lambda v: v.tensor_copy(abr(D0 + 1024, D), afr(X0A + 512, D)),
                  deps=[t_gb], inc=SDVE)

        if upto == "mod":
            dbg_exit([
                (0, lambda: colsA[:], [capl_cols[-1], cbpl_cols[-1], ga_cols[-1], gb_cols[-1]], False),
                (128, lambda: sab[:], [sab_cols[-1]], True),
                (256, lambda: sbb[:], [sbb_cols[-1]], True),
            ])
            return nc

        if upto == "mod":
            if ops_limit is not None:
                del sch.ops[ops_limit:]
            dbg_exit([
                (0, lambda: colsA[:], [capl_cols[-1], cbpl_cols[-1], ga_cols[-1], gb_cols[-1]] if ops_limit is None else [], False),
                (128, lambda: sab[:], [sab_cols[-1]] if ops_limit is None else [], True),
                (256, lambda: sbb[:], [sbb_cols[-1]] if ops_limit is None else [], True),
                (384, lambda: AF32[0:1, RW1:RW1 + D], [capl_cols[-1]] if ops_limit is None else [], False),
                (385, lambda: AF32[0:1, X0A:X0A + D], [sab_cols[-1]] if ops_limit is None else [], False),
            ])
            return nc

        t_mm = None
        for kk in range(DC):
            t_mm = E("pe", lambda t, kk=kk: t.matmul(
                PB[6][0:128, 0:1], ws[:, kk * 128:(kk + 1) * 128], sab[:, kk:kk + 1],
                start=(kk == 0), stop=(kk == 7)),
                deps=[t_ws, sab_cols[kk], mmGb[7]] + (t_cbev if kk == 0 else []), inc=SPE)
        t_cs = E("dve", lambda v: v.scalar_tensor_tensor(
            out=csvec[:, 0:1], in0=PB[6][0:128, 0:1], scalar=1.0, in1=csvec[:, 4:5],
            op0=ALU.mult, op1=ALU.add), deps=[t_mm, t_bs], inc=SDVE)
        t_mm = None
        for kk in range(DC):
            t_mm = E("pe", lambda t, kk=kk: t.matmul(
                PB[6][0:128, 1:2], wiTs[:, kk * 128:(kk + 1) * 128], sab[:, kk:kk + 1],
                start=(kk == 0), stop=(kk == 7)),
                deps=[t_wiT, sab_cols[kk]] + ([t_cs] if kk == 0 else []), inc=SPE)
        t_cu = E("dve", lambda v: v.scalar_tensor_tensor(
            out=csvec[:, 1:2], in0=PB[6][0:128, 1:2], scalar=1.0, in1=csvec[:, 3:4],
            op0=ALU.mult, op1=ALU.add), deps=[t_mm, t_bi], inc=SDVE)
        t_csb = E("dve", lambda v: v.tensor_copy(csb[:], csvec[:, 0:2]), deps=[t_cs, t_cu],
                  inc=SDVE)
        t_mm = E("pe", lambda t: t.matmul(PB[6][0:128, 2:3], at[:], csb[:, 0:1],
                                          start=True, stop=False), deps=[t_at, t_csb], inc=SPE)
        t_mm = E("pe", lambda t: t.matmul(PB[6][0:128, 2:3], btS[:], csb[:, 1:2],
                                          start=False, stop=True), deps=[t_bt, t_mm], inc=SPE)
        t_bz = E("dve", lambda v: v.tensor_copy(bzc[:], PB[6][0:128, 2:3]), deps=[t_mm], inc=SDVE)
        t_mm = E("pe", lambda t: t.matmul(PB[6][0:128, 3:4], ct[:], csb[:, 0:1],
                                          start=True, stop=True), deps=[t_ct, t_csb, t_bz], inc=SPE)
        t_ccs = E("dve", lambda v: v.tensor_copy(csvec[:, 2:3], PB[6][0:128, 3:4]),
                  deps=[t_mm], inc=SDVE)
        t_mm = E("pe", lambda t: t.matmul(PB[6][0:128, 4:5], at[:], csb[:, 0:1],
                                          start=True, stop=True), deps=[t_at, t_csb, t_ccs], inc=SPE)
        t_acs = E("dve", lambda v: v.tensor_copy(csvec[:, 5:6], PB[6][0:128, 4:5]),
                  deps=[t_mm], inc=SDVE)

        crr_last = t_acs
        for nm, wt, wtok, dst in (("crr", wr, t_wr, crr), ("crk", wk, t_wk, crk)):
            for gg in range(2):
                t_mm = None
                for kk in range(DC):
                    t_mm = E("pe", lambda t, kk=kk, gg=gg, wt=wt: t.matmul(
                        PB[6][0:1, 0:512], sbb[:, kk:kk + 1],
                        wt[:, kk * D + gg * 512:kk * D + (gg + 1) * 512],
                        start=(kk == 0), stop=(kk == 7)),
                        deps=[wtok, sbb_cols[kk], crr_last], inc=SPE)
                crr_last = E("dve", lambda v, gg=gg: v.tensor_copy(
                    afr(X0A + 2560, 1024)[0:1, gg * 512:(gg + 1) * 512], PB[6][0:1, 0:512]),
                    deps=[t_mm], inc=SDVE)
            cols = row_to_cols(lambda i: afr(X0A + 2560, 1024)[0:1, i * 128:(i + 1) * 128],
                               dst, 0, DC, [crr_last])
            crr_last = cols[-1]
        t_crcols = crr_last

        t_wsf = None
        for m in range(DC):
            t_wsf = E("act", lambda s, m=m: s.activation(
                out=ws[:, m * 128:(m + 1) * 128], in_=ws[:, m * 128:(m + 1) * 128],
                func=AF.Identity, bias=0.0, scale=colsA[:, m:m + 1]),
                deps=[t_cs, capl_cols[m], t_ws], inc=SACT)
        bc_t = []
        for gg in range(2):
            tm = E("pe", lambda t, gg=gg: t.matmul(
                PB[6][0:128, 0:512], onesrow[0:1, 0:128], abr(D0 + gg * 512, 512),
                start=True, stop=True),
                deps=[t_caplb, t_crcols] + (bc_t[:1] if gg else []), inc=SPE)
            bc_t.append(E("dve", lambda v, gg=gg: v.tensor_copy(
                ABF[:, G0 + gg * 512:G0 + (gg + 1) * 512], PB[6][0:128, 0:512]),
                deps=[tm], inc=SDVE))
        t_wia = E("dve", lambda v: v.tensor_mul(wia[:], wiNs[:], ab(G0, 1024)),
                  deps=[bc_t[1], t_win], inc=SDVE)
        t_wibl = None
        for m in range(DC):
            tm = E("pe", lambda t, m=m: t.matmul(
                PB[6][0:128, 0:128], wia[:, m * 128:(m + 1) * 128], btS[:],
                start=True, stop=True),
                deps=[t_wia, t_bt] + ([t_wibl] if t_wibl else []), inc=SPE)
            t_wibl = E("dve", lambda v, m=m: v.tensor_copy(
                wib[:, m * 128:(m + 1) * 128], PB[6][0:128, 0:128]), deps=[tm], inc=SDVE)
        t_wib = t_wibl
        t_wrf = t_wkf = None
        for m in range(DC):
            t_wrf = E("act", lambda s, m=m: s.activation(
                out=wr[:, m * D:(m + 1) * D], in_=wr[:, m * D:(m + 1) * D],
                func=AF.Identity, bias=0.0, scale=colsA[:, 8 + m:9 + m]),
                deps=[t_crcols, cbpl_cols[m], t_wr], inc=SACT)
            t_wkf = E("act", lambda s, m=m: s.activation(
                out=wk[:, m * D:(m + 1) * D], in_=wk[:, m * D:(m + 1) * D],
                func=AF.Identity, bias=0.0, scale=colsA[:, 8 + m:9 + m]),
                deps=[t_crcols, cbpl_cols[m], t_wk], inc=SACT)
        bc_t = []
        for gg in range(2):
            tm = E("pe", lambda t, gg=gg: t.matmul(
                PB[6][0:128, 0:512], onesrow[0:1, 0:128], abr(D0 + 1024 + gg * 512, 512),
                start=True, stop=True),
                deps=[t_gbb, t_wib] + (bc_t[:1] if gg else []), inc=SPE)
            bc_t.append(E("dve", lambda v, gg=gg: v.tensor_copy(
                ABF[:, G0 + gg * 512:G0 + (gg + 1) * 512], PB[6][0:128, 0:512]),
                deps=[tm, t_wia], inc=SDVE))
        t_wvf = None
        for m in range(DC):
            t_wvf = E("dve", lambda v, m=m: v.tensor_mul(
                wv[:, m * D:(m + 1) * D], wv[:, m * D:(m + 1) * D], ab(G0, 1024)),
                deps=[bc_t[1], t_wv], inc=SDVE)

        t_pad = E("dve", lambda v: v.memset(ab(F0, 1), 0.0),
                  deps=[mmB[7], mmGb[7]], inc=SDVE)

        t_hcols = None
        for m in range(DC):
            t_r = E("dve", lambda v, m=m: v.reciprocal(csvec[:, 6:7], colsA[:, 8 + m:9 + m]),
                    deps=[cbpl_cols[m], t_cs, t_cu] + ([t_hcols] if t_hcols else []), inc=SDVE)
            t_m = E("dve", lambda v, m=m: v.tensor_mul(csvec[:, 7:8], sbb[:, m:m + 1],
                    csvec[:, 6:7]), deps=[t_r, sbb_cols[m]], inc=SDVE)
            t_hcols = E("dve", lambda v, m=m: v.tensor_scalar(
                out=hcols[:, m:m + 1], in0=csvec[:, 7:8], scalar1=-1.0, scalar2=None,
                op0=ALU.mult), deps=[t_m], inc=SDVE)

        if upto == "pro":
            dbg_exit([
                (0, lambda: colsA[:], [t_cs, t_cu, t_ccs, t_acs, t_crcols, t_wvf, t_wsf if t_wsf else None, t_wkf], False),
                (128, lambda: csvec[:], [t_acs, t_bz], False),
                (256, lambda: crr[:], [t_crcols], False),
                (384, lambda: crk[:], [t_crcols], False),
                (512, lambda: bzc[:], [t_bz], False),
                (640, lambda: sab[:], [t_crcols], True),
                (768, lambda: sbb[:], [t_crcols], True),
                (896, lambda: wib[:, 0:256], [t_wib], True),
            ])
            return nc

        bfree = {0: mmGb[7], 1: mmGb[7], 2: mmGb[7], 3: tr_prev, 4: t_gb, 5: t_gb,
                 6: bc_t[1], 7: t_cbev[1]}
        xbufb = lambda par: ABF[:, A0 + par * 4096:A0 + (par + 1) * 4096]
        xsqP = lambda: ABF[:, B0:B0 + 4096]
        xcb = lambda: ABF[:, C0:C0 + 4096]
        sbc_v = ABF[:, G0:G0 + 512]
        mbc_v = ABF[:, G0 + 512:G0 + 1024]
        rowsS = lambda i: AF32[0:1, X1A + 2048 + i * CH:X1A + 2048 + (i + 1) * CH]
        rowbf = lambda i: ABF[0:1, D0 + 2816 + i * CH:D0 + 2816 + (i + 1) * CH]
        spb_c = lambda c0, w: ABF[:, F0 + c0:F0 + c0 + w]
        wsb_c = lambda c0, w: ABF[:, H0 + c0:H0 + c0 + w]
        thb_c = lambda c0, w: ABF[:, E0 + c0:E0 + c0 + w]

        t_xdma = [None] * NCH
        t_wev = [None] * NCH
        t_spev = [None] * NCH
        xb_last = {0: t_silub, 1: t_silub}
        rows_free = t_caplb
        xsq_free = t_gbb
        xc_free = t_gbb
        for c in range(NCH):
            par = c % 2
            t0 = c * CH
            t_xdma[c] = E("gp", lambda g, t0=t0, par=par: g.dma_start(
                out=xbufb(par).rearrange("p (r t) -> p r t", r=8), in_=dram["xT"][:, t0:t0 + CH].rearrange("(r p) t -> p r t", p=128)),
                deps=[xb_last[par]], stream=f"x_{par}")
            t_sq = [None] * DC
            for m in range(DC):
                t_sq[m] = E("gp", lambda g, m=m, par=par: g.tensor_mul(
                    xsqP()[:, m * CH:(m + 1) * CH], xbufb(par)[:, m * CH:(m + 1) * CH],
                    xbufb(par)[:, m * CH:(m + 1) * CH]),
                    deps=[t_xdma[c], xsq_free], inc=SGPC)
            t_sum = t_ssq = None
            for m in range(DC):
                t_sum = E("pe", lambda t, m=m, par=par: t.matmul(
                    PB[0][0:1, 0:CH], onescol[:], xbufb(par)[:, m * CH:(m + 1) * CH],
                    start=(m == 0), stop=(m == 7)), deps=[t_xdma[c], bfree[0], rows_free], inc=SPE)
            for m in range(DC):
                t_ssq = E("pe", lambda t, m=m: t.matmul(
                    PB[1][0:1, 0:CH], onescol[:], xsqP()[:, m * CH:(m + 1) * CH],
                    start=(m == 0), stop=(m == 7)), deps=[t_sq[m], bfree[1], rows_free], inc=SPE)
            t_mrbf = E("dve", lambda v: v.tensor_scalar(
                out=rowbf(0), in0=PB[0][0:1, 0:CH], scalar1=1.0 / D, scalar2=None,
                op0=ALU.mult), deps=[t_sum], inc=SDVE)
            t_mf = E("dve", lambda v: v.tensor_scalar(
                out=rowsS(0), in0=PB[0][0:1, 0:CH], scalar1=1.0 / D, scalar2=None,
                op0=ALU.mult), deps=[t_sum], inc=SDVE)
            t_msq = E("dve", lambda v: v.tensor_mul(rowsS(1), rowsS(0), rowsS(0)),
                      deps=[t_mf], inc=SDVE)
            t_var = E("dve", lambda v: v.scalar_tensor_tensor(
                out=rowsS(2), in0=PB[1][0:1, 0:CH], scalar=1.0 / D, in1=rowsS(1),
                op0=ALU.mult, op1=ALU.subtract), deps=[t_ssq, t_msq], inc=SDVE)
            bfree[0] = t_mf
            bfree[1] = t_var
            t_ln = E("act", lambda s: s.activation(
                out=rowsS(3), in_=rowsS(2), func=AF.Ln, bias=epsc[0:1, 0:1], scale=1.0),
                deps=[t_var, t_wsf], inc=SACT)
            t_rstd = E("act", lambda s: s.activation(
                out=rowsS(2), in_=rowsS(3), func=AF.Exp, bias=0.0, scale=-0.5),
                deps=[t_ln], inc=SACT)
            t_rstdb = E("dve", lambda v: v.tensor_copy(rowbf(1), rowsS(2)),
                        deps=[t_rstd], inc=SDVE)
            rows_free = t_rstdb
            t_sbcm = E("pe", lambda t: t.matmul(
                PB[2][0:128, 0:CH], onesrow[0:1, 0:128], rowbf(1), start=True, stop=True),
                deps=[t_rstdb, bfree[2]], inc=SPE)
            t_mbcm = E("pe", lambda t: t.matmul(
                PB[3][0:128, 0:CH], onesrow[0:1, 0:128], rowbf(0), start=True, stop=True),
                deps=[t_mrbf, bfree[3]], inc=SPE)
            t_sbce = E("dve", lambda v: v.tensor_copy(sbc_v, PB[2][0:128, 0:CH]),
                       deps=[t_sbcm], inc=SDVE)
            t_mbce = E("dve", lambda v: v.tensor_copy(mbc_v, PB[3][0:128, 0:CH]),
                       deps=[t_mbcm], inc=SDVE)
            bfree[2] = t_sbce
            bfree[3] = t_mbce
            t_xc = [None] * DC
            for m in range(DC):
                t_xc[m] = E("dve", lambda v, m=m, par=par: v.tensor_sub(
                    xcb()[:, m * CH:(m + 1) * CH], xbufb(par)[:, m * CH:(m + 1) * CH], mbc_v),
                    deps=[t_mbce, t_xdma[c], xc_free], inc=SDVE)
            xb_last[par] = t_xc[DC - 1]
            t_mm = None
            for m in range(DC):
                t_mm = E("pe", lambda t, m=m: t.matmul(
                    PB[5][0:128, 0:CH], ws[:, m * 128:(m + 1) * 128], xcb()[:, m * CH:(m + 1) * CH],
                    start=(m == 0), stop=(m == 7)), deps=[t_xc[m], t_wsf, bfree[5]], inc=SPE)
            t_spev[c] = E("dve", lambda v, t0=t0: v.tensor_mul(
                spb_c(1 + t0, CH), PB[5][0:128, 0:CH], sbc_v), deps=[t_mm, t_sbce], inc=SDVE)
            bfree[5] = t_spev[c]
            t_spa = E("pe", lambda t, t0=t0: t.matmul(
                PB[4][0:128, 0:CH], at[:], spb_c(t0, CH), start=True, stop=True),
                deps=[t_spev[c], t_at, bfree[4]] + ([t_pad] if c == 0 else []), inc=SPE)
            t_mm = None
            for m in range(DC):
                t_mm = E("pe", lambda t, m=m: t.matmul(
                    PB[6][0:128, 0:CH], wib[:, m * 128:(m + 1) * 128], xcb()[:, m * CH:(m + 1) * CH],
                    start=(m == 0), stop=(m == 7)), deps=[t_xc[m], t_wib, bfree[6]], inc=SPE)
            t_w1 = E("dve", lambda v: v.tensor_mul(af(X1A, CH)[:, 0:CH], PB[6][0:128, 0:CH], sbc_v),
                     deps=[t_mm, t_sbce], inc=SDVE)
            t_wev[c] = E("dve", lambda v, t0=t0: v.tensor_add(
                wsb_c(W + t0, CH), af(X1A, CH)[:, 0:CH], PB[4][0:128, 0:CH]),
                deps=[t_w1, t_spa, t_init], inc=SDVE)
            if c == 0:
                t_wev[0] = E("dve", lambda v: v.tensor_sub(
                    wsb_c(W, 1), wsb_c(W, 1), csvec[:, 5:6]),
                    deps=[t_wev[0], t_acs], inc=SDVE)
            bfree[4] = t_wev[c]
            bfree[6] = t_wev[c]
            xsq_free = t_ssq
            xc_free = t_mm

        if upto == "pre":
            dbg_exit([
                (0, lambda: ABF[:, H0 + W:H0 + W + T], [t_wev[NCH - 1]], True),
                (128, lambda: ABF[:, F0 + 1:F0 + 1 + T], [t_spev[NCH - 1]], True),
            ])
            return nc

        all_w = t_wev[NCH - 1]
        def slot(i):
            return PB[i % 8][0:128, 0:R]
        ring_free0 = [bfree[k] for k in range(8)]
        t_fill = [None] * ROUNDS
        t_cmm = [None] * ROUNDS
        t_tanh = [None] * ROUNDS

        def emit_fill(i):
            deps = [all_w, t_identb]
            if i >= 8:
                deps.append(t_tanh[i - 8])
            else:
                deps.append(ring_free0[i % 8])
            if R > 1:
                srcf = lambda i=i: ABF[:, H0 + i:H0 + i + (R - 1) * S + 1:S]
            else:
                srcf = lambda i=i: ABF[:, H0 + W + i:H0 + W + i + 1]
            t_fill[i] = E("pe", lambda t, srcf=srcf, i=i: t.matmul(
                slot(i), identb[:], srcf(), start=True, stop=False), deps=deps, inc=SPE)

        for i in range(min(8, ROUNDS)):
            emit_fill(i)
        for i in range(ROUNDS):
            deps = [t_fill[i], t_bz, t_wvf]
            if i > 0:
                deps.append(t_tanh[i - 1])
            if W > 0 and i == W:
                t_fix = E("dve", lambda v, i=i: v.memset(thb_c(i * R, 1), 0.0),
                          deps=[t_tanh[i - 1]], inc=SDVE)
                deps.append(t_fix)
            t_cmm[i] = E("pe", lambda t, i=i: t.matmul(
                slot(i), at[:], thb_c(i * R, R), start=False, stop=True), deps=deps, inc=SPE)
            t_tanh[i] = E("act", lambda s, i=i: s.activation(
                out=thb_c((i + 1) * R, R), in_=slot(i), func=AF.Tanh, bias=bzc[:], scale=1.0),
                deps=[t_cmm[i]], inc=SACT)
            if i + 8 < ROUNDS:
                emit_fill(i + 8)
        t_scan = t_tanh[ROUNDS - 1]

        if upto == "scan":
            dbg_exit([
                (0, lambda: ABF[:, E0 + (W + 1) * R:E0 + (W + 1) * R + min((ROUNDS - W) * R, T)], [t_scan], True),
            ])
            return nc

        x0b = lambda par: AF32[:, X0A + par * 2048:X0A + (par + 1) * 2048]
        x1bv = lambda par: AF32[:, X1A + par * 2048:X1A + (par + 1) * 2048]
        kscrv = lambda p: AF32[:, WSB + p * CB:WSB + (p + 1) * CB]
        vscrv = lambda p: AF32[:, WSB + 512 + p * CB:WSB + 512 + (p + 1) * CB]
        m2row = lambda par: AF32[0:1, WSB + 1024 + par * CB:WSB + 1024 + (par + 1) * CB]
        rowsA = lambda par, i: AF32[0:1, RW1 + par * 512 + i * CB:RW1 + par * 512 + (i + 1) * CB]
        xbfc = lambda par: ABF[:, A0 + par * 2048:A0 + (par + 1) * 2048]
        xsqv = lambda par: ABF[:, A0 + 4096 + par * 2048:A0 + 4096 + (par + 1) * 2048]
        rbufv = ABF[:, B0:B0 + 2048]
        kabv = ABF[:, B0 + 2048:B0 + 4096]
        xrbv = lambda par: ABF[:, C0 + par * 2048:C0 + (par + 1) * 2048]
        xn2v = lambda par, m: ABF[:, D0 + par * 2056 + m * 257:D0 + par * 2056 + m * 257 + 257]
        xkbv = lambda par: ABF[:, H0 + par * 2048:H0 + (par + 1) * 2048]
        s2bcv = ABF[:, G0:G0 + CB]
        m2bcv = ABF[:, G0 + CB:G0 + 2 * CB]
        rstd2r = lambda par: ABF[0:1, G0 + 512 + par * CB:G0 + 512 + (par + 1) * CB]
        ycbv = lambda par: ABF[:, G0 + 1024 + par * CB:G0 + 1024 + (par + 1) * CB]
        epiv = lambda p: AF32[:, WSB + 1536 + p * CB:WSB + 1536 + (p + 1) * CB]

        bk = {i: t_scan for i in range(8)}
        bkh = {}
        t_x0 = [None] * NCB
        t_yc = [None] * NCB
        out2_last = [None] * NCB
        t_x1m = [[None] * DC for _ in range(NCB)]
        t_cast = [[None] * DC for _ in range(NCB)]
        t_sqx = [[None] * DC for _ in range(NCB)]
        t_sumT = [None] * NCB
        t_ssqT = [None] * NCB
        t_m2r = [None] * NCB
        t_varT = [None] * NCB
        t_rstd = [None] * NCB
        t_s2m = [None] * NCB
        t_m2m = [None] * NCB
        t_s2e = [None] * NCB
        t_m2e = [None] * NCB
        t_sub = [[None] * DC for _ in range(NCB)]
        t_xn = [[None] * DC for _ in range(NCB)]
        t_halo = [[None] * DC for _ in range(NCB)]
        t_dm = [[None] * DC for _ in range(NCB)]
        t_xr = [[None] * DC for _ in range(NCB)]
        t_xk = [[None] * DC for _ in range(NCB)]
        t_sig = [[None] * DC for _ in range(NCB)]
        t_rel = [[None] * DC for _ in range(NCB)]
        t_ksq = [[None] * DC for _ in range(NCB)]
        t_vm = [[None] * DC for _ in range(NCB)]
        t_x2 = [[None] * DC for _ in range(NCB)]
        t_store = [None] * NCB
        r_done = [None] * NCB
        k_done = [None] * NCB
        v_done = [None] * NCB
        pre_gp = t_sq[DC - 1]

        for c in range(NCB + 1):
            par = c % 2
            t0 = c * CB
            if c < NCB:
                dep0 = [gb_cols[-1], t_crcols] if c < 2 else [t_x1m[c - 2][DC - 1]]
                t_x0[c] = E("sp", lambda s, t0=t0, par=par: s.dma_start(
                    out=x0b(par).rearrange("p (r t) -> p r t", r=8),
                    in_=dram["xT"][:, t0:t0 + CB].rearrange("(r p) t -> p r t", p=128)),
                    deps=dep0, stream=f"x0_{par}")
                t_mm = None
                for rr in range(CB // S):
                    base = (W + 1) * R + (t0 // S + rr)
                    t_mm = E("pe", lambda t, base=base, rr=rr: t.matmul(
                        PB[2][0:128, rr * S:rr * S + S],
                        ct[:], ABF[:, E0 + base:E0 + base + (S - 1) * R + 1:R],
                        start=(rr == 0), stop=False),
                        deps=([t_scan, t_ct, bk[2]] if rr == 0 else [t_mm]), inc=SPE)
                t_mm = E("pe", lambda t, t0=t0: t.matmul(
                    PB[2][0:128, 0:CB], ct[:], spb_c(1 + t0, CB), start=False, stop=True),
                    deps=[t_mm], inc=SPE)
                ycw = [out2_last[c - 2]] if c >= 2 else []
                t_yc[c] = E("act", lambda s, par=par: s.activation(
                    out=ycbv(par), in_=PB[2][0:128, 0:CB], func=AF.Identity,
                    bias=csvec[:, 2:3], scale=1.0), deps=[t_mm, t_ccs] + ycw, inc=SACT)
                bk[2] = t_yc[c]
                for m in range(DC):
                    bb = 3 - (m % 2)
                    ep = m % 2
                    dd = [t_yc[c], t_wo, t_bobf, bkh.get(bb, bk[bb])]
                    t_mm = E("pe", lambda t, m=m, bb=bb, par=par: t.matmul(
                        PB[bb][0:128, 0:CB], wo[:, m * 128:(m + 1) * 128], ycbv(par),
                        start=True, stop=False), deps=dd, inc=SPE)
                    t_mm = E("pe", lambda t, m=m, bb=bb: t.matmul(
                        PB[bb][0:128, 0:CB], borowS[0:1, m * 128:(m + 1) * 128],
                        onesrow[0:1, 0:CB], start=False, stop=True), deps=[t_mm], inc=SPE)
                    if m == DC - 1:
                        out2_last[c] = t_mm
                    edeps = [t_mm, ga_cols[m]]
                    if m >= 2:
                        edeps.append(t_x1m[c][m - 2])
                    elif c >= 1:
                        edeps.append(t_x1m[c - 1][DC - 2 + ep])
                    t_sc = E("act", lambda s, m=m, bb=bb, ep=ep: s.activation(
                        out=epiv(ep), in_=PB[bb][0:128, 0:CB], func=AF.Identity,
                        bias=0.0, scale=colsA[:, 16 + m:17 + m]), deps=edeps, inc=SACT)
                    bkh[bb] = t_sc
                    wdeps = [t_sc, t_x0[c]]
                    if c >= 2:
                        wdeps.append(t_store[c - 2])
                    elif par == 0:
                        wdeps.append(t_w1)
                    else:
                        wdeps.append(rows_free)
                    t_x1m[c][m] = E("dve", lambda v, m=m, ep=ep, par=par: v.tensor_add(
                        x1bv(par)[:, m * CB:(m + 1) * CB], epiv(ep),
                        x0b(par)[:, m * CB:(m + 1) * CB]), deps=wdeps, inc=SDVE)
                    cdeps = [t_x1m[c][m], pre_gp]
                    if c >= 2:
                        cdeps += [t_sumT[c - 2], t_sub[c - 2][m]]
                    t_cast[c][m] = E("act", lambda s, m=m, par=par: s.activation(
                        out=xbfc(par)[:, m * CB:(m + 1) * CB],
                        in_=x1bv(par)[:, m * CB:(m + 1) * CB], func=AF.Identity,
                        bias=0.0, scale=1.0), deps=cdeps, inc=SACT)
                    sdeps = [t_cast[c][m]]
                    if c >= 2:
                        sdeps.append(t_ssqT[c - 2])
                    t_sqx[c][m] = E("gp", lambda g, m=m, par=par: g.tensor_mul(
                        xsqv(par)[:, m * CB:(m + 1) * CB],
                        xbfc(par)[:, m * CB:(m + 1) * CB],
                        xbfc(par)[:, m * CB:(m + 1) * CB]), deps=sdeps, inc=SGPC)
                bk[3] = bkh[3]
                bk[2] = bkh[2]
            if c >= 1:
                pr = (c - 1) % 2
                for m in range(DC):
                    bb = 4 + (m % 2)
                    t_mm = None
                    for kk in range(DC):
                        dd = [t_mm] if t_mm else [bk[bb], t_wrf, t_xr[c - 1][DC - 1]]
                        t_mm = E("pe", lambda t, m=m, kk=kk, bb=bb, pr=pr: t.matmul(
                            PB[bb][0:128, 0:CB], wr[:, kk * D + m * 128:kk * D + (m + 1) * 128],
                            xrbv(pr)[:, kk * CB:(kk + 1) * CB], start=(kk == 0), stop=(kk == 7)),
                            deps=dd, inc=SPE)
                    if m == DC - 1:
                        r_done[c - 1] = t_mm
                    rw = [t_vm[c - 2][m]] if c >= 2 else []
                    t_sig[c - 1][m] = E("act", lambda s, m=m, bb=bb: s.activation(
                        out=rbufv[:, m * CB:(m + 1) * CB], in_=PB[bb][0:128, 0:CB],
                        func=AF.Sigmoid, bias=crr[:, m:m + 1], scale=1.0),
                        deps=[t_mm, t_crcols] + rw, inc=SACT)
                    bk[bb] = t_sig[c - 1][m]
            if c < NCB:
                t_mm = None
                for m in range(DC):
                    dd = [t_cast[c][m]] + ([t_mm] if t_mm else [bk[0]])
                    t_mm = E("pe", lambda t, m=m, par=par: t.matmul(
                        PB[0][0:1, 0:CB], onescol[:], xbfc(par)[:, m * CB:(m + 1) * CB],
                        start=(m == 0), stop=(m == 7)), deps=dd, inc=SPE)
                t_sumT[c] = t_mm
                t_mm = None
                for m in range(DC):
                    dd = [t_sqx[c][m]] + ([t_mm] if t_mm else [bk[1]])
                    t_mm = E("pe", lambda t, m=m, par=par: t.matmul(
                        PB[1][0:1, 0:CB], onescol[:], xsqv(par)[:, m * CB:(m + 1) * CB],
                        start=(m == 0), stop=(m == 7)), deps=dd, inc=SPE)
                t_ssqT[c] = t_mm
                mdeps = [t_sumT[c]]
                if c >= 2:
                    mdeps.append(t_m2m[c - 2])
                t_m2r[c] = E("dve", lambda v, par=par: v.tensor_scalar(
                    out=m2row(par), in0=PB[0][0:1, 0:CB], scalar1=1.0 / D, scalar2=None,
                    op0=ALU.mult), deps=mdeps, inc=SDVE)
                qdeps = [t_m2r[c]]
                qdeps.append(t_rstd[c - 2] if c >= 2 else cbpl_cols[-1])
                t_msq = E("dve", lambda v, par=par: v.tensor_mul(
                    rowsA(par, 0), m2row(par), m2row(par)), deps=qdeps, inc=SDVE)
                t_varT[c] = E("dve", lambda v, par=par: v.scalar_tensor_tensor(
                    out=rowsA(par, 1), in0=PB[1][0:1, 0:CB], scalar=1.0 / D,
                    in1=rowsA(par, 0), op0=ALU.mult, op1=ALU.subtract),
                    deps=[t_ssqT[c], t_msq], inc=SDVE)
                t_ln = E("act", lambda s, par=par: s.activation(
                    out=rowsA(par, 0), in_=rowsA(par, 1), func=AF.Ln,
                    bias=epsc[0:1, 0:1], scale=1.0), deps=[t_varT[c], t_msq], inc=SACT)
                rw = [t_s2m[c - 2]] if c >= 2 else []
                t_rstd[c] = E("act", lambda s, par=par: s.activation(
                    out=rstd2r(par), in_=rowsA(par, 0), func=AF.Exp, bias=0.0,
                    scale=-0.5), deps=[t_ln] + rw, inc=SACT)
                t_s2m[c] = E("pe", lambda t, par=par: t.matmul(
                    PB[6][0:128, 0:CB], onesrow[0:1, 0:128], rstd2r(par),
                    start=True, stop=True), deps=[t_rstd[c], bk[6]], inc=SPE)
                s2r = [t_xn[c - 1][DC - 1]] if c >= 1 else []
                t_s2e[c] = E("dve", lambda v: v.tensor_copy(s2bcv, PB[6][0:128, 0:CB]),
                             deps=[t_s2m[c]] + s2r, inc=SDVE)
                t_m2m[c] = E("pe", lambda t, par=par: t.matmul(
                    PB[7][0:128, 0:CB], onesrowf[0:1, 0:128], m2row(par),
                    start=True, stop=True), deps=[t_m2r[c], bk[7]], inc=SPE)
                m2rd = [t_sub[c - 1][DC - 1]] if c >= 1 else []
                t_m2e[c] = E("dve", lambda v: v.tensor_copy(m2bcv, PB[7][0:128, 0:CB]),
                             deps=[t_m2m[c]] + m2rd, inc=SDVE)
                bk[6] = t_s2e[c]
                bk[7] = t_m2e[c]
            if c < NCB:
                for m in range(DC):
                    if c == 0:
                        t_halo[c][m] = E("dve", lambda v, m=m, par=par: v.tensor_copy(
                            xn2v(par, m)[:, 0:1], hcols[:, m:m + 1]),
                            deps=[t_hcols, t_wvf, pre_gp], inc=SDVE)
                    else:
                        hdeps = [t_xn[c - 1][m]]
                        if c >= 2:
                            hdeps.append(t_dm[c - 2][m])
                        t_halo[c][m] = E("dve", lambda v, m=m, par=par: v.tensor_copy(
                            xn2v(par, m)[:, 0:1], xn2v(1 - par, m)[:, CB:CB + 1]),
                            deps=hdeps, inc=SDVE)
                for m in range(DC):
                    sdeps = [t_m2e[c], t_cast[c][m]]
                    if c == 0:
                        sdeps.append(t_wvf)
                    if c >= 1:
                        sdeps.append(t_halo[c - 1][m])
                    if c >= 2:
                        sdeps.append(t_xr[c - 2][m])
                    t_sub[c][m] = E("gp", lambda g, m=m, par=par: g.tensor_sub(
                        xn2v(par, m)[:, 1:1 + CB], xbfc(par)[:, m * CB:(m + 1) * CB],
                        m2bcv), deps=sdeps, inc=SGPC)
                for m in range(DC):
                    t_xn[c][m] = E("dve", lambda v, m=m, par=par: v.tensor_mul(
                        xn2v(par, m)[:, 1:1 + CB], xn2v(par, m)[:, 1:1 + CB], s2bcv),
                        deps=[t_sub[c][m], t_s2e[c]], inc=SDVE)
                for m in range(DC):
                    ddeps = [t_xn[c][m], t_halo[c][m]]
                    if c >= 2:
                        ddeps.append(r_done[c - 2])
                    t_dm[c][m] = E("gp", lambda g, m=m, par=par: g.tensor_sub(
                        xrbv(par)[:, m * CB:(m + 1) * CB], xn2v(par, m)[:, 1:1 + CB],
                        xn2v(par, m)[:, 0:CB]), deps=ddeps, inc=SGPC)
                for m in range(DC):
                    xkw = [k_done[c - 2]] if c >= 2 else []
                    t_xk[c][m] = E("dve", lambda v, m=m, par=par: v.scalar_tensor_tensor(
                        out=xkbv(par)[:, m * CB:(m + 1) * CB],
                        in0=xrbv(par)[:, m * CB:(m + 1) * CB],
                        scalar=mukS[:, m:m + 1], in1=xn2v(par, m)[:, 0:CB],
                        op0=ALU.mult, op1=ALU.add), deps=[t_dm[c][m], t_muk] + xkw, inc=SDVE)
                    t_xr[c][m] = E("dve", lambda v, m=m, par=par: v.scalar_tensor_tensor(
                        out=xrbv(par)[:, m * CB:(m + 1) * CB],
                        in0=xrbv(par)[:, m * CB:(m + 1) * CB],
                        scalar=murS[:, m:m + 1], in1=xn2v(par, m)[:, 0:CB],
                        op0=ALU.mult, op1=ALU.add), deps=[t_xk[c][m], t_mur], inc=SDVE)
            if c >= 1:
                pr = (c - 1) % 2
                for m in range(DC):
                    bb = 6 + (m % 2)
                    kp = m % 2
                    t_mm = None
                    for kk in range(DC):
                        dd = [t_mm] if t_mm else [bk[bb], t_wkf, t_xk[c - 1][DC - 1]]
                        t_mm = E("pe", lambda t, m=m, kk=kk, bb=bb, pr=pr: t.matmul(
                            PB[bb][0:128, 0:CB], wk[:, kk * D + m * 128:kk * D + (m + 1) * 128],
                            xkbv(pr)[:, kk * CB:(kk + 1) * CB], start=(kk == 0), stop=(kk == 7)),
                            deps=dd, inc=SPE)
                    if m == DC - 1:
                        k_done[c - 1] = t_mm
                    if m >= 2:
                        kw = [t_ksq[c - 1][m - 2]]
                    elif c >= 2:
                        kw = [t_ksq[c - 2][DC - 2 + kp]]
                    else:
                        kw = []
                    t_rel[c - 1][m] = E("act", lambda s, m=m, bb=bb, kp=kp: s.activation(
                        out=kscrv(kp), in_=PB[bb][0:128, 0:CB], func=AF.Relu,
                        bias=crk[:, m:m + 1], scale=1.0), deps=[t_mm, t_crcols] + kw, inc=SACT)
                    bk[bb] = t_rel[c - 1][m]
                    kabw = [v_done[c - 2]] if c >= 2 else []
                    t_ksq[c - 1][m] = E("act", lambda s, m=m, kp=kp: s.activation(
                        out=kabv[:, m * CB:(m + 1) * CB], in_=kscrv(kp),
                        func=AF.Square, bias=0.0, scale=1.0),
                        deps=[t_rel[c - 1][m]] + kabw, inc=SACT)
            if c >= 1:
                pr = (c - 1) % 2
                vfree = {0: (t_m2r[c] if c < NCB else t_vm[c - 2][DC - 2]),
                         1: (t_varT[c] if c < NCB else t_vm[c - 2][DC - 1])}
                for m in range(DC):
                    bb = m % 2
                    vp = m % 2
                    t_mm = None
                    for kk in range(DC):
                        dd = [t_mm] if t_mm else [t_ksq[c - 1][DC - 1], t_wvf, vfree[bb]]
                        t_mm = E("pe", lambda t, m=m, kk=kk, bb=bb: t.matmul(
                            PB[bb][0:128, 0:CB], wv[:, kk * D + m * 128:kk * D + (m + 1) * 128],
                            kabv[:, kk * CB:(kk + 1) * CB], start=(kk == 0), stop=(kk == 7)),
                            deps=dd, inc=SPE)
                    if m == DC - 1:
                        v_done[c - 1] = t_mm
                    if m >= 2:
                        vwm = [t_x2[c - 1][m - 2]]
                    elif c >= 2:
                        vwm = [t_x2[c - 2][DC - 2 + vp]]
                    else:
                        vwm = []
                    t_vm[c - 1][m] = E("dve", lambda v, m=m, bb=bb, vp=vp: v.tensor_mul(
                        vscrv(vp), rbufv[:, m * CB:(m + 1) * CB], PB[bb][0:128, 0:CB]),
                        deps=[t_mm, t_sig[c - 1][m]] + vwm, inc=SDVE)
                    vfree[bb] = t_vm[c - 1][m]
                    t_x2[c - 1][m] = E("dve", lambda v, m=m, vp=vp, pr=pr: v.tensor_add(
                        x1bv(pr)[:, m * CB:(m + 1) * CB], x1bv(pr)[:, m * CB:(m + 1) * CB],
                        vscrv(vp)),
                        deps=[t_vm[c - 1][m], t_x1m[c - 1][m], t_cast[c - 1][m],
                              t_sqx[c - 1][m]], inc=SDVE)
                bk[0] = t_vm[c - 1][DC - 2]
                bk[1] = t_vm[c - 1][DC - 1]
                t_store[c - 1] = E("sp", lambda s, c=c, pr=pr: s.dma_start(
                    out=yout[:, (c - 1) * CB:c * CB].rearrange("(r p) t -> p r t", p=128),
                    in_=x1bv(pr).rearrange("p (r t) -> p r t", r=8)),
                    deps=[t_x2[c - 1][DC - 1]], stream=f"yo_{pr}")

        E("gp", lambda g: g.engine_nop(), deps=[t_store[NCB - 2], t_store[NCB - 1]])

        sch.lint()
        import os as _os
        if _os.environ.get("KRACE"):
            for issue in sch.racecheck()[:300]:
                print("RACE:", issue)

        @block.tensor
        def _(t):
            sch.run("pe", t, skip_sem=SPE)

        @block.scalar
        def _(s):
            sch.run("act", s)

        @block.vector
        def _(v):
            sch.run("dve", v)

        @block.gpsimd
        def _(g):
            sch.run("gp", g)

        @block.sync
        def _(s):
            sch.run("sp", s)

    return nc


_NC_CACHE = None


def _get_nc():
    global _NC_CACHE
    if _NC_CACHE is None:
        _NC_CACHE = build_nc()
    return _NC_CACHE


def kernel(**inputs):
    x = np.asarray(inputs["L0_attn_x"], dtype=np.float32)
    Wi = np.asarray(inputs["L0_attn_Wi"], dtype=np.float32)
    bi = np.asarray(inputs["L0_attn_bi"], dtype=np.float32)
    Ws = np.asarray(inputs["L0_attn_Ws"], dtype=np.float32)
    bs = np.asarray(inputs["L0_attn_bs"], dtype=np.float32)
    Wo = np.asarray(inputs["L0_attn_Wo"], dtype=np.float32)
    bo = np.asarray(inputs["L0_attn_bo"], dtype=np.float32)
    A = np.asarray(inputs["L0_attn_A"], dtype=np.float32)
    B = np.asarray(inputs["L0_attn_B"], dtype=np.float32)
    C = np.asarray(inputs["L0_attn_C"], dtype=np.float32)
    W_r = np.asarray(inputs["L0_ffn_W_r"], dtype=np.float32)
    W_k = np.asarray(inputs["L0_ffn_W_k"], dtype=np.float32)
    W_v = np.asarray(inputs["L0_ffn_W_v"], dtype=np.float32)
    mu_r = np.asarray(inputs["L0_ffn_mu_r"], dtype=np.float32)
    mu_k = np.asarray(inputs["L0_ffn_mu_k"], dtype=np.float32)
    cmat = np.asarray(inputs["L0_c"], dtype=np.float32)
    wmod = np.asarray(inputs["L0_wmod"], dtype=np.float32)

    shared = {
        "wrT": np.ascontiguousarray(W_r.T), "wkT": np.ascontiguousarray(W_k.T),
        "wvT": np.ascontiguousarray(W_v.T), "woT": np.ascontiguousarray(Wo.T),
        "wsT": np.ascontiguousarray(Ws.T), "wiT": np.ascontiguousarray(Wi.T),
        "wiN": np.ascontiguousarray(Wi),
        "aT": np.ascontiguousarray(A.T), "bT": np.ascontiguousarray(B.T),
        "cT": np.ascontiguousarray(C.T),
        "wmod": np.ascontiguousarray(wmod),
        "bi": np.ascontiguousarray(bi.reshape(NST, 1)),
        "bs": np.ascontiguousarray(bs.reshape(NST, 1)),
        "borow": np.ascontiguousarray(bo.reshape(1, D)),
        "murc": np.ascontiguousarray(mu_r.reshape(DC, 128).T),
        "mukc": np.ascontiguousarray(mu_k.reshape(DC, 128).T),
    }
    in_maps = []
    for b in range(8):
        m = dict(shared)
        m["xT"] = np.ascontiguousarray(x[b].T)
        m["ccol"] = np.ascontiguousarray(cmat[b].reshape(DC, 128).T)
        in_maps.append(m)

    nc = _get_nc()
    res = run_bass_kernel_spmd(nc, in_maps, core_ids=list(range(8)))
    out = np.stack([np.ascontiguousarray(r["yout"].T) for r in res.results], axis=0)
    return out.astype(np.float32)
